# version 02
# baseline (speedup 1.0000x reference)
import contextlib
import types
import numpy as np
import ml_dtypes
import concourse.bass as bass
import concourse.mybir as mybir
from concourse.bass_utils import run_bass_kernel_spmd

F32 = mybir.dt.float32
BF16 = mybir.dt.bfloat16
AF = mybir.ActivationFunctionType
ALU = mybir.AluOpType
bfd = ml_dtypes.bfloat16

RG = [[0, 1, 2, 3, 4, 5, 6, 7]]
NCORE = 8
B, S, H = 16, 2048, 1024
CH = 64
NCH = 32
IN = 2048
JS = IN // NCORE
T = B * CH
NLOSS = float(B * CH * H)
EPS = 1e-6
LNEPS = 1e-5
BASE_LR = 0.01
CPC = NCH // NCORE
TPC = 2 * S
NE = 17
KDE = NE * 512


class Ctx:
    def __init__(self, nc, es):
        self.nc = nc
        self.sems = {}
        self.cnt = {}
        for n in ("q", "g", "s", "v", "c", "t"):
            self.sems[n] = es.enter_context(nc.semaphore("sem_" + n))
            self.cnt[n] = 0

    def inc(self, instr, name, k=1):
        instr.then_inc(self.sems[name], k)
        self.cnt[name] += k
        return self.cnt[name]

    def dma(self, eng, qn, out, in_, **kw):
        return self.inc(eng.dma_start(out=out, in_=in_, **kw), qn, 16)

    def tr(self, eng, qn, out, in_):
        return self.inc(eng.dma_start_transpose(out=out, in_=in_), qn, 16)

    def w(self, eng, name, val=None):
        eng.wait_ge(self.sems[name], self.cnt[name] if val is None else val)

    def vsw(self, eng, ins, name="v"):
        eng.drain()

    def barrier(self, block):
        snap = dict(self.cnt)
        for ename in ("sync", "gpsimd", "scalar", "vector", "tensor"):
            dec = getattr(block, ename)

            @dec
            def _(e):
                for sname, v in snap.items():
                    e.wait_ge(self.sems[sname], v)


def _mmslice(buf, col0, width):
    return buf[:, col0:col0 + width]


def phase_a(st):
    nc, c, block, d = st.nc, st.c, st.block, st.dram
    sb = st.sb
    P = st.P

    knq = sb("knq", [128, 8], F32)
    qnq = sb("qnq", [128, 8], F32)
    wT_bf = sb("wT_bf", [128, 8 * H])
    tw_bf = sb("tw_bf", [128, 8 * T])
    xck = sb("xck", [128, 8 * T])
    silk = sb("silk", [128, 8 * T])
    kout = sb("kout", [128, 8 * T])
    sqs = sb("sqs", [128, 1024])
    rrow = sb("rrow", [1, 1024], F32)
    rrow2 = sb("rrow2", [1, 64], F32)
    rrowR = sb("rrowR", [1, 1024], F32)
    rstcR = sb("rstcR", [128, 1], F32)
    rbf = sb("rbf", [1, 1024])
    RBs = sb("RBs", [128, 1024])
    smrow = sb("smrow", [1, 32], F32)
    KNB = sb("KNB", [128, 1024])
    xbk = sb("xbk", [128, 8 * S])
    silt = sb("silt", [128, 1024])
    vtok = sb("vtok", [128, 1024])
    ktok = sb("ktok", [128, 1024])
    ssc = sb("ssc", [128, 1], F32)
    rstc = sb("rstc", [128, 1], F32)

    @block.sync
    def _(sy):
        c.dma(sy, "q", knq[:], d["kn"])
        c.dma(sy, "q", qnq[:], d["qn"])

    @block.vector
    def _(v):
        ins = v.memset(smrow[:], 0.0)
        c.inc(ins, "v")

    @block.gpsimd
    def _(g):
        c.dma(g, "g", st.ones_r[:], d["onesh"][0:1, :])
        c.dma(g, "g", st.ones_c[:], d["onesh"][:, 0:1],
              allow_slow_non_contiguous=True)
        c.dma(g, "g", st.eye_bf[:], d["eyeh"])
        c.dma(g, "g", st.knrow_bf[:], d["knrow"])
        c.dma(g, "g", wT_bf[:].rearrange("p (k h) -> p k h", k=8), d["wkT"].rearrange("(k p) h -> p k h", p=128))
    g_static = c.cnt["g"]

    for s in range(CPC):
        @block.gpsimd
        def _(g):
            if s == 0:
                c.w(g, "g", g_static)
            else:
                c.w(g, "g")
                c.w(g, "v")
                c.w(g, "t")
            c.dma(g, "g", xck[:].rearrange("p (k t) -> p k t", k=8),
                  d["xfc"][s * H:(s + 1) * H, :].rearrange("(k p) t -> p k t", p=128))
        ld_x = c.cnt["g"]

        for m in range(8):
            pm = P[m % 2]

            @block.tensor
            def _(t):
                if m == 0:
                    c.w(t, "g", ld_x)
                if m >= 2:
                    c.w(t, "c")
                ins = None
                for k in range(8):
                    for nh in range(2):
                        ins = t.matmul(
                            pm[:, nh * 512:nh * 512 + 512],
                            wT_bf[:, k * H + m * 128:k * H + m * 128 + 128],
                            xck[:, k * T + nh * 512:k * T + nh * 512 + 512],
                            start=(k == 0), stop=(k == 7))
                c.inc(ins, "t")
            mm_done = c.cnt["t"]

            @block.scalar
            def _(sc):
                c.w(sc, "t", mm_done)
                ins = sc.activation(out=silk[:, m * T:(m + 1) * T], in_=pm[:],
                                    func=AF.Silu)
                c.inc(ins, "c")
        sil_done = c.cnt["c"]

        for m in range(8):
            @block.scalar
            def _(sc):
                if m == 0:
                    c.w(sc, "c", sil_done)
                ins = sc.activation(out=sqs[:], in_=silk[:, m * T:(m + 1) * T],
                                    func=AF.Square)
                c.inc(ins, "c")
            sq_done = c.cnt["c"]

            @block.tensor
            def _(t):
                c.w(t, "c", sq_done)
                ins = None
                for nh in range(2):
                    ins = t.matmul(P[2][0:1, nh * 512:nh * 512 + 512], st.ones_c[:],
                                   sqs[:, nh * 512:nh * 512 + 512],
                                   start=(m == 0), stop=(m == 7))
                if m == 7:
                    c.inc(ins, "t")
        ss_done = c.cnt["t"]

        @block.scalar
        def _(sc):
            c.w(sc, "t", ss_done)
            ins = sc.copy(out=rrow[:], in_=P[2][0:1, :])
            c.inc(ins, "c")

        @block.vector
        def _(v):
            c.w(v, "c")
            ins = v.tensor_scalar(out=rrow[:], in0=rrow[:], scalar1=1.0 / float(H),
                                  scalar2=EPS, op0=ALU.mult, op1=ALU.add)
            c.inc(ins, "v")

        @block.scalar
        def _(sc):
            c.w(sc, "v")
            ins = sc.activation(out=rrow[:], in_=rrow[:], func=AF.Sqrt)
            c.inc(ins, "c")

        @block.vector
        def _(v):
            c.w(v, "c")
            v.reciprocal(out=rrowR[:], in_=rrow[:])
            v.drain()
            ins = v.tensor_copy(out=rbf[:], in_=rrowR[:])
            c.inc(ins, "v")

        @block.tensor
        def _(t):
            c.w(t, "v")
            ins = None
            for nh in range(2):
                ins = t.matmul(P[3][:, nh * 512:nh * 512 + 512], st.ones_r[:],
                               rbf[0:1, nh * 512:nh * 512 + 512], start=True, stop=True)
            c.inc(ins, "t")

        @block.vector
        def _(v):
            c.w(v, "t")
            v.tensor_copy(out=RBs[:], in_=P[3][:])
            ins = None
            for m in range(8):
                ins = v.scalar_tensor_tensor(
                    out=kout[:, m * T:(m + 1) * T],
                    in0=silk[:, m * T:(m + 1) * T], scalar=knq[:, m:m + 1],
                    in1=RBs[:], op0=ALU.mult, op1=ALU.mult)
            c.inc(ins, "v")
        k_done = c.cnt["v"]

        @block.gpsimd
        def _(g):
            c.w(g, "v", k_done)
            c.dma(g, "g",
                  d["kd_in"][s * H:(s + 1) * H, :].rearrange("(k p) t -> p k t", p=128),
                  kout[:].rearrange("p (k t) -> p k t", k=8))

        for wi, wname in enumerate(("twT", "twA", "twE")):
            @block.gpsimd
            def _(g):
                if wi > 0:
                    c.w(g, "v")
                c.dma(g, "g", tw_bf[:].rearrange("p (k t) -> p k t", k=8),
                      d[wname].rearrange("(k p) t -> p k t", p=128))
            tw_ld = c.cnt["g"]

            @block.vector
            def _(v):
                c.w(v, "g", tw_ld)
                if wi == 0:
                    c.w(v, "v", k_done)
                ins = None
                for m in range(8):
                    ins = v.tensor_tensor(
                        out=silk[:, m * T:(m + 1) * T],
                        in0=xck[:, m * T:(m + 1) * T],
                        in1=tw_bf[:, m * T:(m + 1) * T], op=ALU.mult)
                c.inc(ins, "v")
            z_done = c.cnt["v"]

            @block.tensor
            def _(t):
                c.w(t, "v", z_done)
                ins = None
                for m in range(8):
                    for nh in range(2):
                        ins = t.matmul(P[2][0:1, nh * 512:nh * 512 + 512], st.ones_c[:],
                                       silk[:, m * T + nh * 512:m * T + nh * 512 + 512],
                                       start=(m == 0), stop=(m == 7))
                c.inc(ins, "t")
            dot_done = c.cnt["t"]

            @block.vector
            def _(v):
                c.w(v, "t", dot_done)
                v.tensor_copy(out=rrow[:], in_=P[2][0:1, :])
                v.drain()
                ins = v.reduce_sum(
                    out=rrow2[0:1, 0:16],
                    in_=rrow.rearrange("o (b i) -> o b i", b=16)[:, :, :],
                    axis=mybir.AxisListType.X)
                c.inc(ins, "v")
            seg_done = c.cnt["v"]

            if wi == 0:
                @block.scalar
                def _(sc):
                    c.w(sc, "v", seg_done)
                    ins = sc.activation(out=smrow[0:1, 0:16], in_=rrow2[0:1, 0:16],
                                        func=AF.Sigmoid)
                    c.inc(ins, "c")

                @block.vector
                def _(v):
                    c.w(v, "c")
                    ins = v.tensor_scalar(out=smrow[0:1, 0:16], in0=smrow[0:1, 0:16],
                                          scalar1=BASE_LR, scalar2=None, op0=ALU.mult)
                    c.inc(ins, "v")
            else:
                @block.vector
                def _(v):
                    v.drain()
                    ins = v.reduce_sum(out=rrow2[0:1, 16:17], in_=rrow2[0:1, 0:16],
                                       axis=mybir.AxisListType.X)
                    v.drain()
                    c.inc(ins, "v")

                @block.scalar
                def _(sc):
                    c.w(sc, "v")
                    pass

                @block.vector
                def _(v):
                    v.drain()
                    ins = v.tensor_scalar(out=rrow2[0:1, 16:17], in0=rrow2[0:1, 16:17],
                                          scalar1=1.0 / 16.0, scalar2=None, op0=ALU.mult)
                    v.drain()
                    c.inc(ins, "v")

                @block.scalar
                def _(sc):
                    c.w(sc, "v")
                    ins = sc.activation(out=rrow2[0:1, 17:18], in_=rrow2[0:1, 16:17],
                                        func=AF.Sigmoid)
                    c.inc(ins, "c")

                @block.vector
                def _(v):
                    c.w(v, "c")
                    if wi == 1:
                        ins = v.tensor_scalar(out=smrow[0:1, 16:17],
                                              in0=rrow2[0:1, 17:18],
                                              scalar1=-1.0, scalar2=1.0,
                                              op0=ALU.mult, op1=ALU.add)
                    else:
                        ins = v.tensor_copy(out=smrow[0:1, 17:18], in_=rrow2[0:1, 17:18])
                    c.inc(ins, "v")
        sm_done = c.cnt["v"]

        @block.gpsimd
        def _(g):
            c.w(g, "v", sm_done)
            c.dma(g, "g", d["sm_in"][s:s + 1, :], smrow[:])

    @block.tensor
    def _(t):
        c.w(t, "g")
        ins = t.matmul(P[3][:, 0:512], st.ones_r[:], st.knrow_bf[0:1, 0:512],
                       start=True, stop=True)
        ins = t.matmul(P[3][:, 512:1024], st.ones_r[:], st.knrow_bf[0:1, 512:1024],
                       start=True, stop=True)
        c.inc(ins, "t")

    @block.vector
    def _(v):
        c.w(v, "t")
        ins = v.tensor_copy(out=KNB[:], in_=P[3][:])
        c.inc(ins, "v")
    knb_done = c.cnt["v"]

    for proj in ("q", "k", "v"):
        wsrc = {"q": "wqT", "k": "wkT", "v": "wvT"}[proj]

        @block.gpsimd
        def _(g):
            c.w(g, "t")
            c.w(g, "v")
            c.w(g, "c")
            c.dma(g, "g", wT_bf[:].rearrange("p (k h) -> p k h", k=8), d[wsrc].rearrange("(k p) h -> p k h", p=128))
        w_ld = c.cnt["g"]

        for bb in range(2):
            @block.gpsimd
            def _(g):
                if bb == 0:
                    c.w(g, "g", w_ld)
                else:
                    c.w(g, "t")
                c.dma(g, "g", xbk[:].rearrange("p (k t) -> p k t", k=8),
                      d["xfb"][bb * H:(bb + 1) * H, :].rearrange("(k p) t -> p k t", p=128))
            xb_ld = c.cnt["g"]

            for tt in range(4):
                if proj == "q":
                    for m in range(8):
                        pm = P[m % 2]

                        @block.tensor
                        def _(t):
                            if m == 0 and tt == 0:
                                c.w(t, "g", xb_ld)
                            if m >= 2 or tt > 0:
                                c.w(t, "c")
                            ins = None
                            for k in range(8):
                                ins = t.matmul(
                                    pm[:, 0:512],
                                    wT_bf[:, k * H + m * 128:k * H + m * 128 + 128],
                                    xbk[:, k * S + tt * 512:k * S + tt * 512 + 512],
                                    start=(k == 0), stop=(k == 7))
                            c.inc(ins, "t")
                        mm_done = c.cnt["t"]

                        @block.scalar
                        def _(sc):
                            c.w(sc, "t", mm_done)
                            ins = sc.activation(out=silk[:, m * 512:(m + 1) * 512],
                                                in_=pm[:, 0:512], func=AF.Silu)
                            c.inc(ins, "c")
                    sil_done = c.cnt["c"]

                    for m in range(8):
                        @block.scalar
                        def _(sc):
                            if m == 0:
                                c.w(sc, "c", sil_done)
                            ins = sc.activation(out=sqs[:, 0:512],
                                                in_=silk[:, m * 512:(m + 1) * 512],
                                                func=AF.Square)
                            c.inc(ins, "c")
                        sq_done = c.cnt["c"]

                        @block.tensor
                        def _(t):
                            c.w(t, "c", sq_done)
                            ins = t.matmul(P[2][0:1, 0:512], st.ones_c[:], sqs[:, 0:512],
                                           start=(m == 0), stop=(m == 7))
                            if m == 7:
                                c.inc(ins, "t")

                    @block.vector
                    def _(v):
                        c.w(v, "t")
                        ins = v.tensor_scalar(out=rrow[0:1, 0:512], in0=P[2][0:1, 0:512],
                                              scalar1=1.0 / float(H), scalar2=EPS,
                                              op0=ALU.mult, op1=ALU.add)
                        c.inc(ins, "v")

                    @block.scalar
                    def _(sc):
                        c.w(sc, "v")
                        ins = sc.activation(out=rrow[0:1, 0:512], in_=rrow[0:1, 0:512],
                                            func=AF.Sqrt)
                        c.inc(ins, "c")

                    @block.vector
                    def _(v):
                        c.w(v, "c")
                        v.reciprocal(out=rrowR[0:1, 0:512], in_=rrow[0:1, 0:512])
                        v.drain()
                        ins = v.tensor_copy(out=rbf[0:1, 0:512], in_=rrowR[0:1, 0:512])
                        c.inc(ins, "v")

                    @block.tensor
                    def _(t):
                        c.w(t, "v")
                        ins = t.matmul(P[3][:, 0:512], st.ones_r[:], rbf[0:1, 0:512],
                                       start=True, stop=True)
                        c.inc(ins, "t")

                    @block.vector
                    def _(v):
                        c.w(v, "t")
                        v.tensor_copy(out=RBs[:, 0:512], in_=P[3][:, 0:512])
                        ins = None
                        for m in range(8):
                            ins = v.scalar_tensor_tensor(
                                out=kout[:, m * 512:(m + 1) * 512],
                                in0=silk[:, m * 512:(m + 1) * 512],
                                scalar=qnq[:, m:m + 1], in1=RBs[:, 0:512],
                                op0=ALU.mult, op1=ALU.mult)
                        c.inc(ins, "v")
                    q_done = c.cnt["v"]

                    @block.sync
                    def _(sy):
                        c.w(sy, "v", q_done)
                        c.dma(sy, "q",
                              d["q_dram"][:, bb * S + tt * 512:bb * S + tt * 512 + 512]
                              .rearrange("(k p) t -> p k t", p=128),
                              kout[:, 0:8 * 512].rearrange("p (k t) -> p k t", k=8))
                else:
                    for m in range(4):
                        pm = P[m % 2]

                        @block.tensor
                        def _(t):
                            if m == 0 and tt == 0:
                                c.w(t, "g", xb_ld)
                            if m >= 2 or tt > 0:
                                c.w(t, "c")
                                c.w(t, "v")
                            ins = None
                            for k in range(8):
                                for nh in range(2):
                                    ins = t.matmul(
                                        pm[:, nh * 512:nh * 512 + 512],
                                        xbk[:, k * S + tt * 512 + m * 128:
                                            k * S + tt * 512 + m * 128 + 128],
                                        wT_bf[:, k * H + nh * 512:k * H + nh * 512 + 512],
                                        start=(k == 0), stop=(k == 7))
                            c.inc(ins, "t")
                        mm_done = c.cnt["t"]

                        @block.scalar
                        def _(sc):
                            c.w(sc, "t", mm_done)
                            if proj == "k":
                                sc.activation(out=silt[:], in_=pm[:], func=AF.Silu)
                                ins = sc.activation(out=sqs[:], in_=silt[:],
                                                    func=AF.Square, accum_out=ssc[:])
                                c.inc(ins, "c")
                            else:
                                c.w(sc, "q")
                                ins = sc.activation(out=vtok[:], in_=pm[:], func=AF.Silu)
                                c.inc(ins, "c")
                        act_done = c.cnt["c"]

                        if proj == "k":
                            @block.vector
                            def _(v):
                                c.w(v, "c", act_done)
                                ins = v.tensor_scalar(out=rstc[:], in0=ssc[:],
                                                      scalar1=1.0 / float(H), scalar2=EPS,
                                                      op0=ALU.mult, op1=ALU.add)
                                c.inc(ins, "v")

                            @block.scalar
                            def _(sc):
                                c.w(sc, "v")
                                ins = sc.activation(out=rstc[:], in_=rstc[:], func=AF.Sqrt)
                                c.inc(ins, "c")

                            @block.vector
                            def _(v):
                                c.w(v, "c")
                                c.w(v, "q")
                                v.reciprocal(out=rstcR[:], in_=rstc[:])
                                v.drain()
                                ins = v.scalar_tensor_tensor(
                                    out=ktok[:], in0=silt[:], scalar=rstcR[:],
                                    in1=KNB[:], op0=ALU.mult, op1=ALU.mult)
                                c.inc(ins, "v")
                        ops_done = c.cnt["v"]

                        @block.sync
                        def _(sy):
                            row0 = (bb * S + tt * 512 + m * 128)
                            if proj == "k":
                                c.w(sy, "v", ops_done)
                                c.dma(sy, "q", d["ktok_dram"][row0:row0 + 128, :], ktok[:])
                            else:
                                c.w(sy, "c", act_done)
                                c.dma(sy, "q", d["vtok_dram"][row0:row0 + 128, :], vtok[:])

    dstream = sb("dstream", [128, 1024])
    kstream = sb("kstream", [128, 1024])

    for bb in range(2):
        for tt in range(4):
            for m in range(4):
                row0 = bb * S + tt * 512 + m * 128

                @block.sync
                def _(sy):
                    c.w(sy, "q")
                    c.w(sy, "v")
                    c.dma(sy, "q", kstream[:], d["ktok_dram"][row0:row0 + 128, :])
                    c.dma(sy, "q", dstream[:], d["vtok_dram"][row0:row0 + 128, :])
                ld2 = c.cnt["q"]

                @block.vector
                def _(v):
                    c.w(v, "q", ld2)
                    ins = v.tensor_tensor(out=dstream[:], in0=kstream[:], in1=dstream[:],
                                          op=ALU.subtract)
                    c.inc(ins, "v")
                d_done = c.cnt["v"]

                @block.sync
                def _(sy):
                    c.w(sy, "v", d_done)
                    base_tok = tt * 512 + m * 128
                    for seg in range(2):
                        ch = (base_tok + seg * 64) // CH
                        c.dma(sy, "q",
                              d["d_own"][ch * 128 + bb * CH:ch * 128 + bb * CH + CH, :],
                              dstream[seg * 64:(seg + 1) * 64, :])

    @block.gpsimd
    def _(g):
        c.w(g, "g")
        c.w(g, "q")
        c.w(g, "v")
        ins = g.collective_compute("AllGather", ALU.bypass, RG,
                                   ins=[d["kd_in"].opt()],
                                   outs=[d["kd_all"].opt()])
        c.inc(ins, "g", 1)
        ins = g.collective_compute("AllGather", ALU.bypass, RG,
                                   ins=[d["sm_in"].opt()],
                                   outs=[d["sm_all"].opt()])
        c.inc(ins, "g", 1)
    st.ag_done = c.cnt["g"]

    th_sb = sb("th_sb", [16, 32])
    esel_bf = sb("esel_bf", [16, 128])
    berow = sb("berow", [1, 64])

    @block.gpsimd
    def _(g):
        c.w(g, "g", st.ag_done)
        c.dma(g, "g", esel_bf[:], d["eselh"])
        c.dma(g, "g", th_sb[:], d["sm_all"][:, 0:16].rearrange("ch b -> b ch"),
              allow_slow_non_contiguous=True)
        c.dma(g, "g", berow[0:1, 0:32], d["sm_all"][:, 16:17].rearrange("ch o -> o ch"),
              allow_slow_non_contiguous=True)
        c.dma(g, "g", berow[0:1, 32:64], d["sm_all"][:, 17:18].rearrange("ch o -> o ch"),
              allow_slow_non_contiguous=True)
    tbl_ld = c.cnt["g"]

    @block.tensor
    def _(t):
        c.w(t, "g", tbl_ld)
        ins = t.matmul(P[0][:, 0:32], esel_bf[:], th_sb[:], start=True, stop=True)
        c.inc(ins, "t")
        ins = t.matmul(P[1][:, 0:64], st.ones_r[:], berow[0:1, :], start=True, stop=True)
        c.inc(ins, "t")

    @block.vector
    def _(v):
        c.w(v, "t")
        v.tensor_scalar(out=st.GS[:], in0=P[0][:, 0:32], scalar1=2.0 / NLOSS,
                        scalar2=None, op0=ALU.mult)
        v.tensor_copy(out=st.BT[:], in_=P[1][:, 0:32])
        v.tensor_copy(out=st.ET[:], in_=P[1][:, 32:64])
        ins = v.tensor_copy(out=st.BErow[:], in_=P[1][0:1, 0:64])
        c.inc(ins, "v")
    st.tbl_done = c.cnt["v"]


def scan(st):
    nc, c, block, d = st.nc, st.c, st.block, st.dram
    sb = st.sb
    P = st.P
    dbg = st.dbg

    kf = [sb(f"kf{i}", [128, 8 * T]) for i in range(2)]
    kt = sb("kt", [128, 8 * H], is_transpose_dest=True)
    dblk = [sb(f"dblk{i}", [128, H]) for i in range(2)]
    h_j = sb("h_j", [128, 2 * T])
    U_j = sb("U_j", [128, 2 * T])
    h_tok = sb("h_tok", [128, 8 * JS], is_transpose_dest=True)
    U_tok = sb("U_tok", [128, 8 * JS], is_transpose_dest=True)
    ypp = [sb(f"ypp{i}", [128, H]) for i in range(2)]
    yblk = sb("yblk", [128, H], F32)
    LNB = sb("LNB", [128, H])
    t1s = sb("t1s", [128, H], F32)
    esb = sb("esb", [128, H], F32)
    Qsb = sb("Qsb", [128, H], F32)
    Qw = sb("Qw", [128, H])
    dyb = sb("dyb", [128, H])
    dy_tok = sb("dy_tok", [128, 8 * H])
    df2 = sb("df2", [128, 2 * T], is_transpose_dest=True)
    dW1sb = sb("dW1sb", [128, 2 * H], F32)
    dW0sb = sb("dW0sb", [128, 8 * JS], F32)
    dPre = sb("dPre", [128, 8 * JS])
    sscol = sb("sscol", [128, 1], F32)
    rstdc = sb("rstdc", [128, 1], F32)
    rstdR = sb("rstdR", [128, 1], F32)
    ccol = sb("ccol", [128, 1], F32)
    wcol = sb("wcol", [128, 1], F32)
    bneg = sb("bneg", [128, 1], F32)
    nrmc = sb("nrmc", [128, 12], F32)
    nrmb = sb("nrmb", [128, 12])
    nrow = sb("nrow", [1, 1536], F32)
    n2row = sb("n2row", [1, 1536], F32)
    scr_bf = sb("scr_bf", [128, H])
    clipb = sb("clipb", [1, 2])
    ncol = sb("ncol", [128, 1], F32)

    w0f, w1j, s0f, s1j = st.w0f, st.w1j, st.s0f, st.s1j
    w0f_bf, w1j_bf, w1hc_bf = st.w0f_bf, st.w1j_bf, st.w1hc_bf
    ln_row, sln_row, ln_bfr = st.ln_row, st.sln_row, st.ln_bfr

    @block.sync
    def _(sy):
        c.w(sy, "g", st.ag_done)
        c.dma(sy, "q", w0f[:], d["w0fs"])
        c.dma(sy, "q", w1j[:], d["w1js"])
        c.dma(sy, "q", ln_row[:], d["lnr"])
    st_ld = c.cnt["q"]

    @block.vector
    def _(v):
        c.w(v, "q", st_ld)
        v.memset(s0f[:], 0.0)
        v.memset(s1j[:], 0.0)
        v.memset(sln_row[:], 0.0)
        v.memset(nrow[:], 0.0)
        v.tensor_copy(out=w0f_bf[:], in_=w0f[:])
        v.tensor_copy(out=w1j_bf[:], in_=w1j[:])
        ins = v.tensor_copy(out=ln_bfr[:], in_=ln_row[:])
        c.inc(ins, "v")
    init_done = c.cnt["v"]

    def emit_w1hc(wait_v):
        @block.sync
        def _(sy):
            c.w(sy, "v", wait_v)
            c.dma(sy, "q", d["w1scr"][:, :].rearrange("(k p) h -> p k h", p=128),
                  w1j_bf[:].rearrange("p (k h) -> p k h", k=2))
            c.w(sy, "q")
            for m in range(8):
                c.tr(sy, "q", w1hc_bf[:, m * JS:(m + 1) * JS],
                     d["w1scr"][:, m * 128:(m + 1) * 128])
        return c.cnt["q"]

    w1hc_ready = emit_w1hc(init_done)

    def kd_base(ch):
        return (ch // CPC) * CPC * H + (ch % CPC) * H

    def emit_kf(ch, buf, wait_t):
        kb = kd_base(ch)

        @block.sync
        def _(sy):
            c.w(sy, "g", st.ag_done)
            c.w(sy, "t", wait_t)
            c.dma(sy, "q", kf[buf][:].rearrange("p (k t) -> p k t", k=8),
                  d["kd_all"][kb:kb + H, :].rearrange("(k p) t -> p k t", p=128))
        return c.cnt["q"]

    def emit_kt(ch, wait_t):
        kb = kd_base(ch)

        @block.sync
        def _(sy):
            c.w(sy, "g", st.ag_done)
            c.w(sy, "t", wait_t)
            for m in range(8):
                c.tr(sy, "q", kt[:, m * H:(m + 1) * H],
                     d["kd_all"][kb:kb + H, m * 128:(m + 1) * 128])
        return c.cnt["q"]

    def emit_dblk(ch, buf, wait_v):
        @block.sync
        def _(sy):
            c.w(sy, "v", wait_v)
            c.dma(sy, "q", dblk[buf][:], d["d_own"][ch * 128:(ch + 1) * 128, :])
        return c.cnt["q"]

    kf_rdy = emit_kf(0, 0, 0)
    kt_rdy = emit_kt(0, 0)
    db_rdy = emit_dblk(0, 0, 0)
    m1_done_prev = 0
    m5_done_prev = 0
    ew_done_prev = 0
    upd_done = init_done

    for ch in range(NCH):
        buf = ch % 2
        nbuf = 1 - buf

        if ch + 1 < NCH:
            kf_rdy_next = emit_kf(ch + 1, nbuf, m1_done_prev)
            db_rdy_next = emit_dblk(ch + 1, nbuf, ew_done_prev)

        @block.tensor
        def _(t):
            c.w(t, "q", kf_rdy)
            c.w(t, "v", upd_done)
            for jm in range(2):
                ins = None
                for nh in range(2):
                    for k in range(8):
                        ins = t.matmul(
                            P[jm][:, nh * 512:nh * 512 + 512],
                            w0f_bf[:, k * JS + jm * 128:k * JS + jm * 128 + 128],
                            kf[buf][:, k * T + nh * 512:k * T + nh * 512 + 512],
                            start=(k == 0), stop=(k == 7))
                c.inc(ins, "t")
        m1_done = c.cnt["t"]
        m1_done_prev = m1_done

        @block.scalar
        def _(sc):
            c.w(sc, "t", m1_done)
            for jm in range(2):
                sc.activation(out=h_j[:, jm * T:(jm + 1) * T], in_=P[jm][:], func=AF.Silu)
                ins = sc.activation(out=U_j[:, jm * T:(jm + 1) * T], in_=P[jm][:],
                                    func=AF.Derivative_silu)
            c.inc(ins, "c")
        hu_done = c.cnt["c"]

        @block.scalar
        def _(sc):
            c.w(sc, "c", hu_done)
            c.dma(sc, "s", d["hU_dram"][0:JS, :].rearrange("(k p) t -> p k t", p=128),
                  h_j[:].rearrange("p (k t) -> p k t", k=2))
            c.dma(sc, "s", d["hU_dram"][JS:2 * JS, :].rearrange("(k p) t -> p k t", p=128),
                  U_j[:].rearrange("p (k t) -> p k t", k=2))
            c.w(sc, "s")
            for m in range(8):
                c.tr(sc, "s", h_tok[:, m * JS:(m + 1) * JS],
                     d["hU_dram"][0:JS, m * 128:(m + 1) * 128])
                c.tr(sc, "s", U_tok[:, m * JS:(m + 1) * JS],
                     d["hU_dram"][JS:2 * JS, m * 128:(m + 1) * 128])
        hU_tok_rdy = c.cnt["s"]

        gv_y = []
        for tm in range(8):
            pm = P[2 + (tm % 2)]

            @block.tensor
            def _(t):
                if tm == 0:
                    c.w(t, "c", hu_done)
                if tm >= 2:
                    c.w(t, "v")
                ins = None
                for nh in range(2):
                    for k in range(2):
                        ins = t.matmul(
                            pm[:, nh * 512:nh * 512 + 512],
                            h_j[:, k * T + tm * 128:k * T + tm * 128 + 128],
                            w1j_bf[:, k * H + nh * 512:k * H + nh * 512 + 512],
                            start=(k == 0), stop=(k == 1))
                c.inc(ins, "t")
            y_mm = c.cnt["t"]

            @block.vector
            def _(v):
                c.w(v, "t", y_mm)
                if tm >= 2:
                    c.w(v, "g")
                ins = v.tensor_copy(out=ypp[tm % 2][:], in_=pm[:])
                c.inc(ins, "v")
            y_ev = c.cnt["v"]

            @block.gpsimd
            def _(g):
                c.w(g, "v", y_ev)
                if ch > 0 and tm == 0:
                    c.w(g, "g")
                c.dma(g, "g", d["yrs_in"][tm * 128:(tm + 1) * 128, :], ypp[tm % 2][:])
            gv_y.append(c.cnt["g"])

        @block.gpsimd
        def _(g):
            c.w(g, "g", gv_y[-1])
            ins = g.collective_compute("ReduceScatter", ALU.add, RG,
                                       ins=[d["yrs_in"].opt()],
                                       outs=[d["yrs_out"].opt()])
            c.inc(ins, "g", 1)
            c.w(g, "g")
            c.dma(g, "g", yblk[:], d["yrs_out"][:, :])
        yblk_rdy = c.cnt["g"]

        @block.tensor
        def _(t):
            c.w(t, "v", upd_done)
            ins = None
            for nh in range(2):
                ins = t.matmul(P[0][:, nh * 512:nh * 512 + 512], st.ones_r[:],
                               ln_bfr[0:1, nh * 512:nh * 512 + 512],
                               start=True, stop=True)
            c.inc(ins, "t")
        lnb_mm = c.cnt["t"]

        @block.vector
        def _(v):
            c.w(v, "t", lnb_mm)
            ins = v.tensor_copy(out=LNB[:], in_=P[0][:])
            c.inc(ins, "v")
        lnb_done = c.cnt["v"]

        @block.scalar
        def _(sc):
            c.w(sc, "g", yblk_rdy)
            ins = sc.activation(out=esb[:], in_=yblk[:], func=AF.Square,
                                accum_out=sscol[:])
            c.inc(ins, "c")
        ssq_done = c.cnt["c"]

        @block.vector
        def _(v):
            c.w(v, "c", ssq_done)
            ins = v.tensor_scalar(out=rstdc[:], in0=sscol[:], scalar1=1.0 / float(H),
                                  scalar2=EPS, op0=ALU.mult, op1=ALU.add)
            c.inc(ins, "v")

        @block.scalar
        def _(sc):
            c.w(sc, "v")
            ins = sc.activation(out=rstdc[:], in_=rstdc[:], func=AF.Sqrt)
            c.inc(ins, "c")
        rstd_sc = c.cnt["c"]

        @block.vector
        def _(v):
            c.w(v, "c", rstd_sc)
            c.w(v, "v", lnb_done)
            c.w(v, "q", db_rdy)
            v.reciprocal(out=rstdR[:], in_=rstdc[:])
            v.drain()
            v.scalar_tensor_tensor(out=t1s[:], in0=yblk[:], scalar=rstdR[:],
                                   in1=LNB[:], op0=ALU.mult, op1=ALU.mult)
            v.tensor_tensor(out=esb[:], in0=t1s[:], in1=dblk[buf][:], op=ALU.add)
            v.tensor_tensor(out=Qsb[:], in0=yblk[:], in1=esb[:], op=ALU.mult)
            ins = v.tensor_tensor(out=t1s[:], in0=Qsb[:], in1=LNB[:], op=ALU.mult)
            c.inc(ins, "v")
        r4_done = c.cnt["v"]

        @block.scalar
        def _(sc):
            c.w(sc, "v", r4_done)
            ins = sc.activation(out=dW1sb[:, 0:H], in_=t1s[:], func=AF.Identity,
                                accum_out=ccol[:])
            c.inc(ins, "c")
        ccol_done = c.cnt["c"]

        @block.vector
        def _(v):
            c.w(v, "c", ccol_done)
            v.tensor_tensor(out=wcol[:], in0=st.GS[:, ch:ch + 1], in1=rstdR[:],
                            op=ALU.mult)
            v.drain()
            ins = v.tensor_scalar(out=Qw[:], in0=Qsb[:], scalar1=wcol[:], scalar2=None,
                                  op0=ALU.mult)
            c.inc(ins, "v")
            c.vsw(v, v.tensor_tensor(out=bneg[:], in0=rstdR[:], in1=rstdR[:], op=ALU.mult))
            c.vsw(v, v.tensor_tensor(out=bneg[:], in0=bneg[:], in1=wcol[:], op=ALU.mult))
            c.vsw(v, v.tensor_tensor(out=bneg[:], in0=bneg[:], in1=ccol[:], op=ALU.mult))
            v.tensor_scalar(out=bneg[:], in0=bneg[:], scalar1=-1.0 / float(H),
                            scalar2=None, op0=ALU.mult)
            v.drain()
            v.tensor_tensor(out=t1s[:], in0=esb[:], in1=LNB[:], op=ALU.mult)
            v.tensor_scalar(out=t1s[:], in0=t1s[:], scalar1=wcol[:], scalar2=None,
                            op0=ALU.mult)
            v.drain()
            ins = v.scalar_tensor_tensor(out=dyb[:], in0=yblk[:], scalar=bneg[:],
                                         in1=t1s[:], op0=ALU.mult, op1=ALU.add)
            c.inc(ins, "v")
        ew_done = c.cnt["v"]
        ew_done_prev = ew_done
        qw_v = ew_done

        @block.tensor
        def _(t):
            c.w(t, "v", qw_v)
            ins = None
            for nh in range(2):
                ins = t.matmul(P[1][0:1, nh * 512:nh * 512 + 512], st.ones_c[:],
                               Qw[:, nh * 512:nh * 512 + 512], start=True, stop=True)
            c.inc(ins, "t")
        gln_mm = c.cnt["t"]

        @block.vector
        def _(v):
            c.w(v, "t", gln_mm)
            ins = v.tensor_copy(out=nrow[0:1, 0:1024], in_=P[1][0:1, :])
            c.inc(ins, "v")
        gln_ev = c.cnt["v"]

        @block.gpsimd
        def _(g):
            c.w(g, "v", ew_done)
            c.dma(g, "g", d["dy_in"][:, :], dyb[:])
            c.w(g, "g")
            ins = g.collective_compute("AllGather", ALU.bypass, RG,
                                       ins=[d["dy_in"].opt()],
                                       outs=[d["dy_all"].opt()])
            c.inc(ins, "g", 1)
        ag_dy = c.cnt["g"]

        @block.sync
        def _(sy):
            c.w(sy, "g", ag_dy)
            c.dma(sy, "q", dy_tok[:].rearrange("p (k h) -> p k h", k=8),
                  d["dy_all"][:, :].rearrange("(k p) h -> p k h", p=128))
        dytok_rdy = c.cnt["q"]

        @block.tensor
        def _(t):
            c.w(t, "q", dytok_rdy)
            c.w(t, "s", hU_tok_rdy)
            c.w(t, "v", gln_ev)
            for jm in range(2):
                ins = None
                for nh in range(2):
                    for k in range(8):
                        ins = t.matmul(
                            P[jm][:, nh * 512:nh * 512 + 512],
                            h_tok[:, k * JS + jm * 128:k * JS + jm * 128 + 128],
                            dy_tok[:, k * H + nh * 512:k * H + nh * 512 + 512],
                            start=(k == 0), stop=(k == 7))
                c.inc(ins, "t")
        m3_done = c.cnt["t"]

        @block.scalar
        def _(sc):
            c.w(sc, "t", m3_done)
            for jm in range(2):
                ins = sc.activation(out=dW0sb[:, 0:H], in_=P[jm][:], func=AF.Square,
                                    accum_out=nrmc[:, jm:jm + 1])
            c.inc(ins, "c")
        n1_done = c.cnt["c"]

        @block.vector
        def _(v):
            c.w(v, "c", n1_done)
            for jm in range(2):
                ins = v.tensor_copy(out=dW1sb[:, jm * H:(jm + 1) * H], in_=P[jm][:])
            c.inc(ins, "v")
        dw1_ev = c.cnt["v"]

        df_rdy = []
        for khc in range(0, 8, 2):
            @block.sync
            def _(sy):
                c.w(sy, "g", ag_dy)
                if khc >= 2:
                    c.w(sy, "t")
                for kk in range(2):
                    c.tr(sy, "q", df2[:, kk * T:(kk + 1) * T],
                         d["dy_all"][:, (khc + kk) * 128:(khc + kk) * 128 + 128])
            df_rdy.append(c.cnt["q"])

            @block.tensor
            def _(t):
                c.w(t, "q", df_rdy[-1])
                if khc == 0:
                    c.w(t, "q", w1hc_ready)
                    c.w(t, "v", dw1_ev)
                ins = None
                for kk in range(2):
                    kh = khc + kk
                    for tm in range(8):
                        pq = P[2 + tm // 4]
                        ins = t.matmul(
                            pq[:, (tm % 4) * 256:(tm % 4) * 256 + 256],
                            df2[:, kk * T + tm * 128:kk * T + tm * 128 + 128],
                            w1hc_bf[:, kh * JS:(kh + 1) * JS],
                            start=(kh == 0), stop=(kh == 7))
                c.inc(ins, "t")
        m4_done = c.cnt["t"]

        @block.vector
        def _(v):
            c.w(v, "t", m4_done)
            c.w(v, "s", hU_tok_rdy)
            ins = None
            for tm in range(8):
                pq = P[2 + tm // 4]
                ins = v.tensor_tensor(out=dPre[:, tm * JS:(tm + 1) * JS],
                                      in0=pq[:, (tm % 4) * 256:(tm % 4) * 256 + 256],
                                      in1=U_tok[:, tm * JS:(tm + 1) * JS], op=ALU.mult)
            c.inc(ins, "v")
        dpre_done = c.cnt["v"]

        @block.tensor
        def _(t):
            c.w(t, "v", dpre_done)
            c.w(t, "q", kt_rdy)
            for hm in range(8):
                pq = P[hm // 4]
                ins = None
                for k in range(8):
                    ins = t.matmul(
                        pq[:, (hm % 4) * 256:(hm % 4) * 256 + 256],
                        kt[:, k * H + hm * 128:k * H + hm * 128 + 128],
                        dPre[:, k * JS:(k + 1) * JS],
                        start=(k == 0), stop=(k == 7))
                c.inc(ins, "t")
        m5_done = c.cnt["t"]
        m5_done_prev = m5_done

        @block.scalar
        def _(sc):
            c.w(sc, "t", m5_done)
            for hm in range(8):
                pq = P[hm // 4]
                ins = sc.activation(out=esb[:, 0:256],
                                    in_=pq[:, (hm % 4) * 256:(hm % 4) * 256 + 256],
                                    func=AF.Square, accum_out=nrmc[:, 2 + hm:3 + hm])
            c.inc(ins, "c")
        n0_done = c.cnt["c"]

        @block.vector
        def _(v):
            c.w(v, "c", n0_done)
            for hm in range(8):
                pq = P[hm // 4]
                ins = v.tensor_copy(out=dW0sb[:, hm * JS:(hm + 1) * JS],
                                    in_=pq[:, (hm % 4) * 256:(hm % 4) * 256 + 256])
            ins = v.tensor_copy(out=nrmb[:, 0:10], in_=nrmc[:, 0:10])
            c.inc(ins, "v")
        dw0_ev = c.cnt["v"]

        if ch + 1 < NCH:
            kt_rdy_next = emit_kt(ch + 1, m5_done)

        @block.tensor
        def _(t):
            c.w(t, "v", dw0_ev)
            ins = t.matmul(P[2][0:1, 0:10], st.ones_c[:], nrmb[:, 0:10],
                           start=True, stop=True)
            c.inc(ins, "t")
        nsum_mm = c.cnt["t"]

        @block.vector
        def _(v):
            c.w(v, "t", nsum_mm)
            ins = v.reduce_sum(out=nrow[0:1, 1024:1025], in_=P[2][0:1, 0:10],
                               axis=mybir.AxisListType.X)
            c.inc(ins, "v")
        nrow_done = c.cnt["v"]

        @block.gpsimd
        def _(g):
            c.w(g, "v", nrow_done)
            c.dma(g, "g", d["nr_in"][:, :], nrow[:])
            c.w(g, "g")
            ins = g.collective_compute("AllReduce", ALU.add, RG,
                                       ins=[d["nr_in"].opt()],
                                       outs=[d["nr_out"].opt()])
            c.inc(ins, "g", 1)
            c.w(g, "g")
            c.dma(g, "g", n2row[:], d["nr_out"][:, :])
        ar2_done = c.cnt["g"]

        @block.gpsimd
        def _(g):
            r0w = ch * (H // NCH)
            r0d = ch * (KDE // NCH)
            c.dma(g, "g", d["wgB"][r0w:r0w + H // NCH, :], d["wgS"][r0w:r0w + H // NCH, :])
            c.dma(g, "g", d["wuB"][r0w:r0w + H // NCH, :], d["wuS"][r0w:r0w + H // NCH, :])
            c.dma(g, "g", d["wdB"][r0d:r0d + KDE // NCH, :], d["wdS"][r0d:r0d + KDE // NCH, :])

        @block.scalar
        def _(sc):
            c.w(sc, "g", ar2_done)
            ins = sc.activation(out=esb[0:1, 0:1024], in_=n2row[0:1, 0:1024],
                                func=AF.Square, accum_out=n2row[0:1, 1025:1026])
            c.inc(ins, "c")
        glnsq = c.cnt["c"]

        @block.vector
        def _(v):
            c.w(v, "c", glnsq)
            ins = v.tensor_tensor(out=n2row[0:1, 1025:1026], in0=n2row[0:1, 1025:1026],
                                  in1=n2row[0:1, 1024:1025], op=ALU.add)
            c.inc(ins, "v")

        @block.scalar
        def _(sc):
            c.w(sc, "v")
            ins = sc.activation(out=n2row[0:1, 1025:1026], in_=n2row[0:1, 1025:1026],
                                func=AF.Sqrt)
            c.inc(ins, "c")
        tot_done = c.cnt["c"]

        @block.vector
        def _(v):
            c.w(v, "c", tot_done)
            v.tensor_scalar(out=n2row[0:1, 1025:1026], in0=n2row[0:1, 1025:1026],
                            scalar1=1e-6, scalar2=None, op0=ALU.add)
            ins = v.reciprocal(out=n2row[0:1, 1024:1025], in_=n2row[0:1, 1025:1026])
            c.vsw(v, ins)
            c.vsw(v, v.tensor_scalar(out=n2row[0:1, 1024:1025], in0=n2row[0:1, 1024:1025],
                                     scalar1=1.0, scalar2=None, op0=ALU.min))
            v.tensor_copy(out=clipb[0:1, 0:1], in_=n2row[0:1, 1024:1025])
            ins = v.tensor_scalar(out=clipb[0:1, 1:2], in0=n2row[0:1, 1024:1025],
                                  scalar1=-1.0, scalar2=None, op0=ALU.mult)
            c.inc(ins, "v")
        clip_done = c.cnt["v"]

        @block.tensor
        def _(t):
            c.w(t, "v", clip_done)
            ins = t.matmul(P[2][:, 0:2], st.ones_r[:], clipb[0:1, 0:2],
                           start=True, stop=True)
            c.inc(ins, "t")
        clip_mm = c.cnt["t"]

        @block.vector
        def _(v):
            c.w(v, "t", clip_mm)
            v.tensor_copy(out=ncol[:], in_=P[2][:, 1:2])
            v.drain()

            bcol = st.BT[:, ch:ch + 1]
            ecol = st.ET[:, ch:ch + 1]
            v.tensor_scalar(out=s0f[:], in0=s0f[:], scalar1=ecol, scalar2=None,
                            op0=ALU.mult)
            v.scalar_tensor_tensor(out=s0f[:], in0=dW0sb[:], scalar=ncol[:],
                                   in1=s0f[:], op0=ALU.mult, op1=ALU.add)
            v.tensor_scalar(out=s1j[:], in0=s1j[:], scalar1=ecol, scalar2=None,
                            op0=ALU.mult)
            v.scalar_tensor_tensor(out=s1j[:], in0=dW1sb[:], scalar=ncol[:],
                                   in1=s1j[:], op0=ALU.mult, op1=ALU.add)
            v.tensor_scalar(out=w0f[:], in0=w0f[:], scalar1=bcol, scalar2=None,
                            op0=ALU.mult)
            v.tensor_tensor(out=w0f[:], in0=w0f[:], in1=s0f[:], op=ALU.add)
            v.tensor_scalar(out=w1j[:], in0=w1j[:], scalar1=bcol, scalar2=None,
                            op0=ALU.mult)
            v.tensor_tensor(out=w1j[:], in0=w1j[:], in1=s1j[:], op=ALU.add)
            brow = st.BErow[0:1, ch:ch + 1]
            erow = st.BErow[0:1, 32 + ch:33 + ch]
            ins = v.tensor_scalar(out=sln_row[:], in0=sln_row[:], scalar1=erow,
                                  scalar2=None, op0=ALU.mult)
            c.vsw(v, ins)
            ins = v.scalar_tensor_tensor(out=sln_row[:], in0=n2row[0:1, 0:1024],
                                         scalar=n2row[0:1, 1024:1025], in1=sln_row[:],
                                         op0=ALU.mult, op1=ALU.subtract)
            c.vsw(v, ins)
            ins = v.tensor_scalar(out=sln_row[:], in0=sln_row[:], scalar1=-1.0,
                                  scalar2=None, op0=ALU.mult)
            c.vsw(v, ins)
            ins = v.tensor_scalar(out=ln_row[:], in0=ln_row[:], scalar1=brow,
                                  scalar2=None, op0=ALU.mult)
            c.vsw(v, ins)
            ins = v.tensor_tensor(out=ln_row[:], in0=ln_row[:], in1=sln_row[:], op=ALU.add)
            c.vsw(v, ins)
            v.tensor_copy(out=ln_bfr[:], in_=ln_row[:])
            v.tensor_copy(out=w0f_bf[:], in_=w0f[:])
            ins = v.tensor_copy(out=w1j_bf[:], in_=w1j[:])
            c.inc(ins, "v")
        upd_done = c.cnt["v"]

        if ch + 1 < NCH:
            w1hc_ready = emit_w1hc(upd_done)
            kf_rdy = kf_rdy_next
            kt_rdy = kt_rdy_next
            db_rdy = db_rdy_next

        if dbg.get("scan_ch") == ch:
            @block.gpsimd
            def _(g):
                c.w(g, "v", upd_done)
                c.dma(g, "g", st.dbg_ap["w0f"], w0f[:])
                c.dma(g, "g", st.dbg_ap["w1j"], w1j[:])
                c.dma(g, "g", st.dbg_ap["lnrow"], ln_row[:])
                c.dma(g, "g", st.dbg_ap["yblk"], yblk[:])
                c.dma(g, "g", st.dbg_ap["dyall"], d["dy_all"][:, :])
                c.dma(g, "g", st.dbg_ap["n2row"], n2row[:])
                c.dma(g, "g", st.dbg_ap["GS"], st.GS[:])
                c.dma(g, "g", st.dbg_ap["Qw"], Qw[:])
                c.dma(g, "g", st.dbg_ap["ccol"], ccol[:])
                c.dma(g, "g", st.dbg_ap["esb"], esb[:])
                c.dma(g, "g", st.dbg_ap["t1s"], t1s[:])
                c.dma(g, "g", st.dbg_ap["wcol"], wcol[:])
                c.dma(g, "g", st.dbg_ap["nrow"], nrow[:])
                c.dma(g, "g", st.dbg_ap["nri"], d["nr_in"][:, :])
                c.dma(g, "g", st.dbg_ap["nro"], d["nr_out"][:, :])

    st.upd_done = upd_done
    st.wcast_done = c.cnt["g"]




def final_attn(st):
    nc, c, block, d = st.nc, st.c, st.block, st.dram
    sb = st.sb
    P = st.P

    @block.gpsimd
    def _(g):
        c.w(g, "v", st.upd_done)
        c.dma(g, "g", d["w0_in"][:, :].rearrange("(k p) j -> p k j", p=128),
              st.w0f_bf[:].rearrange("p (k j) -> p k j", k=8))
        c.dma(g, "g", d["w1_in"][:, :].rearrange("(k p) h -> p k h", p=128),
              st.w1j_bf[:].rearrange("p (k h) -> p k h", k=2))
        c.w(g, "g")
        ins = g.collective_compute("AllGather", ALU.bypass, RG,
                                   ins=[d["w0_in"].opt()], outs=[d["w0_all"].opt()])
        c.inc(ins, "g", 1)
        ins = g.collective_compute("AllGather", ALU.bypass, RG,
                                   ins=[d["w1_in"].opt()], outs=[d["w1_all"].opt()])
        c.inc(ins, "g", 1)
    mem_ag = c.cnt["g"]

    w0a = sb("w0a", [128, 8 * IN])
    w1a = sb("w1a", [128, 16 * H])
    qt = sb("qt", [128, 8 * 512])
    h2 = sb("h2", [128, 16 * 512])
    y2sb = sb("y2sb", [128, 8 * 512])
    zb = sb("zb", [128, 512])
    sqb = sb("sqb", [128, 512])
    srow = sb("srow", [1, 1024], F32)
    srow_bf = sb("srow_bf", [1, 1024])
    RB2 = sb("RB2", [128, 512])
    MB2 = sb("MB2", [128, 512])
    lnq = sb("lnq", [128, 8], F32)
    lnbf32 = sb("lnbf32", [128, 1024], F32)
    xstr = sb("xstr", [128, 512], F32)
    vrow = sb("vrow", [1, 512], F32)
    zfp = sb("zfp", [128, 8 * 512], F32)
    x1o = sb("x1o", [128, 512], F32)

    @block.sync
    def _(sy):
        c.w(sy, "g", mem_ag)
        for rr in range(8):
            c.dma(sy, "q",
                  w0a[:].rearrange("p (k r j) -> p k r j", k=8, r=8)[:, :, rr, :],
                  d["w0_all"][rr * H:(rr + 1) * H, :]
                  .rearrange("(k p) j -> p k j", p=128))
        c.dma(sy, "q", w1a[:].rearrange("p (k h) -> p k h", k=16),
              d["w1_all"].rearrange("(k p) h -> p k h", p=128))
    mem_ld = c.cnt["q"]

    @block.tensor
    def _(t):
        c.w(t, "v", st.upd_done)
        ins = None
        for nh in range(2):
            ins = t.matmul(P[0][:, nh * 512:nh * 512 + 512], st.ones_r[:],
                           st.ln_bfr[0:1, nh * 512:nh * 512 + 512], start=True, stop=True)
        c.inc(ins, "t")

    @block.vector
    def _(v):
        c.w(v, "t")
        ins = v.tensor_copy(out=lnbf32[:], in_=P[0][:])
        c.inc(ins, "v")
    lnb32_done = c.cnt["v"]

    @block.tensor
    def _(t):
        c.w(t, "v", lnb32_done)
        ins = None
        for m in range(8):
            ins = t.transpose(P[1][:, m * 128:(m + 1) * 128],
                              lnbf32[:, m * 128:(m + 1) * 128], st.eye_f32[:])
        c.inc(ins, "t")
    lnq_mm = c.cnt["t"]

    @block.vector
    def _(v):
        c.w(v, "t", lnq_mm)
        ins = None
        for m in range(8):
            ins = v.tensor_copy(out=lnq[:, m:m + 1], in_=P[1][:, m * 128:m * 128 + 1])
        c.inc(ins, "v")
    lnq_done = c.cnt["v"]

    for tt in range(8):
        bb = tt // 4

        @block.sync
        def _(sy):
            c.w(sy, "q", mem_ld)
            if tt > 0:
                c.w(sy, "v")
            c.dma(sy, "q", qt[:].rearrange("p (k t) -> p k t", k=8),
                  d["q_dram"][:, tt * 512:(tt + 1) * 512]
                  .rearrange("(k p) t -> p k t", p=128))
        q_ld = c.cnt["q"]

        for im in range(16):
            qtr = [P[0], P[0], P[1], P[1]][im % 4]
            half = (im % 2) * 512

            @block.tensor
            def _(t):
                if im == 0:
                    c.w(t, "q", q_ld)
                    c.w(t, "v", lnq_done)
                if im >= 4:
                    c.w(t, "c")
                ins = None
                for k in range(8):
                    ins = t.matmul(
                        qtr[:, half:half + 512],
                        w0a[:, k * IN + im * 128:k * IN + im * 128 + 128],
                        qt[:, k * 512:(k + 1) * 512],
                        start=(k == 0), stop=(k == 7))
                c.inc(ins, "t")
            mm_done = c.cnt["t"]

            @block.scalar
            def _(sc):
                c.w(sc, "t", mm_done)
                ins = sc.activation(out=h2[:, im * 512:(im + 1) * 512],
                                    in_=qtr[:, half:half + 512], func=AF.Silu)
                c.inc(ins, "c")
        h2_done = c.cnt["c"]

        for hm in range(8):
            half = (hm % 2) * 512

            @block.tensor
            def _(t):
                if hm == 0:
                    c.w(t, "c", h2_done)
                if hm >= 2:
                    c.w(t, "v")
                    c.w(t, "c")
                ins = None
                for k in range(16):
                    ins = t.matmul(
                        P[2][:, half:half + 512],
                        w1a[:, k * H + hm * 128:k * H + hm * 128 + 128],
                        h2[:, k * 512:(k + 1) * 512],
                        start=(k == 0), stop=(k == 15))
                c.inc(ins, "t")
            mm_done = c.cnt["t"]

            @block.scalar
            def _(sc):
                c.w(sc, "t", mm_done)
                ins = sc.activation(out=sqb[:], in_=P[2][:, half:half + 512],
                                    func=AF.Square)
                c.inc(ins, "c")
            sq_done = c.cnt["c"]

            @block.vector
            def _(v):
                c.w(v, "t", mm_done)
                ins = v.tensor_copy(out=y2sb[:, hm * 512:(hm + 1) * 512],
                                    in_=P[2][:, half:half + 512])
                c.inc(ins, "v")

            @block.tensor
            def _(t):
                c.w(t, "c", sq_done)
                ins = t.matmul(P[3][0:1, 0:512], st.ones_c[:], sqb[:],
                               start=(hm == 0), stop=(hm == 7))
                if hm == 7:
                    c.inc(ins, "t")
        ss_mm = c.cnt["t"]

        @block.vector
        def _(v):
            c.w(v, "t", ss_mm)
            ins = v.tensor_scalar(out=srow[0:1, 0:512], in0=P[3][0:1, 0:512],
                                  scalar1=1.0 / float(H), scalar2=EPS,
                                  op0=ALU.mult, op1=ALU.add)
            c.inc(ins, "v")

        @block.scalar
        def _(sc):
            c.w(sc, "v")
            ins = sc.activation(out=srow[0:1, 0:512], in_=srow[0:1, 0:512],
                                func=AF.Sqrt)
            c.inc(ins, "c")

        @block.vector
        def _(v):
            c.w(v, "c")
            v.reciprocal(out=srow[0:1, 512:1024], in_=srow[0:1, 0:512])
            v.drain()
            ins = v.tensor_copy(out=srow_bf[0:1, 0:512], in_=srow[0:1, 512:1024])
            c.inc(ins, "v")

        @block.tensor
        def _(t):
            c.w(t, "v")
            ins = t.matmul(P[3][:, 512:1024], st.ones_r[:], srow_bf[0:1, 0:512],
                           start=True, stop=True)
            c.inc(ins, "t")

        @block.vector
        def _(v):
            c.w(v, "t")
            ins = v.tensor_copy(out=RB2[:], in_=P[3][:, 512:1024])
            c.inc(ins, "v")
        rb2_done = c.cnt["v"]

        for hm in range(8):
            @block.sync
            def _(sy):
                c.w(sy, "v", rb2_done if hm == 0 else c.cnt["v"])
                c.dma(sy, "q", xstr[:],
                      d["xfb"][bb * H + hm * 128:bb * H + hm * 128 + 128,
                               (tt % 4) * 512:(tt % 4) * 512 + 512])
            x_ld = c.cnt["q"]

            @block.vector
            def _(v):
                c.w(v, "q", x_ld)
                sl = slice(hm * 512, (hm + 1) * 512)
                v.scalar_tensor_tensor(out=zfp[:, sl], in0=y2sb[:, sl],
                                       scalar=lnq[:, hm:hm + 1], in1=RB2[:],
                                       op0=ALU.mult, op1=ALU.mult)
                v.tensor_tensor(out=zfp[:, sl], in0=zfp[:, sl], in1=qt[:, sl],
                                op=ALU.add)
                ins = v.tensor_tensor(out=zfp[:, sl], in0=zfp[:, sl], in1=xstr[:],
                                      op=ALU.add)
                c.inc(ins, "v")
            z_done = c.cnt["v"]

            @block.scalar
            def _(sc):
                c.w(sc, "v", z_done)
                sl = slice(hm * 512, (hm + 1) * 512)
                sc.activation(out=zb[:], in_=zfp[:, sl], func=AF.Identity)
                ins = sc.activation(out=sqb[:], in_=zfp[:, sl], func=AF.Square)
                c.inc(ins, "c")
            zb_done = c.cnt["c"]

            @block.tensor
            def _(t):
                c.w(t, "c", zb_done)
                t.matmul(P[0][0:1, 0:512], st.ones_c[:], zb[:],
                         start=(hm == 0), stop=(hm == 7))
                ins = t.matmul(P[0][0:1, 512:1024], st.ones_c[:], sqb[:],
                               start=(hm == 0), stop=(hm == 7))
                if hm == 7:
                    c.inc(ins, "t")
        st_mm = c.cnt["t"]

        @block.vector
        def _(v):
            c.w(v, "t", st_mm)
            v.tensor_scalar(out=srow[0:1, 0:512], in0=P[0][0:1, 0:512],
                            scalar1=1.0 / float(H), scalar2=None, op0=ALU.mult)
            ins = v.tensor_scalar(out=srow[0:1, 512:1024], in0=P[0][0:1, 512:1024],
                                  scalar1=1.0 / float(H), scalar2=None, op0=ALU.mult)
            c.vsw(v, ins)
            ins = v.tensor_tensor(out=vrow[0:1, 0:512], in0=srow[0:1, 0:512],
                                  in1=srow[0:1, 0:512], op=ALU.mult)
            c.vsw(v, ins)
            ins = v.tensor_tensor(out=vrow[0:1, 0:512], in0=srow[0:1, 512:1024],
                                  in1=vrow[0:1, 0:512], op=ALU.subtract)
            c.inc(ins, "v")
        var_done = c.cnt["v"]

        @block.vector
        def _(v):
            c.w(v, "v", var_done)
            ins = v.tensor_scalar(out=vrow[0:1, 0:512], in0=vrow[0:1, 0:512],
                                  scalar1=LNEPS, scalar2=None, op0=ALU.add)
            c.inc(ins, "v")

        @block.scalar
        def _(sc):
            c.w(sc, "v")
            ins = sc.activation(out=vrow[0:1, 0:512], in_=vrow[0:1, 0:512],
                                func=AF.Sqrt)
            c.inc(ins, "c")

        @block.vector
        def _(v):
            c.w(v, "c")
            v.reciprocal(out=srow[0:1, 512:1024], in_=vrow[0:1, 0:512])
            v.drain()
            v.tensor_copy(out=srow_bf[0:1, 0:512], in_=srow[0:1, 0:512])
            ins = v.tensor_copy(out=srow_bf[0:1, 512:1024], in_=srow[0:1, 512:1024])
            c.inc(ins, "v")
        rows_done = c.cnt["v"]

        @block.tensor
        def _(t):
            c.w(t, "v", rows_done)
            t.matmul(P[1][:, 0:512], st.ones_r[:], srow_bf[0:1, 0:512],
                     start=True, stop=True)
            ins = t.matmul(P[1][:, 512:1024], st.ones_r[:], srow_bf[0:1, 512:1024],
                           start=True, stop=True)
            c.inc(ins, "t")
        bc_mm = c.cnt["t"]

        @block.vector
        def _(v):
            c.w(v, "t", bc_mm)
            v.tensor_copy(out=MB2[:], in_=P[1][:, 0:512])
            ins = v.tensor_copy(out=RB2[:], in_=P[1][:, 512:1024])
            c.inc(ins, "v")
        bc_done = c.cnt["v"]

        for hm in range(8):
            @block.vector
            def _(v):
                if hm == 0:
                    c.w(v, "v", bc_done)
                c.w(v, "q")
                sl = slice(hm * 512, (hm + 1) * 512)
                v.tensor_tensor(out=x1o[:], in0=zfp[:, sl], in1=MB2[:], op=ALU.subtract)
                ins = v.tensor_tensor(out=x1o[:], in0=x1o[:], in1=RB2[:], op=ALU.mult)
                c.inc(ins, "v")
            x1_done = c.cnt["v"]

            @block.sync
            def _(sy):
                c.w(sy, "v", x1_done)
                c.dma(sy, "q",
                      d["x1scr"][hm * 128:(hm + 1) * 128,
                                 tt * 512:(tt + 1) * 512], x1o[:])
    st.x1_stores = c.cnt["q"]

    if st.dbg.get("x1"):
        @block.gpsimd
        def _(g):
            c.w(g, "q", st.x1_stores)
            c.dma(g, "g", st.dbg_ap["x1"], d["x1scr"][:, :])


def ffn(st):
    nc, c, block, d = st.nc, st.c, st.block, st.dram
    sb = st.sb
    P = st.P

    x1bf = sb("x1bf", [128, 8 * 1024])
    xlo = sb("xlo", [128, 8 * 1024])
    xf32 = sb("xf32", [128, 1024], F32)
    ghi = sb("ghi", [128, 8 * 16])
    glo = sb("glo", [128, 8 * 16])
    g32 = sb("g32", [128, 8 * 16], F32)
    brows = sb("brows", [1, 48])
    b32 = sb("b32", [1, 16], F32)
    rsigall = sb("rsigall", [128, 128], F32)
    rtmp = sb("rtmp", [128, 16], F32)
    eq1 = sb("eq1", [128, 16], F32)
    eq2 = sb("eq2", [128, 16], F32)
    m1c = sb("m1c", [128, 1], F32)
    m2c = sb("m2c", [128, 1], F32)
    mRc = sb("mRc", [128, 1], F32)
    wts = sb("wts", [128, 8 * NE], F32)
    wg_sb = [sb(f"wg_sb{i}", [128, 8 * 512]) for i in range(2)]
    wu_sb = [sb(f"wu_sb{i}", [128, 8 * 512]) for i in range(2)]
    sgl = sb("sgl", [128, 512])
    huw = sb("huw", [128, 512])
    act_sb = sb("act_sb", [128, 8 * 512])
    wd_t = [sb(f"wd_t{i}", [128, 1024]) for i in range(2)]
    ac_t = [sb(f"ac_t{i}", [128, 512], is_transpose_dest=True) for i in range(2)]
    zfp2 = sb("zfp2", [128, 8 * 512], F32)
    zb2 = sb("zb2", [128, 512])
    sqb2 = sb("sqb2", [128, 512])
    sr2 = sb("sr2", [1, 1024], F32)
    sr2b = sb("sr2b", [1, 1024])
    vr2 = sb("vr2", [1, 512], F32)
    MB3 = sb("MB3", [128, 512])
    RB3 = sb("RB3", [128, 512])
    xfp3 = sb("xfp3", [128, 512], F32)
    outt = sb("outt", [128, 512], F32)

    @block.gpsimd
    def _(g):
        c.w(g, "q", st.x1_stores)
        c.dma(g, "g", ghi[:].rearrange("p (k e) -> p k e", k=8), d["gwH"].rearrange("(k p) e -> p k e", p=128))
        c.dma(g, "g", g32[:].rearrange("p (k e) -> p k e", k=8), d["gwH"].rearrange("(k p) e -> p k e", p=128))
        c.dma(g, "g", brows[0:1, 0:16], d["gbH"])
        c.dma(g, "g", b32[:], d["gbH"])
    g_ld = c.cnt["g"]

    @block.vector
    def _(v):
        c.w(v, "g", g_ld)
        v.tensor_tensor(out=glo[:], in0=g32[:], in1=ghi[:], op=ALU.subtract)
        ins = v.scalar_tensor_tensor(out=brows[0:1, 16:32], in0=brows[0:1, 0:16],
                                     scalar=-1.0, in1=b32[:],
                                     op0=ALU.mult, op1=ALU.add)
        c.inc(ins, "v")
    glo_done = c.cnt["v"]

    for cc in range(4):
        col0 = cc * 1024

        @block.gpsimd
        def _(g):
            c.w(g, "q", st.x1_stores)
            if cc > 0:
                c.w(g, "t")
                c.w(g, "v")
            c.dma(g, "g", x1bf[:].rearrange("p (k t) -> p k t", k=8),
                  d["x1scr"][:, col0:col0 + 1024].rearrange("(k p) t -> p k t", p=128))
        x_ld = c.cnt["g"]

        for k in range(8):
            @block.sync
            def _(sy):
                c.w(sy, "v")
                c.dma(sy, "q", xf32[:],
                      d["x1scr"][k * 128:(k + 1) * 128, col0:col0 + 1024])
            xf_ld = c.cnt["q"]

            @block.vector
            def _(v):
                c.w(v, "q", xf_ld)
                c.w(v, "g", x_ld)
                if k == 0:
                    c.w(v, "v", glo_done)
                ins = v.tensor_tensor(out=xlo[:, k * 1024:(k + 1) * 1024],
                                      in0=xf32[:], in1=x1bf[:, k * 1024:(k + 1) * 1024],
                                      op=ALU.subtract)
                c.inc(ins, "v")
        xlo_done = c.cnt["v"]

        @block.vector
        def _(v):
            ins = v.memset(wts[:], 1.0)
            c.inc(ins, "v")
        wts_init = c.cnt["v"]

        for tm in range(8):
            @block.tensor
            def _(t):
                if tm == 0:
                    c.w(t, "v", xlo_done)
                    c.w(t, "v", wts_init)
                first = True
                for k in range(8):
                    t.matmul(P[2][:, tm * 16:tm * 16 + 16],
                             x1bf[:, k * 1024 + tm * 128:k * 1024 + tm * 128 + 128],
                             ghi[:, k * 16:(k + 1) * 16], start=first, stop=False)
                    first = False
                    t.matmul(P[2][:, tm * 16:tm * 16 + 16],
                             x1bf[:, k * 1024 + tm * 128:k * 1024 + tm * 128 + 128],
                             glo[:, k * 16:(k + 1) * 16], start=False, stop=False)
                    t.matmul(P[2][:, tm * 16:tm * 16 + 16],
                             xlo[:, k * 1024 + tm * 128:k * 1024 + tm * 128 + 128],
                             ghi[:, k * 16:(k + 1) * 16], start=False, stop=False)
                t.matmul(P[2][:, tm * 16:tm * 16 + 16], st.ones_r[:], brows[0:1, 0:16],
                         start=False, stop=False)
                ins = t.matmul(P[2][:, tm * 16:tm * 16 + 16], st.ones_r[:],
                               brows[0:1, 16:32], start=False, stop=(True))
                if tm == 7:
                    c.inc(ins, "t")
        r_mm = c.cnt["t"]

        @block.scalar
        def _(sc):
            c.w(sc, "t", r_mm)
            ins = sc.activation(out=rsigall[:], in_=P[2][:, 0:128], func=AF.Sigmoid)
            c.inc(ins, "c")
        sig_done = c.cnt["c"]

        for tm in range(8):
            @block.vector
            def _(v):
                if tm == 0:
                    c.w(v, "c", sig_done)
                rsig_t = rsigall[:, tm * 16:(tm + 1) * 16]
                v.reduce_max(out=m1c[:], in_=rsig_t, axis=mybir.AxisListType.X)
                v.drain()
                v.tensor_scalar(out=eq1[:], in0=rsig_t, scalar1=m1c[:], scalar2=None,
                                op0=ALU.is_equal)
                v.drain()
                v.scalar_tensor_tensor(out=rtmp[:], in0=eq1[:], scalar=-1e30,
                                       in1=rsig_t, op0=ALU.mult, op1=ALU.add)
                v.drain()
                v.reduce_max(out=m2c[:], in_=rtmp[:], axis=mybir.AxisListType.X)
                v.drain()
                v.tensor_scalar(out=eq2[:], in0=rtmp[:], scalar1=m2c[:],
                                scalar2=None, op0=ALU.is_equal)
                v.drain()
                v.tensor_tensor(out=eq1[:], in0=eq1[:], in1=eq2[:], op=ALU.add)
                v.tensor_tensor(out=m1c[:], in0=m1c[:], in1=m2c[:], op=ALU.add)
                v.drain()
                v.reciprocal(out=mRc[:], in_=m1c[:])
                v.drain()
                v.tensor_scalar(out=rtmp[:], in0=rsig_t, scalar1=mRc[:], scalar2=None,
                                op0=ALU.mult)
                v.drain()
                ins = v.tensor_tensor(out=wts[:, tm * NE:tm * NE + 16], in0=rtmp[:],
                                      in1=eq1[:], op=ALU.mult)
                c.inc(ins, "v")
        wts_done = c.cnt["v"]

        st_act = []
        for e in range(NE):
            ebuf = e % 2

            @block.sync
            def _(sy):
                if e >= 2:
                    c.w(sy, "t")
                c.dma(sy, "q", wg_sb[ebuf][:].rearrange("p (k i) -> p k i", k=8),
                      d["wgB"][:, e * 512:(e + 1) * 512]
                      .rearrange("(k p) i -> p k i", p=128))
                c.dma(sy, "q", wu_sb[ebuf][:].rearrange("p (k i) -> p k i", k=8),
                      d["wuB"][:, e * 512:(e + 1) * 512]
                      .rearrange("(k p) i -> p k i", p=128))
            w_ld = c.cnt["q"]

            for tm in range(8):
                qg = [P[0], P[2]][tm % 2]
                qu = [P[1], P[3]][tm % 2]
                half = ((tm // 2) % 2) * 512

                @block.tensor
                def _(t):
                    if tm == 0:
                        c.w(t, "q", w_ld)
                        c.w(t, "v", wts_done)
                        c.w(t, "g", st.wcast_done)
                    if tm >= 4:
                        c.w(t, "v")
                        c.w(t, "c")
                    ins = None
                    for k in range(8):
                        t.matmul(qg[:, half:half + 512],
                                 x1bf[:, k * 1024 + tm * 128:k * 1024 + tm * 128 + 128],
                                 wg_sb[ebuf][:, k * 512:(k + 1) * 512],
                                 start=(k == 0), stop=(k == 7))
                        ins = t.matmul(qu[:, half:half + 512],
                                       x1bf[:, k * 1024 + tm * 128:
                                            k * 1024 + tm * 128 + 128],
                                       wu_sb[ebuf][:, k * 512:(k + 1) * 512],
                                       start=(k == 0), stop=(k == 7))
                    c.inc(ins, "t")
                up_mm = c.cnt["t"]

                @block.scalar
                def _(sc):
                    c.w(sc, "t", up_mm)
                    ins = sc.activation(out=sgl[:], in_=qg[:, half:half + 512],
                                        func=AF.Silu)
                    c.inc(ins, "c")
                sg_done = c.cnt["c"]

                @block.vector
                def _(v):
                    c.w(v, "c", sg_done)
                    if tm == 0:
                        c.w(v, "s")
                    v.tensor_scalar(out=huw[:], in0=qu[:, half:half + 512],
                                    scalar1=wts[:, tm * NE + e:tm * NE + e + 1],
                                    scalar2=None, op0=ALU.mult)
                    ins = v.tensor_tensor(out=act_sb[:, tm * 512:(tm + 1) * 512],
                                          in0=sgl[:], in1=huw[:], op=ALU.mult)
                    c.inc(ins, "v")
            a_done = c.cnt["v"]

            @block.scalar
            def _(sc):
                c.w(sc, "v", a_done)
                c.dma(sc, "s",
                      d["act_dram"][e * 1024:(e + 1) * 1024, :]
                      .rearrange("(tm p) i -> p tm i", p=128),
                      act_sb[:].rearrange("p (tm i) -> p tm i", tm=8))
            st_act.append(c.cnt["s"])

        for th in range(2):
            nk = KDE // 128
            ld_hist = []
            mm_hist = []
            for k in range(nk):
                kb = k % 2
                e_of_k = k // 4

                @block.sync
                def _(sy):
                    if k == 0:
                        c.w(sy, "s", st_act[-1])
                        c.w(sy, "t")
                    if k >= 2:
                        c.w(sy, "t", mm_hist[k - 2])
                    c.dma(sy, "q", wd_t[kb][:], d["wdB"][k * 128:(k + 1) * 128, :])
                ld_w = c.cnt["q"]

                @block.scalar
                def _(sc):
                    if k == 0:
                        c.w(sc, "s", st_act[-1])
                        c.w(sc, "t")
                    if k >= 2:
                        c.w(sc, "t", mm_hist[k - 2])
                    c.tr(sc, "s", ac_t[kb][:],
                         d["act_dram"][e_of_k * 1024 + th * 512:
                                       e_of_k * 1024 + th * 512 + 512,
                                       (k % 4) * 128:(k % 4) * 128 + 128])
                ld_hist.append((ld_w, c.cnt["s"]))

                @block.tensor
                def _(t):
                    c.w(t, "q", ld_hist[-1][0])
                    c.w(t, "s", ld_hist[-1][1])
                    ins = None
                    for m in range(8):
                        ins = t.matmul(P[m // 2][:, (m % 2) * 512:(m % 2) * 512 + 512],
                                       wd_t[kb][:, m * 128:(m + 1) * 128],
                                       ac_t[kb][:],
                                       start=(k == 0), stop=(k == nk - 1))
                    c.inc(ins, "t")
                mm_hist.append(c.cnt["t"])
            dn_mm = c.cnt["t"]

            for m in range(8):
                @block.sync
                def _(sy):
                    if m == 0:
                        c.w(sy, "t", dn_mm)
                    c.w(sy, "v")
                    c.dma(sy, "q", xfp3[:],
                          d["x1scr"][m * 128:(m + 1) * 128,
                                     col0 + th * 512:col0 + th * 512 + 512])
                xl = c.cnt["q"]

                @block.vector
                def _(v):
                    c.w(v, "q", xl)
                    ins = v.scalar_tensor_tensor(
                        out=zfp2[:, m * 512:(m + 1) * 512], in0=xfp3[:], scalar=2.0,
                        in1=P[m // 2][:, (m % 2) * 512:(m % 2) * 512 + 512],
                        op0=ALU.mult, op1=ALU.add)
                    c.inc(ins, "v")
            z2_done = c.cnt["v"]

            if cc == 0 and th == 0 and st.dbg.get("ffn0"):
                @block.gpsimd
                def _(g):
                    c.w(g, "v", z2_done)
                    c.dma(g, "g", st.dbg_ap["zfp2"], zfp2[:])

            for m in range(8):
                @block.scalar
                def _(sc):
                    if m == 0:
                        c.w(sc, "v", z2_done)
                    sl = slice(m * 512, (m + 1) * 512)
                    sc.activation(out=zb2[:], in_=zfp2[:, sl], func=AF.Identity)
                    ins = sc.activation(out=sqb2[:], in_=zfp2[:, sl], func=AF.Square)
                    c.inc(ins, "c")
                zb2_done = c.cnt["c"]

                @block.tensor
                def _(t):
                    c.w(t, "c", zb2_done)
                    t.matmul(P[0][0:1, 0:512], st.ones_c[:], zb2[:],
                             start=(m == 0), stop=(m == 7))
                    ins = t.matmul(P[0][0:1, 512:1024], st.ones_c[:], sqb2[:],
                                   start=(m == 0), stop=(m == 7))
                    if m == 7:
                        c.inc(ins, "t")
            st2_mm = c.cnt["t"]

            @block.vector
            def _(v):
                c.w(v, "t", st2_mm)
                v.tensor_scalar(out=sr2[0:1, 0:512], in0=P[0][0:1, 0:512],
                                scalar1=1.0 / float(H), scalar2=None, op0=ALU.mult)
                ins = v.tensor_scalar(out=sr2[0:1, 512:1024], in0=P[0][0:1, 512:1024],
                                      scalar1=1.0 / float(H), scalar2=None, op0=ALU.mult)
                c.vsw(v, ins)
                ins = v.tensor_tensor(out=vr2[0:1, 0:512], in0=sr2[0:1, 0:512],
                                      in1=sr2[0:1, 0:512], op=ALU.mult)
                c.vsw(v, ins)
                ins = v.tensor_tensor(out=vr2[0:1, 0:512], in0=sr2[0:1, 512:1024],
                                      in1=vr2[0:1, 0:512], op=ALU.subtract)
                c.inc(ins, "v")
            v2_done = c.cnt["v"]

            @block.vector
            def _(v):
                c.w(v, "v", v2_done)
                ins = v.tensor_scalar(out=vr2[0:1, 0:512], in0=vr2[0:1, 0:512],
                                      scalar1=LNEPS, scalar2=None, op0=ALU.add)
                c.inc(ins, "v")

            @block.scalar
            def _(sc):
                c.w(sc, "v")
                ins = sc.activation(out=vr2[0:1, 0:512], in_=vr2[0:1, 0:512],
                                    func=AF.Sqrt)
                c.inc(ins, "c")

            @block.vector
            def _(v):
                c.w(v, "c")
                v.reciprocal(out=sr2[0:1, 512:1024], in_=vr2[0:1, 0:512])
                v.drain()
                v.tensor_copy(out=sr2b[0:1, 0:512], in_=sr2[0:1, 0:512])
                ins = v.tensor_copy(out=sr2b[0:1, 512:1024], in_=sr2[0:1, 512:1024])
                c.inc(ins, "v")
            rows2 = c.cnt["v"]

            @block.tensor
            def _(t):
                c.w(t, "v", rows2)
                t.matmul(P[1][:, 0:512], st.ones_r[:], sr2b[0:1, 0:512],
                         start=True, stop=True)
                ins = t.matmul(P[1][:, 512:1024], st.ones_r[:], sr2b[0:1, 512:1024],
                               start=True, stop=True)
                c.inc(ins, "t")
            bc2_mm = c.cnt["t"]

            @block.vector
            def _(v):
                c.w(v, "t", bc2_mm)
                v.tensor_copy(out=MB3[:], in_=P[1][:, 0:512])
                ins = v.tensor_copy(out=RB3[:], in_=P[1][:, 512:1024])
                c.inc(ins, "v")
            bc2_done = c.cnt["v"]

            for m in range(8):
                @block.vector
                def _(v):
                    if m == 0:
                        c.w(v, "v", bc2_done)
                    c.w(v, "q")
                    sl = slice(m * 512, (m + 1) * 512)
                    v.tensor_tensor(out=outt[:], in0=zfp2[:, sl], in1=MB3[:],
                                    op=ALU.subtract)
                    ins = v.tensor_tensor(out=outt[:], in0=outt[:], in1=RB3[:],
                                          op=ALU.mult)
                    c.inc(ins, "v")
                o_done = c.cnt["v"]

                @block.sync
                def _(sy):
                    c.w(sy, "v", o_done)
                    c.dma(sy, "q",
                          d["yout"][m * 128:(m + 1) * 128,
                                    col0 + th * 512:col0 + th * 512 + 512], outt[:])

    @block.gpsimd
    def _(g):
        for sname in ("q", "g", "s", "v", "c", "t"):
            c.w(g, sname)


def build_nc(dbg=None):
    dbg = dict(dbg or {})
    nc = bass.Bass("TRN2", target_bir_lowering=False)
    d = {}

    def din(name, shape, dt=F32):
        d[name] = nc.dram_tensor(name, shape, dt, kind="ExternalInput").ap()

    def dou(name, shape, dt=F32):
        d[name] = nc.dram_tensor(name, shape, dt, kind="ExternalOutput").ap()

    def dit(name, shape, dt=BF16):
        d[name] = nc.dram_tensor(name, shape, dt).ap()

    din("xfc", [CPC * H, T]); din("xfb", [2 * H, S])
    din("wqT", [H, H]); din("wkT", [H, H]); din("wvT", [H, H])
    din("qn", [128, 8]); din("kn", [128, 8]); din("knrow", [1, H])
    din("twT", [H, T]); din("twA", [H, T]); din("twE", [H, T])
    din("w0fs", [128, 8 * JS]); din("w1js", [128, 2 * H]); din("lnr", [1, H])
    din("eselh", [16, 128]); din("onesh", [128, 128]); din("eyeh", [128, 128])
    din("gwH", [H, 16]); din("gbH", [1, 16])
    din("wgS", [H, KDE]); din("wuS", [H, KDE]); din("wdS", [KDE, H])
    dou("yout", [H, TPC])

    dit("kd_in", [CPC * H, T]); dit("kd_all", [NCORE * CPC * H, T])
    dit("sm_in", [CPC, 32], F32); dit("sm_all", [NCH, 32], F32)
    dit("q_dram", [H, TPC])
    dit("ktok_dram", [TPC, H]); dit("vtok_dram", [TPC, H])
    dit("d_own", [NCH * 128, H])
    dit("hU_dram", [2 * JS, T])
    dit("yrs_in", [T, H]); dit("yrs_out", [128, H])
    dit("dy_in", [128, H]); dit("dy_all", [T, H])
    dit("nr_in", [1, 1536], F32); dit("nr_out", [1, 1536], F32)
    dit("w1scr", [JS, H])
    dit("w0_in", [H, JS]); dit("w0_all", [NCORE * H, JS])
    dit("w1_in", [JS, H]); dit("w1_all", [IN, H])
    dit("x1scr", [H, TPC], F32)
    dit("wgB", [H, KDE]); dit("wuB", [H, KDE]); dit("wdB", [KDE, H])
    dit("act_dram", [NE * T, 512])

    dbg_shapes = {
        "kall": ([NCORE * CPC * H, T], BF16), "sm": ([NCH, 32], F32),
        "qd": ([H, TPC], BF16), "down": ([NCH * 128, H], BF16),
        "w0f": ([128, 8 * JS], F32), "w1j": ([128, 2 * H], F32),
        "lnrow": ([1, H], F32), "yblk": ([128, H], F32),
        "dyall": ([T, H], BF16), "n2row": ([1, 1536], F32),
        "x1": ([H, TPC], F32),
        "ktd": ([TPC, H], BF16), "vtd": ([TPC, H], BF16),
        "GS": ([128, NCH], F32), "Qw": ([128, H], BF16), "ccol": ([128, 1], F32),
        "esb": ([128, H], F32), "t1s": ([128, H], F32), "wcol": ([128, 1], F32), "nrow": ([1, 1536], F32),
        "nri": ([1, 1536], F32), "nro": ([1, 1536], F32),
        "wts": ([128, 8 * NE], F32), "actd": ([NE * T, 512], BF16),
        "zfp2": ([128, 8 * 512], F32),
    }
    dbg_ap = {}
    need = set()
    if dbg.get("pa"):
        need |= {"kall", "sm", "qd", "down", "ktd", "vtd"}
    if dbg.get("scan_ch") is not None:
        need |= {"w0f", "w1j", "lnrow", "yblk", "dyall", "n2row",
                 "GS", "Qw", "ccol", "esb", "t1s", "wcol", "nrow", "nri", "nro"}
    if dbg.get("x1"):
        need.add("x1")
    if dbg.get("ffn0"):
        need |= {"wts", "actd", "zfp2"}
    for name in need:
        shp, dt = dbg_shapes[name]
        dbg_ap[name] = nc.dram_tensor("dbg_" + name, shp, dt, kind="ExternalOutput").ap()

    es = contextlib.ExitStack()
    with es:
        c = Ctx(nc, es)
        block = es.enter_context(nc.Block())

        def mk_sb(stack):
            return lambda name, shape, dt=BF16, **kw: stack.enter_context(
                nc.sbuf_tensor(name, shape, dt, **kw))

        psb = mk_sb(es)
        st = types.SimpleNamespace(nc=nc, c=c, block=block, dram=d, dbg=dbg,
                                   dbg_ap=dbg_ap)
        st.ones_r = psb("ones_r", [1, 128])
        st.ones_c = psb("ones_c", [128, 1])
        st.eye_bf = psb("eye_bf", [128, 128])
        st.eye_f32 = psb("eye_f32", [128, 128], F32)
        st.knrow_bf = psb("knrow_bf", [1, H])
        st.GS = psb("GS", [128, NCH], F32)
        st.BT = psb("BT", [128, NCH], F32)
        st.ET = psb("ET", [128, NCH], F32)
        st.BErow = psb("BErow", [1, 64], F32)
        st.w0f = psb("w0f", [128, 8 * JS], F32)
        st.w1j = psb("w1j", [128, 2 * H], F32)
        st.s0f = psb("s0f", [128, 8 * JS], F32)
        st.s1j = psb("s1j", [128, 2 * H], F32)
        st.w0f_bf = psb("w0f_bf", [128, 8 * JS])
        st.w1j_bf = psb("w1j_bf", [128, 2 * H])
        st.w1hc_bf = psb("w1hc_bf", [128, 8 * JS], is_transpose_dest=True)
        st.ln_row = psb("ln_row", [1, H], F32)
        st.sln_row = psb("sln_row", [1, H], F32)
        st.ln_bfr = psb("ln_bfr", [1, H])
        st.P = [es.enter_context(nc.psum_tensor(f"P{i}", [128, 1024], F32))
                for i in range(4)]

        @block.sync
        def _(sy):
            c.dma(sy, "q", st.eye_f32[:], d["eyeh"])

        esA = contextlib.ExitStack()
        st.sb = mk_sb(esA)
        phase_a(st)
        if dbg.get("pa"):
            @block.gpsimd
            def _(g):
                for sname in ("q", "g", "s", "v", "c", "t"):
                    c.w(g, sname)
                c.dma(g, "g", dbg_ap["kall"], d["kd_all"][:, :])
                c.dma(g, "g", dbg_ap["sm"], d["sm_all"][:, :])
                c.dma(g, "g", dbg_ap["qd"], d["q_dram"][:, :])
                c.dma(g, "g", dbg_ap["down"], d["d_own"][:, :])
                c.dma(g, "g", dbg_ap["ktd"], d["ktok_dram"][:, :])
                c.dma(g, "g", dbg_ap["vtd"], d["vtok_dram"][:, :])
        c.barrier(block)
        esA.close()

        if not dbg.get("stop_pa"):
            esS = contextlib.ExitStack()
            st.sb = mk_sb(esS)
            scan(st)
            c.barrier(block)
            esS.close()

        if not dbg.get("stop_pa") and not dbg.get("stop_scan"):
            esF = contextlib.ExitStack()
            st.sb = mk_sb(esF)
            final_attn(st)
            c.barrier(block)
            esF.close()

            esN = contextlib.ExitStack()
            st.sb = mk_sb(esN)
            ffn(st)
            c.barrier(block)
            esN.close()

        @block.gpsimd
        def _(g):
            for sname in ("q", "g", "s", "v", "c", "t"):
                c.w(g, sname)
    return nc


def _host_maps(inputs):
    f32 = np.float32
    x = np.ascontiguousarray(np.asarray(inputs["L0_attn_x"], f32))
    wq = np.asarray(inputs["L0_attn_wq_w"], f32)
    wk = np.asarray(inputs["L0_attn_wk_w"], f32)
    wv = np.asarray(inputs["L0_attn_wv_w"], f32)
    qnw = np.asarray(inputs["L0_attn_q_norm_w"], f32)
    knw = np.asarray(inputs["L0_attn_k_norm_w"], f32)
    aw = np.asarray(inputs["L0_attn_alpha_w"], f32)
    tw = np.asarray(inputs["L0_attn_theta_w"], f32)
    ew = np.asarray(inputs["L0_attn_eta_w"], f32)
    m0 = np.asarray(inputs["L0_attn_mem_w0"], f32)
    m1 = np.asarray(inputs["L0_attn_mem_w1"], f32)
    mln = np.asarray(inputs["L0_attn_mem_ln_w"], f32)
    gw = np.asarray(inputs["L0_ffn_gate_w"], f32)
    gb = np.asarray(inputs["L0_ffn_gate_b"], f32)
    swg = np.asarray(inputs["L0_ffn_shared_wg"], f32)
    swu = np.asarray(inputs["L0_ffn_shared_wu"], f32)
    swd = np.asarray(inputs["L0_ffn_shared_wd"], f32)
    ewg = np.asarray(inputs["L0_ffn_experts_wg"], f32)
    ewu = np.asarray(inputs["L0_ffn_experts_wu"], f32)
    ewd = np.asarray(inputs["L0_ffn_experts_wd"], f32)

    CC = np.ascontiguousarray
    shared = {
        "wqT": CC(wq.T), "wkT": CC(wk.T), "wvT": CC(wv.T),
        "qn": CC(qnw.reshape(8, 128).T), "kn": CC(knw.reshape(8, 128).T),
        "knrow": CC(knw.reshape(1, H)),
        "twT": CC(np.tile(tw.reshape(CH, H).T, (1, B))),
        "twA": CC(np.tile(aw.reshape(CH, H).T, (1, B))),
        "twE": CC(np.tile(ew.reshape(CH, H).T, (1, B))),
        "lnr": CC(mln.reshape(1, H)),
        "onesh": np.ones((128, 128), f32),
        "eyeh": np.eye(128, dtype=f32),
        "gwH": CC(gw), "gbH": CC(gb.reshape(1, 16)),
        "wgS": CC(np.concatenate(
            [ewg.transpose(1, 0, 2).reshape(H, 16 * 512), swg], axis=1)),
        "wuS": CC(np.concatenate(
            [ewu.transpose(1, 0, 2).reshape(H, 16 * 512), swu], axis=1)),
        "wdS": CC(np.concatenate([ewd.reshape(16 * 512, H), swd], axis=0)),
    }
    w0T = m0.T
    xc = x.reshape(B, NCH, CH, H)
    in_maps = []
    for r in range(NCORE):
        esel = np.zeros((16, 128), f32)
        for p in range(128):
            esel[2 * r + p // CH, p] = 1.0
        m = dict(shared)
        m["xfc"] = CC(xc[:, 4 * r:4 * r + 4].transpose(1, 3, 0, 2)
                      .reshape(CPC * H, T))
        m["xfb"] = CC(x[2 * r:2 * r + 2].transpose(0, 2, 1).reshape(2 * H, S))
        m["w0fs"] = CC(w0T[:, JS * r:JS * (r + 1)].reshape(8, 128, JS)
                       .transpose(1, 0, 2).reshape(128, 8 * JS))
        m["w1js"] = CC(m1[:, JS * r:JS * (r + 1)].T.reshape(2, 128, H)
                       .transpose(1, 0, 2).reshape(128, 2 * H))
        m["eselh"] = esel
        in_maps.append(m)
    return in_maps


def kernel(**inputs):
    in_maps = _host_maps(inputs)
    nc = build_nc()
    res = run_bass_kernel_spmd(nc, in_maps, core_ids=list(range(NCORE)))
    outs = []
    for r in range(NCORE):
        y = np.asarray(res.results[r]["yout"], dtype=np.float32)
        outs.append(y.T.reshape(2, S, H))
    return np.ascontiguousarray(np.concatenate(outs, axis=0))


# version 03
# speedup vs baseline: 1.0033x; 1.0033x over previous
import contextlib
import types
import numpy as np
import ml_dtypes
import concourse.bass as bass
import concourse.mybir as mybir
from concourse.bass_utils import run_bass_kernel_spmd

F32 = mybir.dt.float32
BF16 = mybir.dt.bfloat16
AF = mybir.ActivationFunctionType
ALU = mybir.AluOpType
bfd = ml_dtypes.bfloat16

RG = [[0, 1, 2, 3, 4, 5, 6, 7]]
NCORE = 8
B, S, H = 16, 2048, 1024
CH = 64
NCH = 32
IN = 2048
JS = IN // NCORE
T = B * CH
NLOSS = float(B * CH * H)
EPS = 1e-6
LNEPS = 1e-5
BASE_LR = 0.01
CPC = NCH // NCORE
TPC = 2 * S
NE = 17
KDE = NE * 512


class Ctx:
    def __init__(self, nc, es):
        self.nc = nc
        self.sems = {}
        self.cnt = {}
        for n in ("q", "g", "s", "v", "c", "t"):
            self.sems[n] = es.enter_context(nc.semaphore("sem_" + n))
            self.cnt[n] = 0

    def inc(self, instr, name, k=1):
        instr.then_inc(self.sems[name], k)
        self.cnt[name] += k
        return self.cnt[name]

    def dma(self, eng, qn, out, in_, **kw):
        return self.inc(eng.dma_start(out=out, in_=in_, **kw), qn, 16)

    def tr(self, eng, qn, out, in_):
        return self.inc(eng.dma_start_transpose(out=out, in_=in_), qn, 16)

    def w(self, eng, name, val=None):
        eng.wait_ge(self.sems[name], self.cnt[name] if val is None else val)

    def vsw(self, eng, ins, name="v"):
        eng.drain()

    def barrier(self, block):
        snap = dict(self.cnt)
        for ename in ("sync", "gpsimd", "scalar", "vector", "tensor"):
            dec = getattr(block, ename)

            @dec
            def _(e):
                for sname, v in snap.items():
                    e.wait_ge(self.sems[sname], v)


def _mmslice(buf, col0, width):
    return buf[:, col0:col0 + width]


def phase_a(st):
    nc, c, block, d = st.nc, st.c, st.block, st.dram
    sb = st.sb
    P = st.P

    knq = sb("knq", [128, 8], F32)
    qnq = sb("qnq", [128, 8], F32)
    wT_bf = sb("wT_bf", [128, 8 * H])
    tw_bf = sb("tw_bf", [128, 8 * T])
    xck = sb("xck", [128, 8 * T])
    silk = sb("silk", [128, 8 * T])
    kout = sb("kout", [128, 8 * T])
    sqs = sb("sqs", [128, 1024])
    rrow = sb("rrow", [1, 1024], F32)
    rrow2 = sb("rrow2", [1, 64], F32)
    rrowR = sb("rrowR", [1, 1024], F32)
    rstcR = sb("rstcR", [128, 1], F32)
    rbf = sb("rbf", [1, 1024])
    RBs = sb("RBs", [128, 1024])
    smrow = sb("smrow", [1, 32], F32)
    KNB = sb("KNB", [128, 1024])
    xbk = sb("xbk", [128, 8 * S])
    silt = sb("silt", [128, 1024])
    vtok = sb("vtok", [128, 1024])
    ktok = sb("ktok", [128, 1024])
    ssc = sb("ssc", [128, 1], F32)
    rstc = sb("rstc", [128, 1], F32)

    @block.sync
    def _(sy):
        c.dma(sy, "q", knq[:], d["kn"])
        c.dma(sy, "q", qnq[:], d["qn"])

    @block.vector
    def _(v):
        ins = v.memset(smrow[:], 0.0)
        c.inc(ins, "v")

    @block.gpsimd
    def _(g):
        c.dma(g, "g", st.ones_r[:], d["onesh"][0:1, :])
        c.dma(g, "g", st.ones_c[:], d["onesh"][:, 0:1],
              allow_slow_non_contiguous=True)
        c.dma(g, "g", st.eye_bf[:], d["eyeh"])
        c.dma(g, "g", st.knrow_bf[:], d["knrow"])
        c.dma(g, "g", wT_bf[:].rearrange("p (k h) -> p k h", k=8), d["wkT"].rearrange("(k p) h -> p k h", p=128))
    g_static = c.cnt["g"]

    for s in range(CPC):
        @block.gpsimd
        def _(g):
            if s == 0:
                c.w(g, "g", g_static)
            else:
                c.w(g, "g")
                c.w(g, "v")
                c.w(g, "t")
            c.dma(g, "g", xck[:].rearrange("p (k t) -> p k t", k=8),
                  d["xfc"][s * H:(s + 1) * H, :].rearrange("(k p) t -> p k t", p=128))
        ld_x = c.cnt["g"]

        for m in range(8):
            pm = P[m % 2]

            @block.tensor
            def _(t):
                if m == 0:
                    c.w(t, "g", ld_x)
                if m >= 2:
                    c.w(t, "c")
                ins = None
                for k in range(8):
                    for nh in range(2):
                        ins = t.matmul(
                            pm[:, nh * 512:nh * 512 + 512],
                            wT_bf[:, k * H + m * 128:k * H + m * 128 + 128],
                            xck[:, k * T + nh * 512:k * T + nh * 512 + 512],
                            start=(k == 0), stop=(k == 7))
                c.inc(ins, "t")
            mm_done = c.cnt["t"]

            @block.scalar
            def _(sc):
                c.w(sc, "t", mm_done)
                ins = sc.activation(out=silk[:, m * T:(m + 1) * T], in_=pm[:],
                                    func=AF.Silu)
                c.inc(ins, "c")
        sil_done = c.cnt["c"]

        for m in range(8):
            @block.scalar
            def _(sc):
                if m == 0:
                    c.w(sc, "c", sil_done)
                ins = sc.activation(out=sqs[:], in_=silk[:, m * T:(m + 1) * T],
                                    func=AF.Square)
                c.inc(ins, "c")
            sq_done = c.cnt["c"]

            @block.tensor
            def _(t):
                c.w(t, "c", sq_done)
                ins = None
                for nh in range(2):
                    ins = t.matmul(P[2][0:1, nh * 512:nh * 512 + 512], st.ones_c[:],
                                   sqs[:, nh * 512:nh * 512 + 512],
                                   start=(m == 0), stop=(m == 7))
                if m == 7:
                    c.inc(ins, "t")
        ss_done = c.cnt["t"]

        @block.scalar
        def _(sc):
            c.w(sc, "t", ss_done)
            ins = sc.copy(out=rrow[:], in_=P[2][0:1, :])
            c.inc(ins, "c")

        @block.vector
        def _(v):
            c.w(v, "c")
            ins = v.tensor_scalar(out=rrow[:], in0=rrow[:], scalar1=1.0 / float(H),
                                  scalar2=EPS, op0=ALU.mult, op1=ALU.add)
            c.inc(ins, "v")

        @block.scalar
        def _(sc):
            c.w(sc, "v")
            ins = sc.activation(out=rrow[:], in_=rrow[:], func=AF.Sqrt)
            c.inc(ins, "c")

        @block.vector
        def _(v):
            c.w(v, "c")
            v.reciprocal(out=rrowR[:], in_=rrow[:])
            v.drain()
            ins = v.tensor_copy(out=rbf[:], in_=rrowR[:])
            c.inc(ins, "v")

        @block.tensor
        def _(t):
            c.w(t, "v")
            ins = None
            for nh in range(2):
                ins = t.matmul(P[3][:, nh * 512:nh * 512 + 512], st.ones_r[:],
                               rbf[0:1, nh * 512:nh * 512 + 512], start=True, stop=True)
            c.inc(ins, "t")

        @block.vector
        def _(v):
            c.w(v, "t")
            v.tensor_copy(out=RBs[:], in_=P[3][:])
            ins = None
            for m in range(8):
                ins = v.scalar_tensor_tensor(
                    out=kout[:, m * T:(m + 1) * T],
                    in0=silk[:, m * T:(m + 1) * T], scalar=knq[:, m:m + 1],
                    in1=RBs[:], op0=ALU.mult, op1=ALU.mult)
            c.inc(ins, "v")
        k_done = c.cnt["v"]

        @block.gpsimd
        def _(g):
            c.w(g, "v", k_done)
            c.dma(g, "g",
                  d["kd_in"][s * H:(s + 1) * H, :].rearrange("(k p) t -> p k t", p=128),
                  kout[:].rearrange("p (k t) -> p k t", k=8))

        for wi, wname in enumerate(("twT", "twA", "twE")):
            @block.gpsimd
            def _(g):
                if wi > 0:
                    c.w(g, "v")
                c.dma(g, "g", tw_bf[:].rearrange("p (k t) -> p k t", k=8),
                      d[wname].rearrange("(k p) t -> p k t", p=128))
            tw_ld = c.cnt["g"]

            @block.vector
            def _(v):
                c.w(v, "g", tw_ld)
                if wi == 0:
                    c.w(v, "v", k_done)
                ins = None
                for m in range(8):
                    ins = v.tensor_tensor(
                        out=silk[:, m * T:(m + 1) * T],
                        in0=xck[:, m * T:(m + 1) * T],
                        in1=tw_bf[:, m * T:(m + 1) * T], op=ALU.mult)
                c.inc(ins, "v")
            z_done = c.cnt["v"]

            @block.tensor
            def _(t):
                c.w(t, "v", z_done)
                ins = None
                for m in range(8):
                    for nh in range(2):
                        ins = t.matmul(P[2][0:1, nh * 512:nh * 512 + 512], st.ones_c[:],
                                       silk[:, m * T + nh * 512:m * T + nh * 512 + 512],
                                       start=(m == 0), stop=(m == 7))
                c.inc(ins, "t")
            dot_done = c.cnt["t"]

            @block.vector
            def _(v):
                c.w(v, "t", dot_done)
                v.tensor_copy(out=rrow[:], in_=P[2][0:1, :])
                v.drain()
                ins = v.reduce_sum(
                    out=rrow2[0:1, 0:16],
                    in_=rrow.rearrange("o (b i) -> o b i", b=16)[:, :, :],
                    axis=mybir.AxisListType.X)
                c.inc(ins, "v")
            seg_done = c.cnt["v"]

            if wi == 0:
                @block.scalar
                def _(sc):
                    c.w(sc, "v", seg_done)
                    ins = sc.activation(out=smrow[0:1, 0:16], in_=rrow2[0:1, 0:16],
                                        func=AF.Sigmoid)
                    c.inc(ins, "c")

                @block.vector
                def _(v):
                    c.w(v, "c")
                    ins = v.tensor_scalar(out=smrow[0:1, 0:16], in0=smrow[0:1, 0:16],
                                          scalar1=BASE_LR, scalar2=None, op0=ALU.mult)
                    c.inc(ins, "v")
            else:
                @block.vector
                def _(v):
                    v.drain()
                    ins = v.reduce_sum(out=rrow2[0:1, 16:17], in_=rrow2[0:1, 0:16],
                                       axis=mybir.AxisListType.X)
                    v.drain()
                    c.inc(ins, "v")

                @block.scalar
                def _(sc):
                    c.w(sc, "v")
                    pass

                @block.vector
                def _(v):
                    v.drain()
                    ins = v.tensor_scalar(out=rrow2[0:1, 16:17], in0=rrow2[0:1, 16:17],
                                          scalar1=1.0 / 16.0, scalar2=None, op0=ALU.mult)
                    v.drain()
                    c.inc(ins, "v")

                @block.scalar
                def _(sc):
                    c.w(sc, "v")
                    ins = sc.activation(out=rrow2[0:1, 17:18], in_=rrow2[0:1, 16:17],
                                        func=AF.Sigmoid)
                    c.inc(ins, "c")

                @block.vector
                def _(v):
                    c.w(v, "c")
                    if wi == 1:
                        ins = v.tensor_scalar(out=smrow[0:1, 16:17],
                                              in0=rrow2[0:1, 17:18],
                                              scalar1=-1.0, scalar2=1.0,
                                              op0=ALU.mult, op1=ALU.add)
                    else:
                        ins = v.tensor_copy(out=smrow[0:1, 17:18], in_=rrow2[0:1, 17:18])
                    c.inc(ins, "v")
        sm_done = c.cnt["v"]

        @block.gpsimd
        def _(g):
            c.w(g, "v", sm_done)
            c.dma(g, "g", d["sm_in"][s:s + 1, :], smrow[:])

    @block.tensor
    def _(t):
        c.w(t, "g")
        ins = t.matmul(P[3][:, 0:512], st.ones_r[:], st.knrow_bf[0:1, 0:512],
                       start=True, stop=True)
        ins = t.matmul(P[3][:, 512:1024], st.ones_r[:], st.knrow_bf[0:1, 512:1024],
                       start=True, stop=True)
        c.inc(ins, "t")

    @block.vector
    def _(v):
        c.w(v, "t")
        ins = v.tensor_copy(out=KNB[:], in_=P[3][:])
        c.inc(ins, "v")
    knb_done = c.cnt["v"]

    for proj in ("q", "k", "v"):
        wsrc = {"q": "wqT", "k": "wkT", "v": "wvT"}[proj]

        @block.gpsimd
        def _(g):
            c.w(g, "t")
            c.w(g, "v")
            c.w(g, "c")
            c.dma(g, "g", wT_bf[:].rearrange("p (k h) -> p k h", k=8), d[wsrc].rearrange("(k p) h -> p k h", p=128))
        w_ld = c.cnt["g"]

        for bb in range(2):
            @block.gpsimd
            def _(g):
                if bb == 0:
                    c.w(g, "g", w_ld)
                else:
                    c.w(g, "t")
                c.dma(g, "g", xbk[:].rearrange("p (k t) -> p k t", k=8),
                      d["xfb"][bb * H:(bb + 1) * H, :].rearrange("(k p) t -> p k t", p=128))
            xb_ld = c.cnt["g"]

            for tt in range(4):
                if proj == "q":
                    for m in range(8):
                        pm = P[m % 2]

                        @block.tensor
                        def _(t):
                            if m == 0 and tt == 0:
                                c.w(t, "g", xb_ld)
                            if m >= 2 or tt > 0:
                                c.w(t, "c")
                            ins = None
                            for k in range(8):
                                ins = t.matmul(
                                    pm[:, 0:512],
                                    wT_bf[:, k * H + m * 128:k * H + m * 128 + 128],
                                    xbk[:, k * S + tt * 512:k * S + tt * 512 + 512],
                                    start=(k == 0), stop=(k == 7))
                            c.inc(ins, "t")
                        mm_done = c.cnt["t"]

                        @block.scalar
                        def _(sc):
                            c.w(sc, "t", mm_done)
                            ins = sc.activation(out=silk[:, m * 512:(m + 1) * 512],
                                                in_=pm[:, 0:512], func=AF.Silu)
                            c.inc(ins, "c")
                    sil_done = c.cnt["c"]

                    for m in range(8):
                        @block.scalar
                        def _(sc):
                            if m == 0:
                                c.w(sc, "c", sil_done)
                            ins = sc.activation(out=sqs[:, 0:512],
                                                in_=silk[:, m * 512:(m + 1) * 512],
                                                func=AF.Square)
                            c.inc(ins, "c")
                        sq_done = c.cnt["c"]

                        @block.tensor
                        def _(t):
                            c.w(t, "c", sq_done)
                            ins = t.matmul(P[2][0:1, 0:512], st.ones_c[:], sqs[:, 0:512],
                                           start=(m == 0), stop=(m == 7))
                            if m == 7:
                                c.inc(ins, "t")

                    @block.vector
                    def _(v):
                        c.w(v, "t")
                        ins = v.tensor_scalar(out=rrow[0:1, 0:512], in0=P[2][0:1, 0:512],
                                              scalar1=1.0 / float(H), scalar2=EPS,
                                              op0=ALU.mult, op1=ALU.add)
                        c.inc(ins, "v")

                    @block.scalar
                    def _(sc):
                        c.w(sc, "v")
                        ins = sc.activation(out=rrow[0:1, 0:512], in_=rrow[0:1, 0:512],
                                            func=AF.Sqrt)
                        c.inc(ins, "c")

                    @block.vector
                    def _(v):
                        c.w(v, "c")
                        v.reciprocal(out=rrowR[0:1, 0:512], in_=rrow[0:1, 0:512])
                        v.drain()
                        ins = v.tensor_copy(out=rbf[0:1, 0:512], in_=rrowR[0:1, 0:512])
                        c.inc(ins, "v")

                    @block.tensor
                    def _(t):
                        c.w(t, "v")
                        ins = t.matmul(P[3][:, 0:512], st.ones_r[:], rbf[0:1, 0:512],
                                       start=True, stop=True)
                        c.inc(ins, "t")

                    @block.vector
                    def _(v):
                        c.w(v, "t")
                        v.tensor_copy(out=RBs[:, 0:512], in_=P[3][:, 0:512])
                        ins = None
                        for m in range(8):
                            ins = v.scalar_tensor_tensor(
                                out=kout[:, m * 512:(m + 1) * 512],
                                in0=silk[:, m * 512:(m + 1) * 512],
                                scalar=qnq[:, m:m + 1], in1=RBs[:, 0:512],
                                op0=ALU.mult, op1=ALU.mult)
                        c.inc(ins, "v")
                    q_done = c.cnt["v"]

                    @block.sync
                    def _(sy):
                        c.w(sy, "v", q_done)
                        c.dma(sy, "q",
                              d["q_dram"][:, bb * S + tt * 512:bb * S + tt * 512 + 512]
                              .rearrange("(k p) t -> p k t", p=128),
                              kout[:, 0:8 * 512].rearrange("p (k t) -> p k t", k=8))
                else:
                    for m in range(4):
                        pm = P[m % 2]

                        @block.tensor
                        def _(t):
                            if m == 0 and tt == 0:
                                c.w(t, "g", xb_ld)
                            if m >= 2 or tt > 0:
                                c.w(t, "c")
                                c.w(t, "v")
                            ins = None
                            for k in range(8):
                                for nh in range(2):
                                    ins = t.matmul(
                                        pm[:, nh * 512:nh * 512 + 512],
                                        xbk[:, k * S + tt * 512 + m * 128:
                                            k * S + tt * 512 + m * 128 + 128],
                                        wT_bf[:, k * H + nh * 512:k * H + nh * 512 + 512],
                                        start=(k == 0), stop=(k == 7))
                            c.inc(ins, "t")
                        mm_done = c.cnt["t"]

                        @block.scalar
                        def _(sc):
                            c.w(sc, "t", mm_done)
                            if proj == "k":
                                sc.activation(out=silt[:], in_=pm[:], func=AF.Silu)
                                ins = sc.activation(out=sqs[:], in_=silt[:],
                                                    func=AF.Square, accum_out=ssc[:])
                                c.inc(ins, "c")
                            else:
                                c.w(sc, "q")
                                ins = sc.activation(out=vtok[:], in_=pm[:], func=AF.Silu)
                                c.inc(ins, "c")
                        act_done = c.cnt["c"]

                        if proj == "k":
                            @block.vector
                            def _(v):
                                c.w(v, "c", act_done)
                                ins = v.tensor_scalar(out=rstc[:], in0=ssc[:],
                                                      scalar1=1.0 / float(H), scalar2=EPS,
                                                      op0=ALU.mult, op1=ALU.add)
                                c.inc(ins, "v")

                            @block.scalar
                            def _(sc):
                                c.w(sc, "v")
                                ins = sc.activation(out=rstc[:], in_=rstc[:], func=AF.Sqrt)
                                c.inc(ins, "c")

                            @block.vector
                            def _(v):
                                c.w(v, "c")
                                c.w(v, "q")
                                v.reciprocal(out=rstcR[:], in_=rstc[:])
                                v.drain()
                                ins = v.scalar_tensor_tensor(
                                    out=ktok[:], in0=silt[:], scalar=rstcR[:],
                                    in1=KNB[:], op0=ALU.mult, op1=ALU.mult)
                                c.inc(ins, "v")
                        ops_done = c.cnt["v"]

                        @block.sync
                        def _(sy):
                            row0 = (bb * S + tt * 512 + m * 128)
                            if proj == "k":
                                c.w(sy, "v", ops_done)
                                c.dma(sy, "q", d["ktok_dram"][row0:row0 + 128, :], ktok[:])
                            else:
                                c.w(sy, "c", act_done)
                                c.dma(sy, "q", d["vtok_dram"][row0:row0 + 128, :], vtok[:])

    dstream = sb("dstream", [128, 1024])
    kstream = sb("kstream", [128, 1024])

    for bb in range(2):
        for tt in range(4):
            for m in range(4):
                row0 = bb * S + tt * 512 + m * 128

                @block.sync
                def _(sy):
                    c.w(sy, "q")
                    c.w(sy, "v")
                    c.dma(sy, "q", kstream[:], d["ktok_dram"][row0:row0 + 128, :])
                    c.dma(sy, "q", dstream[:], d["vtok_dram"][row0:row0 + 128, :])
                ld2 = c.cnt["q"]

                @block.vector
                def _(v):
                    c.w(v, "q", ld2)
                    ins = v.tensor_tensor(out=dstream[:], in0=kstream[:], in1=dstream[:],
                                          op=ALU.subtract)
                    c.inc(ins, "v")
                d_done = c.cnt["v"]

                @block.sync
                def _(sy):
                    c.w(sy, "v", d_done)
                    base_tok = tt * 512 + m * 128
                    for seg in range(2):
                        ch = (base_tok + seg * 64) // CH
                        c.dma(sy, "q",
                              d["d_own"][ch * 128 + bb * CH:ch * 128 + bb * CH + CH, :],
                              dstream[seg * 64:(seg + 1) * 64, :])

    @block.gpsimd
    def _(g):
        c.w(g, "g")
        c.w(g, "q")
        c.w(g, "v")
        ins = g.collective_compute("AllGather", ALU.bypass, RG,
                                   ins=[d["kd_in"].opt()],
                                   outs=[d["kd_all"].opt()])
        c.inc(ins, "g", 1)
        ins = g.collective_compute("AllGather", ALU.bypass, RG,
                                   ins=[d["sm_in"].opt()],
                                   outs=[d["sm_all"].opt()])
        c.inc(ins, "g", 1)
    st.ag_done = c.cnt["g"]

    th_sb = sb("th_sb", [16, 32])
    esel_bf = sb("esel_bf", [16, 128])
    berow = sb("berow", [1, 64])

    @block.gpsimd
    def _(g):
        c.w(g, "g", st.ag_done)
        c.dma(g, "g", esel_bf[:], d["eselh"])
        c.dma(g, "g", th_sb[:], d["sm_all"][:, 0:16].rearrange("ch b -> b ch"),
              allow_slow_non_contiguous=True)
        c.dma(g, "g", berow[0:1, 0:32], d["sm_all"][:, 16:17].rearrange("ch o -> o ch"),
              allow_slow_non_contiguous=True)
        c.dma(g, "g", berow[0:1, 32:64], d["sm_all"][:, 17:18].rearrange("ch o -> o ch"),
              allow_slow_non_contiguous=True)
    tbl_ld = c.cnt["g"]

    @block.tensor
    def _(t):
        c.w(t, "g", tbl_ld)
        ins = t.matmul(P[0][:, 0:32], esel_bf[:], th_sb[:], start=True, stop=True)
        c.inc(ins, "t")
        ins = t.matmul(P[1][:, 0:64], st.ones_r[:], berow[0:1, :], start=True, stop=True)
        c.inc(ins, "t")

    @block.vector
    def _(v):
        c.w(v, "t")
        v.tensor_scalar(out=st.GS[:], in0=P[0][:, 0:32], scalar1=2.0 / NLOSS,
                        scalar2=None, op0=ALU.mult)
        v.tensor_copy(out=st.BT[:], in_=P[1][:, 0:32])
        v.tensor_copy(out=st.ET[:], in_=P[1][:, 32:64])
        ins = v.tensor_copy(out=st.BErow[:], in_=P[1][0:1, 0:64])
        c.inc(ins, "v")
    st.tbl_done = c.cnt["v"]


def scan(st):
    nc, c, block, d = st.nc, st.c, st.block, st.dram
    sb = st.sb
    P = st.P
    dbg = st.dbg

    kf = [sb(f"kf{i}", [128, 8 * T]) for i in range(2)]
    kt = sb("kt", [128, 8 * H], is_transpose_dest=True)
    dblk = [sb(f"dblk{i}", [128, H]) for i in range(2)]
    h_j = sb("h_j", [128, 2 * T])
    U_j = sb("U_j", [128, 2 * T])
    h_tok = sb("h_tok", [128, 8 * JS], is_transpose_dest=True)
    U_tok = sb("U_tok", [128, 8 * JS], is_transpose_dest=True)
    ypp = [sb(f"ypp{i}", [128, H]) for i in range(2)]
    yblk = sb("yblk", [128, H], F32)
    LNB = sb("LNB", [128, H])
    t1s = sb("t1s", [128, H], F32)
    esb = sb("esb", [128, H], F32)
    Qsb = sb("Qsb", [128, H], F32)
    Qw = sb("Qw", [128, H])
    dyb = sb("dyb", [128, H])
    dy_tok = sb("dy_tok", [128, 8 * H])
    df2 = sb("df2", [128, 2 * T], is_transpose_dest=True)
    dW1sb = sb("dW1sb", [128, 2 * H], F32)
    dW0sb = sb("dW0sb", [128, 8 * JS], F32)
    dPre = sb("dPre", [128, 8 * JS])
    sscol = sb("sscol", [128, 1], F32)
    rstdc = sb("rstdc", [128, 1], F32)
    rstdR = sb("rstdR", [128, 1], F32)
    ccol = sb("ccol", [128, 1], F32)
    wcol = sb("wcol", [128, 1], F32)
    bneg = sb("bneg", [128, 1], F32)
    nrmc = sb("nrmc", [128, 12], F32)
    nrmb = sb("nrmb", [128, 12])
    nrow = sb("nrow", [1, 1536], F32)
    n2row = sb("n2row", [1, 1536], F32)
    scr_bf = sb("scr_bf", [128, H])
    clipb = sb("clipb", [1, 2])
    ncol = sb("ncol", [128, 1], F32)

    w0f, w1j, s0f, s1j = st.w0f, st.w1j, st.s0f, st.s1j
    w0f_bf, w1j_bf, w1hc_bf = st.w0f_bf, st.w1j_bf, st.w1hc_bf
    ln_row, sln_row, ln_bfr = st.ln_row, st.sln_row, st.ln_bfr

    @block.sync
    def _(sy):
        c.w(sy, "g", st.ag_done)
        c.dma(sy, "q", w0f[:], d["w0fs"])
        c.dma(sy, "q", w1j[:], d["w1js"])
        c.dma(sy, "q", ln_row[:], d["lnr"])
    st_ld = c.cnt["q"]

    @block.vector
    def _(v):
        c.w(v, "q", st_ld)
        v.memset(s0f[:], 0.0)
        v.memset(s1j[:], 0.0)
        v.memset(sln_row[:], 0.0)
        v.memset(nrow[:], 0.0)
        v.tensor_copy(out=w0f_bf[:], in_=w0f[:])
        v.tensor_copy(out=w1j_bf[:], in_=w1j[:])
        ins = v.tensor_copy(out=ln_bfr[:], in_=ln_row[:])
        c.inc(ins, "v")
    init_done = c.cnt["v"]

    def emit_w1hc(wait_v):
        @block.sync
        def _(sy):
            c.w(sy, "v", wait_v)
            c.dma(sy, "q", d["w1scr"][:, :].rearrange("(k p) h -> p k h", p=128),
                  w1j_bf[:].rearrange("p (k h) -> p k h", k=2))
            c.w(sy, "q")
            for m in range(8):
                c.tr(sy, "q", w1hc_bf[:, m * JS:(m + 1) * JS],
                     d["w1scr"][:, m * 128:(m + 1) * 128])
        return c.cnt["q"]

    w1hc_ready = emit_w1hc(init_done)

    def kd_base(ch):
        return (ch // CPC) * CPC * H + (ch % CPC) * H

    def emit_kf(ch, buf, wait_t):
        kb = kd_base(ch)

        @block.sync
        def _(sy):
            c.w(sy, "g", st.ag_done)
            c.w(sy, "t", wait_t)
            c.dma(sy, "q", kf[buf][:].rearrange("p (k t) -> p k t", k=8),
                  d["kd_all"][kb:kb + H, :].rearrange("(k p) t -> p k t", p=128))
        return c.cnt["q"]

    def emit_kt(ch, wait_t):
        kb = kd_base(ch)

        @block.sync
        def _(sy):
            c.w(sy, "g", st.ag_done)
            c.w(sy, "t", wait_t)
            for m in range(8):
                c.tr(sy, "q", kt[:, m * H:(m + 1) * H],
                     d["kd_all"][kb:kb + H, m * 128:(m + 1) * 128])
        return c.cnt["q"]

    def emit_dblk(ch, buf, wait_v):
        @block.sync
        def _(sy):
            c.w(sy, "v", wait_v)
            c.dma(sy, "q", dblk[buf][:], d["d_own"][ch * 128:(ch + 1) * 128, :])
        return c.cnt["q"]

    kf_rdy = emit_kf(0, 0, 0)
    kt_rdy = emit_kt(0, 0)
    db_rdy = emit_dblk(0, 0, 0)
    m1_done_prev = 0
    m5_done_prev = 0
    ew_done_prev = 0
    upd_done = init_done

    for ch in range(NCH):
        buf = ch % 2
        nbuf = 1 - buf

        if ch + 1 < NCH:
            kf_rdy_next = emit_kf(ch + 1, nbuf, m1_done_prev)
            db_rdy_next = emit_dblk(ch + 1, nbuf, ew_done_prev)

        @block.tensor
        def _(t):
            c.w(t, "q", kf_rdy)
            c.w(t, "v", upd_done)
            for jm in range(2):
                ins = None
                for nh in range(2):
                    for k in range(8):
                        ins = t.matmul(
                            P[jm][:, nh * 512:nh * 512 + 512],
                            w0f_bf[:, k * JS + jm * 128:k * JS + jm * 128 + 128],
                            kf[buf][:, k * T + nh * 512:k * T + nh * 512 + 512],
                            start=(k == 0), stop=(k == 7))
                c.inc(ins, "t")
        m1_done = c.cnt["t"]
        m1_done_prev = m1_done

        @block.scalar
        def _(sc):
            c.w(sc, "t", m1_done)
            for jm in range(2):
                sc.activation(out=h_j[:, jm * T:(jm + 1) * T], in_=P[jm][:], func=AF.Silu)
                ins = sc.activation(out=U_j[:, jm * T:(jm + 1) * T], in_=P[jm][:],
                                    func=AF.Derivative_silu)
            c.inc(ins, "c")
        hu_done = c.cnt["c"]

        @block.scalar
        def _(sc):
            c.w(sc, "c", hu_done)
            c.dma(sc, "s", d["hU_dram"][0:JS, :].rearrange("(k p) t -> p k t", p=128),
                  h_j[:].rearrange("p (k t) -> p k t", k=2))
            c.dma(sc, "s", d["hU_dram"][JS:2 * JS, :].rearrange("(k p) t -> p k t", p=128),
                  U_j[:].rearrange("p (k t) -> p k t", k=2))
            c.w(sc, "s")
            for m in range(8):
                c.tr(sc, "s", h_tok[:, m * JS:(m + 1) * JS],
                     d["hU_dram"][0:JS, m * 128:(m + 1) * 128])
                c.tr(sc, "s", U_tok[:, m * JS:(m + 1) * JS],
                     d["hU_dram"][JS:2 * JS, m * 128:(m + 1) * 128])
        hU_tok_rdy = c.cnt["s"]

        gv_y = []
        for tm in range(8):
            pm = P[2 + (tm % 2)]

            @block.tensor
            def _(t):
                if tm == 0:
                    c.w(t, "c", hu_done)
                if tm >= 2:
                    c.w(t, "v")
                ins = None
                for nh in range(2):
                    for k in range(2):
                        ins = t.matmul(
                            pm[:, nh * 512:nh * 512 + 512],
                            h_j[:, k * T + tm * 128:k * T + tm * 128 + 128],
                            w1j_bf[:, k * H + nh * 512:k * H + nh * 512 + 512],
                            start=(k == 0), stop=(k == 1))
                c.inc(ins, "t")
            y_mm = c.cnt["t"]

            @block.vector
            def _(v):
                c.w(v, "t", y_mm)
                if tm >= 2:
                    c.w(v, "g")
                ins = v.tensor_copy(out=ypp[tm % 2][:], in_=pm[:])
                c.inc(ins, "v")
            y_ev = c.cnt["v"]

            @block.gpsimd
            def _(g):
                c.w(g, "v", y_ev)
                if ch > 0 and tm == 0:
                    c.w(g, "g")
                c.dma(g, "g", d["yrs_in"][tm * 128:(tm + 1) * 128, :], ypp[tm % 2][:])
            gv_y.append(c.cnt["g"])

        @block.gpsimd
        def _(g):
            c.w(g, "g", gv_y[-1])
            ins = g.collective_compute("ReduceScatter", ALU.add, RG,
                                       ins=[d["yrs_in"].opt()],
                                       outs=[d["yrs_out"].opt()])
            c.inc(ins, "g", 1)
            c.w(g, "g")
            c.dma(g, "g", yblk[:], d["yrs_out"][:, :])
        yblk_rdy = c.cnt["g"]

        @block.tensor
        def _(t):
            c.w(t, "v", upd_done)
            ins = None
            for nh in range(2):
                ins = t.matmul(P[0][:, nh * 512:nh * 512 + 512], st.ones_r[:],
                               ln_bfr[0:1, nh * 512:nh * 512 + 512],
                               start=True, stop=True)
            c.inc(ins, "t")
        lnb_mm = c.cnt["t"]

        @block.vector
        def _(v):
            c.w(v, "t", lnb_mm)
            ins = v.tensor_copy(out=LNB[:], in_=P[0][:])
            c.inc(ins, "v")
        lnb_done = c.cnt["v"]

        @block.scalar
        def _(sc):
            c.w(sc, "g", yblk_rdy)
            ins = sc.activation(out=esb[:], in_=yblk[:], func=AF.Square,
                                accum_out=sscol[:])
            c.inc(ins, "c")
        ssq_done = c.cnt["c"]

        @block.vector
        def _(v):
            c.w(v, "c", ssq_done)
            ins = v.tensor_scalar(out=rstdc[:], in0=sscol[:], scalar1=1.0 / float(H),
                                  scalar2=EPS, op0=ALU.mult, op1=ALU.add)
            c.inc(ins, "v")

        @block.scalar
        def _(sc):
            c.w(sc, "v")
            ins = sc.activation(out=rstdc[:], in_=rstdc[:], func=AF.Sqrt)
            c.inc(ins, "c")
        rstd_sc = c.cnt["c"]

        @block.vector
        def _(v):
            c.w(v, "c", rstd_sc)
            c.w(v, "v", lnb_done)
            c.w(v, "q", db_rdy)
            v.reciprocal(out=rstdR[:], in_=rstdc[:])
            v.drain()
            v.scalar_tensor_tensor(out=t1s[:], in0=yblk[:], scalar=rstdR[:],
                                   in1=LNB[:], op0=ALU.mult, op1=ALU.mult)
            v.tensor_tensor(out=esb[:], in0=t1s[:], in1=dblk[buf][:], op=ALU.add)
            v.tensor_tensor(out=Qsb[:], in0=yblk[:], in1=esb[:], op=ALU.mult)
            ins = v.tensor_tensor(out=t1s[:], in0=Qsb[:], in1=LNB[:], op=ALU.mult)
            c.inc(ins, "v")
        r4_done = c.cnt["v"]

        @block.scalar
        def _(sc):
            c.w(sc, "v", r4_done)
            ins = sc.activation(out=dW1sb[:, 0:H], in_=t1s[:], func=AF.Identity,
                                accum_out=ccol[:])
            c.inc(ins, "c")
        ccol_done = c.cnt["c"]

        @block.vector
        def _(v):
            c.w(v, "c", ccol_done)
            v.tensor_tensor(out=wcol[:], in0=st.GS[:, ch:ch + 1], in1=rstdR[:],
                            op=ALU.mult)
            v.drain()
            ins = v.tensor_scalar(out=Qw[:], in0=Qsb[:], scalar1=wcol[:], scalar2=None,
                                  op0=ALU.mult)
            c.inc(ins, "v")
            c.vsw(v, v.tensor_tensor(out=bneg[:], in0=rstdR[:], in1=rstdR[:], op=ALU.mult))
            c.vsw(v, v.tensor_tensor(out=bneg[:], in0=bneg[:], in1=wcol[:], op=ALU.mult))
            c.vsw(v, v.tensor_tensor(out=bneg[:], in0=bneg[:], in1=ccol[:], op=ALU.mult))
            v.tensor_scalar(out=bneg[:], in0=bneg[:], scalar1=-1.0 / float(H),
                            scalar2=None, op0=ALU.mult)
            v.drain()
            v.tensor_tensor(out=t1s[:], in0=esb[:], in1=LNB[:], op=ALU.mult)
            v.tensor_scalar(out=t1s[:], in0=t1s[:], scalar1=wcol[:], scalar2=None,
                            op0=ALU.mult)
            v.drain()
            ins = v.scalar_tensor_tensor(out=dyb[:], in0=yblk[:], scalar=bneg[:],
                                         in1=t1s[:], op0=ALU.mult, op1=ALU.add)
            c.inc(ins, "v")
        ew_done = c.cnt["v"]
        ew_done_prev = ew_done
        qw_v = ew_done

        @block.tensor
        def _(t):
            c.w(t, "v", qw_v)
            ins = None
            for nh in range(2):
                ins = t.matmul(P[1][0:1, nh * 512:nh * 512 + 512], st.ones_c[:],
                               Qw[:, nh * 512:nh * 512 + 512], start=True, stop=True)
            c.inc(ins, "t")
        gln_mm = c.cnt["t"]

        @block.vector
        def _(v):
            c.w(v, "t", gln_mm)
            ins = v.tensor_copy(out=nrow[0:1, 0:1024], in_=P[1][0:1, :])
            c.inc(ins, "v")
        gln_ev = c.cnt["v"]

        @block.gpsimd
        def _(g):
            c.w(g, "v", ew_done)
            c.dma(g, "g", d["dy_in"][:, :], dyb[:])
            c.w(g, "g")
            ins = g.collective_compute("AllGather", ALU.bypass, RG,
                                       ins=[d["dy_in"].opt()],
                                       outs=[d["dy_all"].opt()])
            c.inc(ins, "g", 1)
        ag_dy = c.cnt["g"]

        @block.sync
        def _(sy):
            c.w(sy, "g", ag_dy)
            c.dma(sy, "q", dy_tok[:].rearrange("p (k h) -> p k h", k=8),
                  d["dy_all"][:, :].rearrange("(k p) h -> p k h", p=128))
        dytok_rdy = c.cnt["q"]

        @block.tensor
        def _(t):
            c.w(t, "q", dytok_rdy)
            c.w(t, "s", hU_tok_rdy)
            c.w(t, "v", gln_ev)
            for jm in range(2):
                ins = None
                for nh in range(2):
                    for k in range(8):
                        ins = t.matmul(
                            P[jm][:, nh * 512:nh * 512 + 512],
                            h_tok[:, k * JS + jm * 128:k * JS + jm * 128 + 128],
                            dy_tok[:, k * H + nh * 512:k * H + nh * 512 + 512],
                            start=(k == 0), stop=(k == 7))
                c.inc(ins, "t")
        m3_done = c.cnt["t"]

        @block.scalar
        def _(sc):
            c.w(sc, "t", m3_done)
            for jm in range(2):
                ins = sc.activation(out=dW0sb[:, 0:H], in_=P[jm][:], func=AF.Square,
                                    accum_out=nrmc[:, jm:jm + 1])
            c.inc(ins, "c")
        n1_done = c.cnt["c"]

        @block.vector
        def _(v):
            c.w(v, "c", n1_done)
            for jm in range(2):
                ins = v.tensor_copy(out=dW1sb[:, jm * H:(jm + 1) * H], in_=P[jm][:])
            c.inc(ins, "v")
        dw1_ev = c.cnt["v"]

        df_rdy = []
        for khc in range(0, 8, 2):
            @block.sync
            def _(sy):
                c.w(sy, "g", ag_dy)
                if khc >= 2:
                    c.w(sy, "t")
                for kk in range(2):
                    c.tr(sy, "q", df2[:, kk * T:(kk + 1) * T],
                         d["dy_all"][:, (khc + kk) * 128:(khc + kk) * 128 + 128])
            df_rdy.append(c.cnt["q"])

            @block.tensor
            def _(t):
                c.w(t, "q", df_rdy[-1])
                if khc == 0:
                    c.w(t, "q", w1hc_ready)
                    c.w(t, "v", dw1_ev)
                ins = None
                for kk in range(2):
                    kh = khc + kk
                    for tm in range(8):
                        pq = P[2 + tm // 4]
                        ins = t.matmul(
                            pq[:, (tm % 4) * 256:(tm % 4) * 256 + 256],
                            df2[:, kk * T + tm * 128:kk * T + tm * 128 + 128],
                            w1hc_bf[:, kh * JS:(kh + 1) * JS],
                            start=(kh == 0), stop=(kh == 7))
                c.inc(ins, "t")
        m4_done = c.cnt["t"]

        @block.vector
        def _(v):
            c.w(v, "t", m4_done)
            c.w(v, "s", hU_tok_rdy)
            ins = None
            for tm in range(8):
                pq = P[2 + tm // 4]
                ins = v.tensor_tensor(out=dPre[:, tm * JS:(tm + 1) * JS],
                                      in0=pq[:, (tm % 4) * 256:(tm % 4) * 256 + 256],
                                      in1=U_tok[:, tm * JS:(tm + 1) * JS], op=ALU.mult)
            c.inc(ins, "v")
        dpre_done = c.cnt["v"]

        @block.tensor
        def _(t):
            c.w(t, "v", dpre_done)
            c.w(t, "q", kt_rdy)
            for hm in range(8):
                pq = P[hm // 4]
                ins = None
                for k in range(8):
                    ins = t.matmul(
                        pq[:, (hm % 4) * 256:(hm % 4) * 256 + 256],
                        kt[:, k * H + hm * 128:k * H + hm * 128 + 128],
                        dPre[:, k * JS:(k + 1) * JS],
                        start=(k == 0), stop=(k == 7))
                c.inc(ins, "t")
        m5_done = c.cnt["t"]
        m5_done_prev = m5_done

        @block.scalar
        def _(sc):
            c.w(sc, "t", m5_done)
            for hm in range(8):
                pq = P[hm // 4]
                ins = sc.activation(out=esb[:, 0:256],
                                    in_=pq[:, (hm % 4) * 256:(hm % 4) * 256 + 256],
                                    func=AF.Square, accum_out=nrmc[:, 2 + hm:3 + hm])
            c.inc(ins, "c")
        n0_done = c.cnt["c"]

        @block.vector
        def _(v):
            c.w(v, "c", n0_done)
            for hm in range(8):
                pq = P[hm // 4]
                ins = v.tensor_copy(out=dW0sb[:, hm * JS:(hm + 1) * JS],
                                    in_=pq[:, (hm % 4) * 256:(hm % 4) * 256 + 256])
            ins = v.tensor_copy(out=nrmb[:, 0:10], in_=nrmc[:, 0:10])
            c.inc(ins, "v")
        dw0_ev = c.cnt["v"]

        if ch + 1 < NCH:
            kt_rdy_next = emit_kt(ch + 1, m5_done)

        @block.tensor
        def _(t):
            c.w(t, "v", dw0_ev)
            ins = t.matmul(P[2][0:1, 0:10], st.ones_c[:], nrmb[:, 0:10],
                           start=True, stop=True)
            c.inc(ins, "t")
        nsum_mm = c.cnt["t"]

        @block.vector
        def _(v):
            c.w(v, "t", nsum_mm)
            ins = v.reduce_sum(out=nrow[0:1, 1024:1025], in_=P[2][0:1, 0:10],
                               axis=mybir.AxisListType.X)
            c.inc(ins, "v")
        nrow_done = c.cnt["v"]

        @block.gpsimd
        def _(g):
            c.w(g, "v", nrow_done)
            c.dma(g, "g", d["nr_in"][:, :], nrow[:])
            c.w(g, "g")
            ins = g.collective_compute("AllReduce", ALU.add, RG,
                                       ins=[d["nr_in"].opt()],
                                       outs=[d["nr_out"].opt()])
            c.inc(ins, "g", 1)
            c.w(g, "g")
            c.dma(g, "g", n2row[:], d["nr_out"][:, :])
        ar2_done = c.cnt["g"]

        @block.gpsimd
        def _(g):
            r0w = ch * (H // NCH)
            r0d = ch * (KDE // NCH)
            c.dma(g, "g", d["wgB"][r0w:r0w + H // NCH, :], d["wgS"][r0w:r0w + H // NCH, :])
            c.dma(g, "g", d["wuB"][r0w:r0w + H // NCH, :], d["wuS"][r0w:r0w + H // NCH, :])
            c.dma(g, "g", d["wdB"][r0d:r0d + KDE // NCH, :], d["wdS"][r0d:r0d + KDE // NCH, :])

        @block.scalar
        def _(sc):
            c.w(sc, "g", ar2_done)
            ins = sc.activation(out=esb[0:1, 0:1024], in_=n2row[0:1, 0:1024],
                                func=AF.Square, accum_out=n2row[0:1, 1025:1026])
            c.inc(ins, "c")
        glnsq = c.cnt["c"]

        @block.vector
        def _(v):
            c.w(v, "c", glnsq)
            ins = v.tensor_tensor(out=n2row[0:1, 1025:1026], in0=n2row[0:1, 1025:1026],
                                  in1=n2row[0:1, 1024:1025], op=ALU.add)
            c.inc(ins, "v")

        @block.scalar
        def _(sc):
            c.w(sc, "v")
            ins = sc.activation(out=n2row[0:1, 1025:1026], in_=n2row[0:1, 1025:1026],
                                func=AF.Sqrt)
            c.inc(ins, "c")
        tot_done = c.cnt["c"]

        @block.vector
        def _(v):
            c.w(v, "c", tot_done)
            v.tensor_scalar(out=n2row[0:1, 1025:1026], in0=n2row[0:1, 1025:1026],
                            scalar1=1e-6, scalar2=None, op0=ALU.add)
            ins = v.reciprocal(out=n2row[0:1, 1024:1025], in_=n2row[0:1, 1025:1026])
            c.vsw(v, ins)
            c.vsw(v, v.tensor_scalar(out=n2row[0:1, 1024:1025], in0=n2row[0:1, 1024:1025],
                                     scalar1=1.0, scalar2=None, op0=ALU.min))
            v.tensor_copy(out=clipb[0:1, 0:1], in_=n2row[0:1, 1024:1025])
            ins = v.tensor_scalar(out=clipb[0:1, 1:2], in0=n2row[0:1, 1024:1025],
                                  scalar1=-1.0, scalar2=None, op0=ALU.mult)
            c.inc(ins, "v")
        clip_done = c.cnt["v"]

        @block.tensor
        def _(t):
            c.w(t, "v", clip_done)
            ins = t.matmul(P[2][:, 0:2], st.ones_r[:], clipb[0:1, 0:2],
                           start=True, stop=True)
            c.inc(ins, "t")
        clip_mm = c.cnt["t"]

        @block.vector
        def _(v):
            c.w(v, "t", clip_mm)
            v.tensor_copy(out=ncol[:], in_=P[2][:, 1:2])
            v.drain()

            bcol = st.BT[:, ch:ch + 1]
            ecol = st.ET[:, ch:ch + 1]
            v.tensor_scalar(out=s0f[:], in0=s0f[:], scalar1=ecol, scalar2=None,
                            op0=ALU.mult)
            v.scalar_tensor_tensor(out=s0f[:], in0=dW0sb[:], scalar=ncol[:],
                                   in1=s0f[:], op0=ALU.mult, op1=ALU.add)
            v.tensor_scalar(out=s1j[:], in0=s1j[:], scalar1=ecol, scalar2=None,
                            op0=ALU.mult)
            v.scalar_tensor_tensor(out=s1j[:], in0=dW1sb[:], scalar=ncol[:],
                                   in1=s1j[:], op0=ALU.mult, op1=ALU.add)
            v.tensor_scalar(out=w0f[:], in0=w0f[:], scalar1=bcol, scalar2=None,
                            op0=ALU.mult)
            v.tensor_tensor(out=w0f[:], in0=w0f[:], in1=s0f[:], op=ALU.add)
            v.tensor_scalar(out=w1j[:], in0=w1j[:], scalar1=bcol, scalar2=None,
                            op0=ALU.mult)
            v.tensor_tensor(out=w1j[:], in0=w1j[:], in1=s1j[:], op=ALU.add)
            brow = st.BErow[0:1, ch:ch + 1]
            erow = st.BErow[0:1, 32 + ch:33 + ch]
            ins = v.tensor_scalar(out=sln_row[:], in0=sln_row[:], scalar1=erow,
                                  scalar2=None, op0=ALU.mult)
            c.vsw(v, ins)
            ins = v.scalar_tensor_tensor(out=sln_row[:], in0=n2row[0:1, 0:1024],
                                         scalar=n2row[0:1, 1024:1025], in1=sln_row[:],
                                         op0=ALU.mult, op1=ALU.subtract)
            c.vsw(v, ins)
            ins = v.tensor_scalar(out=sln_row[:], in0=sln_row[:], scalar1=-1.0,
                                  scalar2=None, op0=ALU.mult)
            c.vsw(v, ins)
            ins = v.tensor_scalar(out=ln_row[:], in0=ln_row[:], scalar1=brow,
                                  scalar2=None, op0=ALU.mult)
            c.vsw(v, ins)
            ins = v.tensor_tensor(out=ln_row[:], in0=ln_row[:], in1=sln_row[:], op=ALU.add)
            c.vsw(v, ins)
            v.tensor_copy(out=ln_bfr[:], in_=ln_row[:])
            v.tensor_copy(out=w0f_bf[:], in_=w0f[:])
            ins = v.tensor_copy(out=w1j_bf[:], in_=w1j[:])
            c.inc(ins, "v")
        upd_done = c.cnt["v"]

        if ch + 1 < NCH:
            w1hc_ready = emit_w1hc(upd_done)
            kf_rdy = kf_rdy_next
            kt_rdy = kt_rdy_next
            db_rdy = db_rdy_next

        if dbg.get("scan_ch") == ch:
            @block.gpsimd
            def _(g):
                c.w(g, "v", upd_done)
                c.dma(g, "g", st.dbg_ap["w0f"], w0f[:])
                c.dma(g, "g", st.dbg_ap["w1j"], w1j[:])
                c.dma(g, "g", st.dbg_ap["lnrow"], ln_row[:])
                c.dma(g, "g", st.dbg_ap["yblk"], yblk[:])
                c.dma(g, "g", st.dbg_ap["dyall"], d["dy_all"][:, :])
                c.dma(g, "g", st.dbg_ap["n2row"], n2row[:])
                c.dma(g, "g", st.dbg_ap["GS"], st.GS[:])
                c.dma(g, "g", st.dbg_ap["Qw"], Qw[:])
                c.dma(g, "g", st.dbg_ap["ccol"], ccol[:])
                c.dma(g, "g", st.dbg_ap["esb"], esb[:])
                c.dma(g, "g", st.dbg_ap["t1s"], t1s[:])
                c.dma(g, "g", st.dbg_ap["wcol"], wcol[:])
                c.dma(g, "g", st.dbg_ap["nrow"], nrow[:])
                c.dma(g, "g", st.dbg_ap["nri"], d["nr_in"][:, :])
                c.dma(g, "g", st.dbg_ap["nro"], d["nr_out"][:, :])

    st.upd_done = upd_done
    st.wcast_done = c.cnt["g"]




def final_attn(st):
    nc, c, block, d = st.nc, st.c, st.block, st.dram
    sb = st.sb
    P = st.P

    @block.gpsimd
    def _(g):
        c.w(g, "v", st.upd_done)
        c.dma(g, "g", d["w0_in"][:, :].rearrange("(k p) j -> p k j", p=128),
              st.w0f_bf[:].rearrange("p (k j) -> p k j", k=8))
        c.dma(g, "g", d["w1_in"][:, :].rearrange("(k p) h -> p k h", p=128),
              st.w1j_bf[:].rearrange("p (k h) -> p k h", k=2))
        c.w(g, "g")
        ins = g.collective_compute("AllGather", ALU.bypass, RG,
                                   ins=[d["w0_in"].opt()], outs=[d["w0_all"].opt()])
        c.inc(ins, "g", 1)
        ins = g.collective_compute("AllGather", ALU.bypass, RG,
                                   ins=[d["w1_in"].opt()], outs=[d["w1_all"].opt()])
        c.inc(ins, "g", 1)
    mem_ag = c.cnt["g"]

    w0a = sb("w0a", [128, 8 * IN])
    w1a = sb("w1a", [128, 16 * H])
    qt = sb("qt", [128, 8 * 512])
    h2 = sb("h2", [128, 16 * 512])
    y2sb = sb("y2sb", [128, 8 * 512])
    zb = sb("zb", [128, 512])
    sqb = sb("sqb", [128, 512])
    srow = sb("srow", [1, 1024], F32)
    srow_bf = sb("srow_bf", [1, 1024])
    RB2 = sb("RB2", [128, 512])
    MB2 = sb("MB2", [128, 512])
    lnq = sb("lnq", [128, 8], F32)
    lnbf32 = sb("lnbf32", [128, 1024], F32)
    xstr = sb("xstr", [128, 512], F32)
    vrow = sb("vrow", [1, 512], F32)
    zfp = sb("zfp", [128, 8 * 512], F32)
    x1o = sb("x1o", [128, 512], F32)
    x1b2 = sb("x1b2", [128, 512])
    x1l2 = sb("x1l2", [128, 512])

    @block.sync
    def _(sy):
        c.w(sy, "g", mem_ag)
        for rr in range(8):
            c.dma(sy, "q",
                  w0a[:].rearrange("p (k r j) -> p k r j", k=8, r=8)[:, :, rr, :],
                  d["w0_all"][rr * H:(rr + 1) * H, :]
                  .rearrange("(k p) j -> p k j", p=128))
        c.dma(sy, "q", w1a[:].rearrange("p (k h) -> p k h", k=16),
              d["w1_all"].rearrange("(k p) h -> p k h", p=128))
    mem_ld = c.cnt["q"]

    @block.tensor
    def _(t):
        c.w(t, "v", st.upd_done)
        ins = None
        for nh in range(2):
            ins = t.matmul(P[0][:, nh * 512:nh * 512 + 512], st.ones_r[:],
                           st.ln_bfr[0:1, nh * 512:nh * 512 + 512], start=True, stop=True)
        c.inc(ins, "t")

    @block.vector
    def _(v):
        c.w(v, "t")
        ins = v.tensor_copy(out=lnbf32[:], in_=P[0][:])
        c.inc(ins, "v")
    lnb32_done = c.cnt["v"]

    @block.tensor
    def _(t):
        c.w(t, "v", lnb32_done)
        ins = None
        for m in range(8):
            ins = t.transpose(P[1][:, m * 128:(m + 1) * 128],
                              lnbf32[:, m * 128:(m + 1) * 128], st.eye_f32[:])
        c.inc(ins, "t")
    lnq_mm = c.cnt["t"]

    @block.vector
    def _(v):
        c.w(v, "t", lnq_mm)
        ins = None
        for m in range(8):
            ins = v.tensor_copy(out=lnq[:, m:m + 1], in_=P[1][:, m * 128:m * 128 + 1])
        c.inc(ins, "v")
    lnq_done = c.cnt["v"]

    for tt in range(8):
        bb = tt // 4

        @block.sync
        def _(sy):
            c.w(sy, "q", mem_ld)
            if tt > 0:
                c.w(sy, "v")
            c.dma(sy, "q", qt[:].rearrange("p (k t) -> p k t", k=8),
                  d["q_dram"][:, tt * 512:(tt + 1) * 512]
                  .rearrange("(k p) t -> p k t", p=128))
        q_ld = c.cnt["q"]

        for im in range(16):
            qtr = [P[0], P[0], P[1], P[1]][im % 4]
            half = (im % 2) * 512

            @block.tensor
            def _(t):
                if im == 0:
                    c.w(t, "q", q_ld)
                    c.w(t, "v", lnq_done)
                if im >= 4:
                    c.w(t, "c")
                ins = None
                for k in range(8):
                    ins = t.matmul(
                        qtr[:, half:half + 512],
                        w0a[:, k * IN + im * 128:k * IN + im * 128 + 128],
                        qt[:, k * 512:(k + 1) * 512],
                        start=(k == 0), stop=(k == 7))
                c.inc(ins, "t")
            mm_done = c.cnt["t"]

            @block.scalar
            def _(sc):
                c.w(sc, "t", mm_done)
                ins = sc.activation(out=h2[:, im * 512:(im + 1) * 512],
                                    in_=qtr[:, half:half + 512], func=AF.Silu)
                c.inc(ins, "c")
        h2_done = c.cnt["c"]

        for hm in range(8):
            half = (hm % 2) * 512

            @block.tensor
            def _(t):
                if hm == 0:
                    c.w(t, "c", h2_done)
                if hm >= 2:
                    c.w(t, "v")
                    c.w(t, "c")
                ins = None
                for k in range(16):
                    ins = t.matmul(
                        P[2][:, half:half + 512],
                        w1a[:, k * H + hm * 128:k * H + hm * 128 + 128],
                        h2[:, k * 512:(k + 1) * 512],
                        start=(k == 0), stop=(k == 15))
                c.inc(ins, "t")
            mm_done = c.cnt["t"]

            @block.scalar
            def _(sc):
                c.w(sc, "t", mm_done)
                ins = sc.activation(out=sqb[:], in_=P[2][:, half:half + 512],
                                    func=AF.Square)
                c.inc(ins, "c")
            sq_done = c.cnt["c"]

            @block.vector
            def _(v):
                c.w(v, "t", mm_done)
                ins = v.tensor_copy(out=y2sb[:, hm * 512:(hm + 1) * 512],
                                    in_=P[2][:, half:half + 512])
                c.inc(ins, "v")

            @block.tensor
            def _(t):
                c.w(t, "c", sq_done)
                ins = t.matmul(P[3][0:1, 0:512], st.ones_c[:], sqb[:],
                               start=(hm == 0), stop=(hm == 7))
                if hm == 7:
                    c.inc(ins, "t")
        ss_mm = c.cnt["t"]

        @block.vector
        def _(v):
            c.w(v, "t", ss_mm)
            ins = v.tensor_scalar(out=srow[0:1, 0:512], in0=P[3][0:1, 0:512],
                                  scalar1=1.0 / float(H), scalar2=EPS,
                                  op0=ALU.mult, op1=ALU.add)
            c.inc(ins, "v")

        @block.scalar
        def _(sc):
            c.w(sc, "v")
            ins = sc.activation(out=srow[0:1, 0:512], in_=srow[0:1, 0:512],
                                func=AF.Sqrt)
            c.inc(ins, "c")

        @block.vector
        def _(v):
            c.w(v, "c")
            v.reciprocal(out=srow[0:1, 512:1024], in_=srow[0:1, 0:512])
            v.drain()
            ins = v.tensor_copy(out=srow_bf[0:1, 0:512], in_=srow[0:1, 512:1024])
            c.inc(ins, "v")

        @block.tensor
        def _(t):
            c.w(t, "v")
            ins = t.matmul(P[3][:, 512:1024], st.ones_r[:], srow_bf[0:1, 0:512],
                           start=True, stop=True)
            c.inc(ins, "t")

        @block.vector
        def _(v):
            c.w(v, "t")
            ins = v.tensor_copy(out=RB2[:], in_=P[3][:, 512:1024])
            c.inc(ins, "v")
        rb2_done = c.cnt["v"]

        for hm in range(8):
            @block.sync
            def _(sy):
                c.w(sy, "v", rb2_done if hm == 0 else c.cnt["v"])
                c.dma(sy, "q", xstr[:],
                      d["xfb"][bb * H + hm * 128:bb * H + hm * 128 + 128,
                               (tt % 4) * 512:(tt % 4) * 512 + 512])
            x_ld = c.cnt["q"]

            @block.vector
            def _(v):
                c.w(v, "q", x_ld)
                sl = slice(hm * 512, (hm + 1) * 512)
                v.scalar_tensor_tensor(out=zfp[:, sl], in0=y2sb[:, sl],
                                       scalar=lnq[:, hm:hm + 1], in1=RB2[:],
                                       op0=ALU.mult, op1=ALU.mult)
                v.tensor_tensor(out=zfp[:, sl], in0=zfp[:, sl], in1=qt[:, sl],
                                op=ALU.add)
                ins = v.tensor_tensor(out=zfp[:, sl], in0=zfp[:, sl], in1=xstr[:],
                                      op=ALU.add)
                c.inc(ins, "v")
            z_done = c.cnt["v"]

            @block.scalar
            def _(sc):
                c.w(sc, "v", z_done)
                sl = slice(hm * 512, (hm + 1) * 512)
                sc.activation(out=zb[:], in_=zfp[:, sl], func=AF.Identity)
                ins = sc.activation(out=sqb[:], in_=zfp[:, sl], func=AF.Square)
                c.inc(ins, "c")
            zb_done = c.cnt["c"]

            @block.tensor
            def _(t):
                c.w(t, "c", zb_done)
                t.matmul(P[0][0:1, 0:512], st.ones_c[:], zb[:],
                         start=(hm == 0), stop=(hm == 7))
                ins = t.matmul(P[0][0:1, 512:1024], st.ones_c[:], sqb[:],
                               start=(hm == 0), stop=(hm == 7))
                if hm == 7:
                    c.inc(ins, "t")
        st_mm = c.cnt["t"]

        @block.vector
        def _(v):
            c.w(v, "t", st_mm)
            v.tensor_scalar(out=srow[0:1, 0:512], in0=P[0][0:1, 0:512],
                            scalar1=1.0 / float(H), scalar2=None, op0=ALU.mult)
            ins = v.tensor_scalar(out=srow[0:1, 512:1024], in0=P[0][0:1, 512:1024],
                                  scalar1=1.0 / float(H), scalar2=None, op0=ALU.mult)
            c.vsw(v, ins)
            ins = v.tensor_tensor(out=vrow[0:1, 0:512], in0=srow[0:1, 0:512],
                                  in1=srow[0:1, 0:512], op=ALU.mult)
            c.vsw(v, ins)
            ins = v.tensor_tensor(out=vrow[0:1, 0:512], in0=srow[0:1, 512:1024],
                                  in1=vrow[0:1, 0:512], op=ALU.subtract)
            c.inc(ins, "v")
        var_done = c.cnt["v"]

        @block.vector
        def _(v):
            c.w(v, "v", var_done)
            ins = v.tensor_scalar(out=vrow[0:1, 0:512], in0=vrow[0:1, 0:512],
                                  scalar1=LNEPS, scalar2=None, op0=ALU.add)
            c.inc(ins, "v")

        @block.scalar
        def _(sc):
            c.w(sc, "v")
            ins = sc.activation(out=vrow[0:1, 0:512], in_=vrow[0:1, 0:512],
                                func=AF.Sqrt)
            c.inc(ins, "c")

        @block.vector
        def _(v):
            c.w(v, "c")
            v.reciprocal(out=srow[0:1, 512:1024], in_=vrow[0:1, 0:512])
            v.drain()
            v.tensor_copy(out=srow_bf[0:1, 0:512], in_=srow[0:1, 0:512])
            ins = v.tensor_copy(out=srow_bf[0:1, 512:1024], in_=srow[0:1, 512:1024])
            c.inc(ins, "v")
        rows_done = c.cnt["v"]

        @block.tensor
        def _(t):
            c.w(t, "v", rows_done)
            t.matmul(P[1][:, 0:512], st.ones_r[:], srow_bf[0:1, 0:512],
                     start=True, stop=True)
            ins = t.matmul(P[1][:, 512:1024], st.ones_r[:], srow_bf[0:1, 512:1024],
                           start=True, stop=True)
            c.inc(ins, "t")
        bc_mm = c.cnt["t"]

        @block.vector
        def _(v):
            c.w(v, "t", bc_mm)
            v.tensor_copy(out=MB2[:], in_=P[1][:, 0:512])
            ins = v.tensor_copy(out=RB2[:], in_=P[1][:, 512:1024])
            c.inc(ins, "v")
        bc_done = c.cnt["v"]

        for hm in range(8):
            @block.vector
            def _(v):
                if hm == 0:
                    c.w(v, "v", bc_done)
                c.w(v, "q")
                sl = slice(hm * 512, (hm + 1) * 512)
                v.tensor_tensor(out=x1o[:], in0=zfp[:, sl], in1=MB2[:], op=ALU.subtract)
                ins = v.tensor_tensor(out=x1o[:], in0=x1o[:], in1=RB2[:], op=ALU.mult)
                c.vsw(v, ins)
                v.tensor_copy(out=x1b2[:], in_=x1o[:])
                v.drain()
                ins = v.tensor_tensor(out=x1l2[:], in0=x1o[:], in1=x1b2[:],
                                      op=ALU.subtract)
                c.inc(ins, "v")
            x1_done = c.cnt["v"]

            @block.sync
            def _(sy):
                c.w(sy, "v", x1_done)
                c.dma(sy, "q",
                      d["x1scr"][hm * 128:(hm + 1) * 128,
                                 tt * 512:(tt + 1) * 512], x1o[:])
                c.dma(sy, "q",
                      d["x1bfd"][hm * 128:(hm + 1) * 128,
                                 tt * 512:(tt + 1) * 512], x1b2[:])
                c.dma(sy, "q",
                      d["x1lod"][hm * 128:(hm + 1) * 128,
                                 tt * 512:(tt + 1) * 512], x1l2[:])
    st.x1_stores = c.cnt["q"]

    if st.dbg.get("x1"):
        @block.gpsimd
        def _(g):
            c.w(g, "q", st.x1_stores)
            c.dma(g, "g", st.dbg_ap["x1"], d["x1scr"][:, :])


def ffn(st):
    nc, c, block, d = st.nc, st.c, st.block, st.dram
    sb = st.sb
    P = st.P

    x1bf = sb("x1bf", [128, 8 * 1024])
    xlo = sb("xlo", [128, 8 * 1024])
    ghi = sb("ghi", [128, 8 * 16])
    glo = sb("glo", [128, 8 * 16])
    g32 = sb("g32", [128, 8 * 16], F32)
    brows = sb("brows", [1, 48])
    b32 = sb("b32", [1, 16], F32)
    rsigall = sb("rsigall", [128, 128], F32)
    rtmp = sb("rtmp", [128, 16], F32)
    eq1 = sb("eq1", [128, 16], F32)
    eq2 = sb("eq2", [128, 16], F32)
    m1c = sb("m1c", [128, 1], F32)
    m2c = sb("m2c", [128, 1], F32)
    mRc = sb("mRc", [128, 1], F32)
    wts = sb("wts", [128, 8 * NE], F32)
    wg_sb = [sb(f"wg_sb{i}", [128, 8 * 512]) for i in range(2)]
    wu_sb = [sb(f"wu_sb{i}", [128, 8 * 512]) for i in range(2)]
    sgl = sb("sgl", [128, 512])
    huw = sb("huw", [128, 512])
    act_sb = sb("act_sb", [128, 8 * 512])
    wd_t = [sb(f"wd_t{i}", [128, 1024]) for i in range(2)]
    ac_t = [sb(f"ac_t{i}", [128, 512], is_transpose_dest=True) for i in range(2)]
    zfp2 = sb("zfp2", [128, 8 * 512], F32)
    zb2 = sb("zb2", [128, 512])
    sqb2 = sb("sqb2", [128, 512])
    sr2 = sb("sr2", [1, 1024], F32)
    sr2b = sb("sr2b", [1, 1024])
    vr2 = sb("vr2", [1, 512], F32)
    MB3 = sb("MB3", [128, 512])
    RB3 = sb("RB3", [128, 512])
    xfp3 = sb("xfp3", [128, 512], F32)
    outt = sb("outt", [128, 512], F32)

    @block.gpsimd
    def _(g):
        c.w(g, "q", st.x1_stores)
        c.dma(g, "g", ghi[:].rearrange("p (k e) -> p k e", k=8), d["gwH"].rearrange("(k p) e -> p k e", p=128))
        c.dma(g, "g", g32[:].rearrange("p (k e) -> p k e", k=8), d["gwH"].rearrange("(k p) e -> p k e", p=128))
        c.dma(g, "g", brows[0:1, 0:16], d["gbH"])
        c.dma(g, "g", b32[:], d["gbH"])
    g_ld = c.cnt["g"]

    @block.vector
    def _(v):
        c.w(v, "g", g_ld)
        v.tensor_tensor(out=glo[:], in0=g32[:], in1=ghi[:], op=ALU.subtract)
        ins = v.scalar_tensor_tensor(out=brows[0:1, 16:32], in0=brows[0:1, 0:16],
                                     scalar=-1.0, in1=b32[:],
                                     op0=ALU.mult, op1=ALU.add)
        c.inc(ins, "v")
    glo_done = c.cnt["v"]

    for cc in range(4):
        col0 = cc * 1024

        @block.sync
        def _(sy):
            c.w(sy, "q", st.x1_stores)
            if cc > 0:
                c.w(sy, "t")
            c.dma(sy, "q", x1bf[:].rearrange("p (k t) -> p k t", k=8),
                  d["x1bfd"][:, col0:col0 + 1024].rearrange("(k p) t -> p k t", p=128))
            c.dma(sy, "q", xlo[:].rearrange("p (k t) -> p k t", k=8),
                  d["x1lod"][:, col0:col0 + 1024].rearrange("(k p) t -> p k t", p=128))
        xlo_done = c.cnt["q"]

        @block.vector
        def _(v):
            ins = v.memset(wts[:], 1.0)
            c.inc(ins, "v")
        wts_init = c.cnt["v"]

        for tm in range(8):
            @block.tensor
            def _(t):
                if tm == 0:
                    c.w(t, "q", xlo_done)
                    c.w(t, "v", wts_init)
                first = True
                for k in range(8):
                    t.matmul(P[2][:, tm * 16:tm * 16 + 16],
                             x1bf[:, k * 1024 + tm * 128:k * 1024 + tm * 128 + 128],
                             ghi[:, k * 16:(k + 1) * 16], start=first, stop=False)
                    first = False
                    t.matmul(P[2][:, tm * 16:tm * 16 + 16],
                             x1bf[:, k * 1024 + tm * 128:k * 1024 + tm * 128 + 128],
                             glo[:, k * 16:(k + 1) * 16], start=False, stop=False)
                    t.matmul(P[2][:, tm * 16:tm * 16 + 16],
                             xlo[:, k * 1024 + tm * 128:k * 1024 + tm * 128 + 128],
                             ghi[:, k * 16:(k + 1) * 16], start=False, stop=False)
                t.matmul(P[2][:, tm * 16:tm * 16 + 16], st.ones_r[:], brows[0:1, 0:16],
                         start=False, stop=False)
                ins = t.matmul(P[2][:, tm * 16:tm * 16 + 16], st.ones_r[:],
                               brows[0:1, 16:32], start=False, stop=(True))
                if tm == 7:
                    c.inc(ins, "t")
        r_mm = c.cnt["t"]

        @block.scalar
        def _(sc):
            c.w(sc, "t", r_mm)
            ins = sc.activation(out=rsigall[:], in_=P[2][:, 0:128], func=AF.Sigmoid)
            c.inc(ins, "c")
        sig_done = c.cnt["c"]

        for tm in range(8):
            @block.vector
            def _(v):
                if tm == 0:
                    c.w(v, "c", sig_done)
                rsig_t = rsigall[:, tm * 16:(tm + 1) * 16]
                v.reduce_max(out=m1c[:], in_=rsig_t, axis=mybir.AxisListType.X)
                v.drain()
                v.tensor_scalar(out=eq1[:], in0=rsig_t, scalar1=m1c[:], scalar2=None,
                                op0=ALU.is_equal)
                v.drain()
                v.scalar_tensor_tensor(out=rtmp[:], in0=eq1[:], scalar=-1e30,
                                       in1=rsig_t, op0=ALU.mult, op1=ALU.add)
                v.drain()
                v.reduce_max(out=m2c[:], in_=rtmp[:], axis=mybir.AxisListType.X)
                v.drain()
                v.tensor_scalar(out=eq2[:], in0=rtmp[:], scalar1=m2c[:],
                                scalar2=None, op0=ALU.is_equal)
                v.drain()
                v.tensor_tensor(out=eq1[:], in0=eq1[:], in1=eq2[:], op=ALU.add)
                v.tensor_tensor(out=m1c[:], in0=m1c[:], in1=m2c[:], op=ALU.add)
                v.drain()
                v.reciprocal(out=mRc[:], in_=m1c[:])
                v.drain()
                v.tensor_scalar(out=rtmp[:], in0=rsig_t, scalar1=mRc[:], scalar2=None,
                                op0=ALU.mult)
                v.drain()
                ins = v.tensor_tensor(out=wts[:, tm * NE:tm * NE + 16], in0=rtmp[:],
                                      in1=eq1[:], op=ALU.mult)
                c.inc(ins, "v")
        wts_done = c.cnt["v"]

        st_act = []
        for e in range(NE):
            ebuf = e % 2

            @block.sync
            def _(sy):
                if e >= 2:
                    c.w(sy, "t")
                c.dma(sy, "q", wg_sb[ebuf][:].rearrange("p (k i) -> p k i", k=8),
                      d["wgB"][:, e * 512:(e + 1) * 512]
                      .rearrange("(k p) i -> p k i", p=128))
                c.dma(sy, "q", wu_sb[ebuf][:].rearrange("p (k i) -> p k i", k=8),
                      d["wuB"][:, e * 512:(e + 1) * 512]
                      .rearrange("(k p) i -> p k i", p=128))
            w_ld = c.cnt["q"]

            for tm in range(8):
                qg = [P[0], P[2]][tm % 2]
                qu = [P[1], P[3]][tm % 2]
                half = ((tm // 2) % 2) * 512

                @block.tensor
                def _(t):
                    if tm == 0:
                        c.w(t, "q", w_ld)
                        c.w(t, "v", wts_done)
                        c.w(t, "g", st.wcast_done)
                    if tm >= 4:
                        c.w(t, "v")
                        c.w(t, "c")
                    ins = None
                    for k in range(8):
                        t.matmul(qg[:, half:half + 512],
                                 x1bf[:, k * 1024 + tm * 128:k * 1024 + tm * 128 + 128],
                                 wg_sb[ebuf][:, k * 512:(k + 1) * 512],
                                 start=(k == 0), stop=(k == 7))
                        ins = t.matmul(qu[:, half:half + 512],
                                       x1bf[:, k * 1024 + tm * 128:
                                            k * 1024 + tm * 128 + 128],
                                       wu_sb[ebuf][:, k * 512:(k + 1) * 512],
                                       start=(k == 0), stop=(k == 7))
                    c.inc(ins, "t")
                up_mm = c.cnt["t"]

                @block.scalar
                def _(sc):
                    c.w(sc, "t", up_mm)
                    ins = sc.activation(out=sgl[:], in_=qg[:, half:half + 512],
                                        func=AF.Silu)
                    c.inc(ins, "c")
                sg_done = c.cnt["c"]

                @block.vector
                def _(v):
                    c.w(v, "c", sg_done)
                    if tm == 0:
                        c.w(v, "s")
                    v.tensor_scalar(out=huw[:], in0=qu[:, half:half + 512],
                                    scalar1=wts[:, tm * NE + e:tm * NE + e + 1],
                                    scalar2=None, op0=ALU.mult)
                    ins = v.tensor_tensor(out=act_sb[:, tm * 512:(tm + 1) * 512],
                                          in0=sgl[:], in1=huw[:], op=ALU.mult)
                    c.inc(ins, "v")
            a_done = c.cnt["v"]

            @block.scalar
            def _(sc):
                c.w(sc, "v", a_done)
                c.dma(sc, "s",
                      d["act_dram"][e * 1024:(e + 1) * 1024, :]
                      .rearrange("(tm p) i -> p tm i", p=128),
                      act_sb[:].rearrange("p (tm i) -> p tm i", tm=8))
            st_act.append(c.cnt["s"])

        for th in range(2):
            nk = KDE // 128
            ld_hist = []
            mm_hist = []
            for k in range(nk):
                kb = k % 2
                e_of_k = k // 4

                @block.sync
                def _(sy):
                    if k == 0:
                        c.w(sy, "s", st_act[-1])
                        c.w(sy, "t")
                    if k >= 2:
                        c.w(sy, "t", mm_hist[k - 2])
                    c.dma(sy, "q", wd_t[kb][:], d["wdB"][k * 128:(k + 1) * 128, :])
                ld_w = c.cnt["q"]

                @block.scalar
                def _(sc):
                    if k == 0:
                        c.w(sc, "s", st_act[-1])
                        c.w(sc, "t")
                    if k >= 2:
                        c.w(sc, "t", mm_hist[k - 2])
                    c.tr(sc, "s", ac_t[kb][:],
                         d["act_dram"][e_of_k * 1024 + th * 512:
                                       e_of_k * 1024 + th * 512 + 512,
                                       (k % 4) * 128:(k % 4) * 128 + 128])
                ld_hist.append((ld_w, c.cnt["s"]))

                @block.tensor
                def _(t):
                    c.w(t, "q", ld_hist[-1][0])
                    c.w(t, "s", ld_hist[-1][1])
                    ins = None
                    for m in range(8):
                        ins = t.matmul(P[m // 2][:, (m % 2) * 512:(m % 2) * 512 + 512],
                                       wd_t[kb][:, m * 128:(m + 1) * 128],
                                       ac_t[kb][:],
                                       start=(k == 0), stop=(k == nk - 1))
                    c.inc(ins, "t")
                mm_hist.append(c.cnt["t"])
            dn_mm = c.cnt["t"]

            for m in range(8):
                @block.sync
                def _(sy):
                    if m == 0:
                        c.w(sy, "t", dn_mm)
                    c.w(sy, "v")
                    c.dma(sy, "q", xfp3[:],
                          d["x1scr"][m * 128:(m + 1) * 128,
                                     col0 + th * 512:col0 + th * 512 + 512])
                xl = c.cnt["q"]

                @block.vector
                def _(v):
                    c.w(v, "q", xl)
                    ins = v.scalar_tensor_tensor(
                        out=zfp2[:, m * 512:(m + 1) * 512], in0=xfp3[:], scalar=2.0,
                        in1=P[m // 2][:, (m % 2) * 512:(m % 2) * 512 + 512],
                        op0=ALU.mult, op1=ALU.add)
                    c.inc(ins, "v")
            z2_done = c.cnt["v"]

            if cc == 0 and th == 0 and st.dbg.get("ffn0"):
                @block.gpsimd
                def _(g):
                    c.w(g, "v", z2_done)
                    c.dma(g, "g", st.dbg_ap["zfp2"], zfp2[:])

            for m in range(8):
                @block.scalar
                def _(sc):
                    if m == 0:
                        c.w(sc, "v", z2_done)
                    sl = slice(m * 512, (m + 1) * 512)
                    sc.activation(out=zb2[:], in_=zfp2[:, sl], func=AF.Identity)
                    ins = sc.activation(out=sqb2[:], in_=zfp2[:, sl], func=AF.Square)
                    c.inc(ins, "c")
                zb2_done = c.cnt["c"]

                @block.tensor
                def _(t):
                    c.w(t, "c", zb2_done)
                    t.matmul(P[0][0:1, 0:512], st.ones_c[:], zb2[:],
                             start=(m == 0), stop=(m == 7))
                    ins = t.matmul(P[0][0:1, 512:1024], st.ones_c[:], sqb2[:],
                                   start=(m == 0), stop=(m == 7))
                    if m == 7:
                        c.inc(ins, "t")
            st2_mm = c.cnt["t"]

            @block.vector
            def _(v):
                c.w(v, "t", st2_mm)
                v.tensor_scalar(out=sr2[0:1, 0:512], in0=P[0][0:1, 0:512],
                                scalar1=1.0 / float(H), scalar2=None, op0=ALU.mult)
                ins = v.tensor_scalar(out=sr2[0:1, 512:1024], in0=P[0][0:1, 512:1024],
                                      scalar1=1.0 / float(H), scalar2=None, op0=ALU.mult)
                c.vsw(v, ins)
                ins = v.tensor_tensor(out=vr2[0:1, 0:512], in0=sr2[0:1, 0:512],
                                      in1=sr2[0:1, 0:512], op=ALU.mult)
                c.vsw(v, ins)
                ins = v.tensor_tensor(out=vr2[0:1, 0:512], in0=sr2[0:1, 512:1024],
                                      in1=vr2[0:1, 0:512], op=ALU.subtract)
                c.inc(ins, "v")
            v2_done = c.cnt["v"]

            @block.vector
            def _(v):
                c.w(v, "v", v2_done)
                ins = v.tensor_scalar(out=vr2[0:1, 0:512], in0=vr2[0:1, 0:512],
                                      scalar1=LNEPS, scalar2=None, op0=ALU.add)
                c.inc(ins, "v")

            @block.scalar
            def _(sc):
                c.w(sc, "v")
                ins = sc.activation(out=vr2[0:1, 0:512], in_=vr2[0:1, 0:512],
                                    func=AF.Sqrt)
                c.inc(ins, "c")

            @block.vector
            def _(v):
                c.w(v, "c")
                v.reciprocal(out=sr2[0:1, 512:1024], in_=vr2[0:1, 0:512])
                v.drain()
                v.tensor_copy(out=sr2b[0:1, 0:512], in_=sr2[0:1, 0:512])
                ins = v.tensor_copy(out=sr2b[0:1, 512:1024], in_=sr2[0:1, 512:1024])
                c.inc(ins, "v")
            rows2 = c.cnt["v"]

            @block.tensor
            def _(t):
                c.w(t, "v", rows2)
                t.matmul(P[1][:, 0:512], st.ones_r[:], sr2b[0:1, 0:512],
                         start=True, stop=True)
                ins = t.matmul(P[1][:, 512:1024], st.ones_r[:], sr2b[0:1, 512:1024],
                               start=True, stop=True)
                c.inc(ins, "t")
            bc2_mm = c.cnt["t"]

            @block.vector
            def _(v):
                c.w(v, "t", bc2_mm)
                v.tensor_copy(out=MB3[:], in_=P[1][:, 0:512])
                ins = v.tensor_copy(out=RB3[:], in_=P[1][:, 512:1024])
                c.inc(ins, "v")
            bc2_done = c.cnt["v"]

            for m in range(8):
                @block.vector
                def _(v):
                    if m == 0:
                        c.w(v, "v", bc2_done)
                    c.w(v, "q")
                    sl = slice(m * 512, (m + 1) * 512)
                    v.tensor_tensor(out=outt[:], in0=zfp2[:, sl], in1=MB3[:],
                                    op=ALU.subtract)
                    ins = v.tensor_tensor(out=outt[:], in0=outt[:], in1=RB3[:],
                                          op=ALU.mult)
                    c.inc(ins, "v")
                o_done = c.cnt["v"]

                @block.sync
                def _(sy):
                    c.w(sy, "v", o_done)
                    c.dma(sy, "q",
                          d["yout"][m * 128:(m + 1) * 128,
                                    col0 + th * 512:col0 + th * 512 + 512], outt[:])

    @block.gpsimd
    def _(g):
        for sname in ("q", "g", "s", "v", "c", "t"):
            c.w(g, sname)


def build_nc(dbg=None):
    dbg = dict(dbg or {})
    nc = bass.Bass("TRN2", target_bir_lowering=False)
    d = {}

    def din(name, shape, dt=F32):
        d[name] = nc.dram_tensor(name, shape, dt, kind="ExternalInput").ap()

    def dou(name, shape, dt=F32):
        d[name] = nc.dram_tensor(name, shape, dt, kind="ExternalOutput").ap()

    def dit(name, shape, dt=BF16):
        d[name] = nc.dram_tensor(name, shape, dt).ap()

    din("xfc", [CPC * H, T]); din("xfb", [2 * H, S])
    din("wqT", [H, H]); din("wkT", [H, H]); din("wvT", [H, H])
    din("qn", [128, 8]); din("kn", [128, 8]); din("knrow", [1, H])
    din("twT", [H, T]); din("twA", [H, T]); din("twE", [H, T])
    din("w0fs", [128, 8 * JS]); din("w1js", [128, 2 * H]); din("lnr", [1, H])
    din("eselh", [16, 128]); din("onesh", [128, 128]); din("eyeh", [128, 128])
    din("gwH", [H, 16]); din("gbH", [1, 16])
    din("wgS", [H, KDE]); din("wuS", [H, KDE]); din("wdS", [KDE, H])
    dou("yout", [H, TPC])

    dit("kd_in", [CPC * H, T]); dit("kd_all", [NCORE * CPC * H, T])
    dit("sm_in", [CPC, 32], F32); dit("sm_all", [NCH, 32], F32)
    dit("q_dram", [H, TPC])
    dit("ktok_dram", [TPC, H]); dit("vtok_dram", [TPC, H])
    dit("d_own", [NCH * 128, H])
    dit("hU_dram", [2 * JS, T])
    dit("yrs_in", [T, H]); dit("yrs_out", [128, H])
    dit("dy_in", [128, H]); dit("dy_all", [T, H])
    dit("nr_in", [1, 1536], F32); dit("nr_out", [1, 1536], F32)
    dit("w1scr", [JS, H])
    dit("w0_in", [H, JS]); dit("w0_all", [NCORE * H, JS])
    dit("w1_in", [JS, H]); dit("w1_all", [IN, H])
    dit("x1scr", [H, TPC], F32)
    dit("x1bfd", [H, TPC]); dit("x1lod", [H, TPC])
    dit("wgB", [H, KDE]); dit("wuB", [H, KDE]); dit("wdB", [KDE, H])
    dit("act_dram", [NE * T, 512])

    dbg_shapes = {
        "kall": ([NCORE * CPC * H, T], BF16), "sm": ([NCH, 32], F32),
        "qd": ([H, TPC], BF16), "down": ([NCH * 128, H], BF16),
        "w0f": ([128, 8 * JS], F32), "w1j": ([128, 2 * H], F32),
        "lnrow": ([1, H], F32), "yblk": ([128, H], F32),
        "dyall": ([T, H], BF16), "n2row": ([1, 1536], F32),
        "x1": ([H, TPC], F32),
        "ktd": ([TPC, H], BF16), "vtd": ([TPC, H], BF16),
        "GS": ([128, NCH], F32), "Qw": ([128, H], BF16), "ccol": ([128, 1], F32),
        "esb": ([128, H], F32), "t1s": ([128, H], F32), "wcol": ([128, 1], F32), "nrow": ([1, 1536], F32),
        "nri": ([1, 1536], F32), "nro": ([1, 1536], F32),
        "wts": ([128, 8 * NE], F32), "actd": ([NE * T, 512], BF16),
        "zfp2": ([128, 8 * 512], F32),
    }
    dbg_ap = {}
    need = set()
    if dbg.get("pa"):
        need |= {"kall", "sm", "qd", "down", "ktd", "vtd"}
    if dbg.get("scan_ch") is not None:
        need |= {"w0f", "w1j", "lnrow", "yblk", "dyall", "n2row",
                 "GS", "Qw", "ccol", "esb", "t1s", "wcol", "nrow", "nri", "nro"}
    if dbg.get("x1"):
        need.add("x1")
    if dbg.get("ffn0"):
        need |= {"wts", "actd", "zfp2"}
    for name in need:
        shp, dt = dbg_shapes[name]
        dbg_ap[name] = nc.dram_tensor("dbg_" + name, shp, dt, kind="ExternalOutput").ap()

    es = contextlib.ExitStack()
    with es:
        c = Ctx(nc, es)
        block = es.enter_context(nc.Block())

        def mk_sb(stack):
            return lambda name, shape, dt=BF16, **kw: stack.enter_context(
                nc.sbuf_tensor(name, shape, dt, **kw))

        psb = mk_sb(es)
        st = types.SimpleNamespace(nc=nc, c=c, block=block, dram=d, dbg=dbg,
                                   dbg_ap=dbg_ap)
        st.ones_r = psb("ones_r", [1, 128])
        st.ones_c = psb("ones_c", [128, 1])
        st.eye_bf = psb("eye_bf", [128, 128])
        st.eye_f32 = psb("eye_f32", [128, 128], F32)
        st.knrow_bf = psb("knrow_bf", [1, H])
        st.GS = psb("GS", [128, NCH], F32)
        st.BT = psb("BT", [128, NCH], F32)
        st.ET = psb("ET", [128, NCH], F32)
        st.BErow = psb("BErow", [1, 64], F32)
        st.w0f = psb("w0f", [128, 8 * JS], F32)
        st.w1j = psb("w1j", [128, 2 * H], F32)
        st.s0f = psb("s0f", [128, 8 * JS], F32)
        st.s1j = psb("s1j", [128, 2 * H], F32)
        st.w0f_bf = psb("w0f_bf", [128, 8 * JS])
        st.w1j_bf = psb("w1j_bf", [128, 2 * H])
        st.w1hc_bf = psb("w1hc_bf", [128, 8 * JS], is_transpose_dest=True)
        st.ln_row = psb("ln_row", [1, H], F32)
        st.sln_row = psb("sln_row", [1, H], F32)
        st.ln_bfr = psb("ln_bfr", [1, H])
        st.P = [es.enter_context(nc.psum_tensor(f"P{i}", [128, 1024], F32))
                for i in range(4)]

        @block.sync
        def _(sy):
            c.dma(sy, "q", st.eye_f32[:], d["eyeh"])

        esA = contextlib.ExitStack()
        st.sb = mk_sb(esA)
        phase_a(st)
        if dbg.get("pa"):
            @block.gpsimd
            def _(g):
                for sname in ("q", "g", "s", "v", "c", "t"):
                    c.w(g, sname)
                c.dma(g, "g", dbg_ap["kall"], d["kd_all"][:, :])
                c.dma(g, "g", dbg_ap["sm"], d["sm_all"][:, :])
                c.dma(g, "g", dbg_ap["qd"], d["q_dram"][:, :])
                c.dma(g, "g", dbg_ap["down"], d["d_own"][:, :])
                c.dma(g, "g", dbg_ap["ktd"], d["ktok_dram"][:, :])
                c.dma(g, "g", dbg_ap["vtd"], d["vtok_dram"][:, :])
        c.barrier(block)
        esA.close()

        if not dbg.get("stop_pa"):
            esS = contextlib.ExitStack()
            st.sb = mk_sb(esS)
            scan(st)
            c.barrier(block)
            esS.close()

        if not dbg.get("stop_pa") and not dbg.get("stop_scan"):
            esF = contextlib.ExitStack()
            st.sb = mk_sb(esF)
            final_attn(st)
            c.barrier(block)
            esF.close()

            esN = contextlib.ExitStack()
            st.sb = mk_sb(esN)
            ffn(st)
            c.barrier(block)
            esN.close()

        @block.gpsimd
        def _(g):
            for sname in ("q", "g", "s", "v", "c", "t"):
                c.w(g, sname)
    return nc


def _host_maps(inputs):
    f32 = np.float32
    x = np.ascontiguousarray(np.asarray(inputs["L0_attn_x"], f32))
    wq = np.asarray(inputs["L0_attn_wq_w"], f32)
    wk = np.asarray(inputs["L0_attn_wk_w"], f32)
    wv = np.asarray(inputs["L0_attn_wv_w"], f32)
    qnw = np.asarray(inputs["L0_attn_q_norm_w"], f32)
    knw = np.asarray(inputs["L0_attn_k_norm_w"], f32)
    aw = np.asarray(inputs["L0_attn_alpha_w"], f32)
    tw = np.asarray(inputs["L0_attn_theta_w"], f32)
    ew = np.asarray(inputs["L0_attn_eta_w"], f32)
    m0 = np.asarray(inputs["L0_attn_mem_w0"], f32)
    m1 = np.asarray(inputs["L0_attn_mem_w1"], f32)
    mln = np.asarray(inputs["L0_attn_mem_ln_w"], f32)
    gw = np.asarray(inputs["L0_ffn_gate_w"], f32)
    gb = np.asarray(inputs["L0_ffn_gate_b"], f32)
    swg = np.asarray(inputs["L0_ffn_shared_wg"], f32)
    swu = np.asarray(inputs["L0_ffn_shared_wu"], f32)
    swd = np.asarray(inputs["L0_ffn_shared_wd"], f32)
    ewg = np.asarray(inputs["L0_ffn_experts_wg"], f32)
    ewu = np.asarray(inputs["L0_ffn_experts_wu"], f32)
    ewd = np.asarray(inputs["L0_ffn_experts_wd"], f32)

    CC = np.ascontiguousarray
    shared = {
        "wqT": CC(wq.T), "wkT": CC(wk.T), "wvT": CC(wv.T),
        "qn": CC(qnw.reshape(8, 128).T), "kn": CC(knw.reshape(8, 128).T),
        "knrow": CC(knw.reshape(1, H)),
        "twT": CC(np.tile(tw.reshape(CH, H).T, (1, B))),
        "twA": CC(np.tile(aw.reshape(CH, H).T, (1, B))),
        "twE": CC(np.tile(ew.reshape(CH, H).T, (1, B))),
        "lnr": CC(mln.reshape(1, H)),
        "onesh": np.ones((128, 128), f32),
        "eyeh": np.eye(128, dtype=f32),
        "gwH": CC(gw), "gbH": CC(gb.reshape(1, 16)),
        "wgS": CC(np.concatenate(
            [ewg.transpose(1, 0, 2).reshape(H, 16 * 512), swg], axis=1)),
        "wuS": CC(np.concatenate(
            [ewu.transpose(1, 0, 2).reshape(H, 16 * 512), swu], axis=1)),
        "wdS": CC(np.concatenate([ewd.reshape(16 * 512, H), swd], axis=0)),
    }
    w0T = m0.T
    xc = x.reshape(B, NCH, CH, H)
    in_maps = []
    for r in range(NCORE):
        esel = np.zeros((16, 128), f32)
        for p in range(128):
            esel[2 * r + p // CH, p] = 1.0
        m = dict(shared)
        m["xfc"] = CC(xc[:, 4 * r:4 * r + 4].transpose(1, 3, 0, 2)
                      .reshape(CPC * H, T))
        m["xfb"] = CC(x[2 * r:2 * r + 2].transpose(0, 2, 1).reshape(2 * H, S))
        m["w0fs"] = CC(w0T[:, JS * r:JS * (r + 1)].reshape(8, 128, JS)
                       .transpose(1, 0, 2).reshape(128, 8 * JS))
        m["w1js"] = CC(m1[:, JS * r:JS * (r + 1)].T.reshape(2, 128, H)
                       .transpose(1, 0, 2).reshape(128, 2 * H))
        m["eselh"] = esel
        in_maps.append(m)
    return in_maps


def kernel(**inputs):
    in_maps = _host_maps(inputs)
    nc = build_nc()
    res = run_bass_kernel_spmd(nc, in_maps, core_ids=list(range(NCORE)))
    outs = []
    for r in range(NCORE):
        y = np.asarray(res.results[r]["yout"], dtype=np.float32)
        outs.append(y.T.reshape(2, S, H))
    return np.ascontiguousarray(np.concatenate(outs, axis=0))


# version 04
# speedup vs baseline: 1.0044x; 1.0012x over previous
import contextlib
import types
import numpy as np
import ml_dtypes
import concourse.bass as bass
import concourse.mybir as mybir
from concourse.bass_utils import run_bass_kernel_spmd

F32 = mybir.dt.float32
BF16 = mybir.dt.bfloat16
AF = mybir.ActivationFunctionType
ALU = mybir.AluOpType
bfd = ml_dtypes.bfloat16

RG = [[0, 1, 2, 3, 4, 5, 6, 7]]
NCORE = 8
B, S, H = 16, 2048, 1024
CH = 64
NCH = 32
IN = 2048
JS = IN // NCORE
T = B * CH
NLOSS = float(B * CH * H)
EPS = 1e-6
LNEPS = 1e-5
BASE_LR = 0.01
CPC = NCH // NCORE
TPC = 2 * S
NE = 17
KDE = NE * 512


class Ctx:
    def __init__(self, nc, es):
        self.nc = nc
        self.sems = {}
        self.cnt = {}
        for n in ("q", "g", "s", "v", "c", "t"):
            self.sems[n] = es.enter_context(nc.semaphore("sem_" + n))
            self.cnt[n] = 0

    def inc(self, instr, name, k=1):
        instr.then_inc(self.sems[name], k)
        self.cnt[name] += k
        return self.cnt[name]

    def dma(self, eng, qn, out, in_, **kw):
        return self.inc(eng.dma_start(out=out, in_=in_, **kw), qn, 16)

    def tr(self, eng, qn, out, in_):
        return self.inc(eng.dma_start_transpose(out=out, in_=in_), qn, 16)

    def w(self, eng, name, val=None):
        eng.wait_ge(self.sems[name], self.cnt[name] if val is None else val)

    def vsw(self, eng, ins, name="v"):
        eng.drain()

    def barrier(self, block):
        snap = dict(self.cnt)
        for ename in ("sync", "gpsimd", "scalar", "vector", "tensor"):
            dec = getattr(block, ename)

            @dec
            def _(e):
                for sname, v in snap.items():
                    e.wait_ge(self.sems[sname], v)


def _mmslice(buf, col0, width):
    return buf[:, col0:col0 + width]


def phase_a(st):
    nc, c, block, d = st.nc, st.c, st.block, st.dram
    sb = st.sb
    P = st.P

    knq = sb("knq", [128, 8], F32)
    qnq = sb("qnq", [128, 8], F32)
    wT_bf = sb("wT_bf", [128, 8 * H])
    tw_bf = sb("tw_bf", [128, 8 * T])
    xck = sb("xck", [128, 8 * T])
    silk = sb("silk", [128, 8 * T])
    kout = sb("kout", [128, 8 * T])
    sqs = sb("sqs", [128, 1024])
    rrow = sb("rrow", [1, 1024], F32)
    rrow2 = sb("rrow2", [1, 64], F32)
    rrowR = sb("rrowR", [1, 1024], F32)
    rstcR = sb("rstcR", [128, 1], F32)
    rbf = sb("rbf", [1, 1024])
    RBs = sb("RBs", [128, 1024])
    smrow = sb("smrow", [1, 32], F32)
    KNB = sb("KNB", [128, 1024])
    xbk = sb("xbk", [128, 8 * S])
    silt = sb("silt", [128, 1024])
    vtok = sb("vtok", [128, 1024])
    ktok = sb("ktok", [128, 1024])
    ssc = sb("ssc", [128, 1], F32)
    rstc = sb("rstc", [128, 1], F32)

    @block.sync
    def _(sy):
        c.dma(sy, "q", knq[:], d["kn"])
        c.dma(sy, "q", qnq[:], d["qn"])

    @block.vector
    def _(v):
        ins = v.memset(smrow[:], 0.0)
        c.inc(ins, "v")

    @block.gpsimd
    def _(g):
        c.dma(g, "g", st.ones_r[:], d["onesh"][0:1, :])
        c.dma(g, "g", st.ones_c[:], d["onesh"][:, 0:1],
              allow_slow_non_contiguous=True)
        c.dma(g, "g", st.eye_bf[:], d["eyeh"])
        c.dma(g, "g", st.knrow_bf[:], d["knrow"])
        c.dma(g, "g", wT_bf[:].rearrange("p (k h) -> p k h", k=8), d["wkT"].rearrange("(k p) h -> p k h", p=128))
    g_static = c.cnt["g"]

    for s in range(CPC):
        @block.gpsimd
        def _(g):
            if s == 0:
                c.w(g, "g", g_static)
            else:
                c.w(g, "g")
                c.w(g, "v")
                c.w(g, "t")
            c.dma(g, "g", xck[:].rearrange("p (k t) -> p k t", k=8),
                  d["xfc"][s * H:(s + 1) * H, :].rearrange("(k p) t -> p k t", p=128))
        ld_x = c.cnt["g"]

        for m in range(8):
            pm = P[m % 2]

            @block.tensor
            def _(t):
                if m == 0:
                    c.w(t, "g", ld_x)
                if m >= 2:
                    c.w(t, "c")
                ins = None
                for k in range(8):
                    for nh in range(2):
                        ins = t.matmul(
                            pm[:, nh * 512:nh * 512 + 512],
                            wT_bf[:, k * H + m * 128:k * H + m * 128 + 128],
                            xck[:, k * T + nh * 512:k * T + nh * 512 + 512],
                            start=(k == 0), stop=(k == 7))
                c.inc(ins, "t")
            mm_done = c.cnt["t"]

            @block.scalar
            def _(sc):
                c.w(sc, "t", mm_done)
                ins = sc.activation(out=silk[:, m * T:(m + 1) * T], in_=pm[:],
                                    func=AF.Silu)
                c.inc(ins, "c")
        sil_done = c.cnt["c"]

        for m in range(8):
            @block.scalar
            def _(sc):
                if m == 0:
                    c.w(sc, "c", sil_done)
                else:
                    c.w(sc, "t")
                ins = sc.activation(out=sqs[:], in_=silk[:, m * T:(m + 1) * T],
                                    func=AF.Square)
                c.inc(ins, "c")
            sq_done = c.cnt["c"]

            @block.tensor
            def _(t):
                c.w(t, "c", sq_done)
                ins = None
                for nh in range(2):
                    ins = t.matmul(P[2][0:1, nh * 512:nh * 512 + 512], st.ones_c[:],
                                   sqs[:, nh * 512:nh * 512 + 512],
                                   start=(m == 0), stop=(m == 7), skip_group_check=True)
                c.inc(ins, "t")
        ss_done = c.cnt["t"]

        @block.scalar
        def _(sc):
            c.w(sc, "t", ss_done)
            ins = sc.copy(out=rrow[:], in_=P[2][0:1, :])
            c.inc(ins, "c")

        @block.vector
        def _(v):
            c.w(v, "c")
            ins = v.tensor_scalar(out=rrow[:], in0=rrow[:], scalar1=1.0 / float(H),
                                  scalar2=EPS, op0=ALU.mult, op1=ALU.add)
            c.inc(ins, "v")

        @block.scalar
        def _(sc):
            c.w(sc, "v")
            ins = sc.activation(out=rrow[:], in_=rrow[:], func=AF.Sqrt)
            c.inc(ins, "c")

        @block.vector
        def _(v):
            c.w(v, "c")
            v.reciprocal(out=rrowR[:], in_=rrow[:])
            v.drain()
            ins = v.tensor_copy(out=rbf[:], in_=rrowR[:])
            c.inc(ins, "v")

        @block.tensor
        def _(t):
            c.w(t, "v")
            ins = None
            for nh in range(2):
                ins = t.matmul(P[3][:, nh * 512:nh * 512 + 512], st.ones_r[:],
                               rbf[0:1, nh * 512:nh * 512 + 512], start=True, stop=True)
            c.inc(ins, "t")

        @block.vector
        def _(v):
            c.w(v, "t")
            v.tensor_copy(out=RBs[:], in_=P[3][:])
            ins = None
            for m in range(8):
                ins = v.scalar_tensor_tensor(
                    out=kout[:, m * T:(m + 1) * T],
                    in0=silk[:, m * T:(m + 1) * T], scalar=knq[:, m:m + 1],
                    in1=RBs[:], op0=ALU.mult, op1=ALU.mult)
            c.inc(ins, "v")
        k_done = c.cnt["v"]

        @block.gpsimd
        def _(g):
            c.w(g, "v", k_done)
            c.dma(g, "g",
                  d["kd_in"][s * H:(s + 1) * H, :].rearrange("(k p) t -> p k t", p=128),
                  kout[:].rearrange("p (k t) -> p k t", k=8))

        for wi, wname in enumerate(("twT", "twA", "twE")):
            @block.gpsimd
            def _(g):
                if wi > 0:
                    c.w(g, "v")
                c.dma(g, "g", tw_bf[:].rearrange("p (k t) -> p k t", k=8),
                      d[wname].rearrange("(k p) t -> p k t", p=128))
            tw_ld = c.cnt["g"]

            @block.vector
            def _(v):
                c.w(v, "g", tw_ld)
                if wi == 0:
                    c.w(v, "v", k_done)
                ins = None
                for m in range(8):
                    ins = v.tensor_tensor(
                        out=silk[:, m * T:(m + 1) * T],
                        in0=xck[:, m * T:(m + 1) * T],
                        in1=tw_bf[:, m * T:(m + 1) * T], op=ALU.mult)
                c.inc(ins, "v")
            z_done = c.cnt["v"]

            @block.tensor
            def _(t):
                c.w(t, "v", z_done)
                ins = None
                for m in range(8):
                    for nh in range(2):
                        ins = t.matmul(P[2][0:1, nh * 512:nh * 512 + 512], st.ones_c[:],
                                       silk[:, m * T + nh * 512:m * T + nh * 512 + 512],
                                       start=(m == 0), stop=(m == 7))
                c.inc(ins, "t")
            dot_done = c.cnt["t"]

            @block.vector
            def _(v):
                c.w(v, "t", dot_done)
                v.tensor_copy(out=rrow[:], in_=P[2][0:1, :])
                v.drain()
                ins = v.reduce_sum(
                    out=rrow2[0:1, 0:16],
                    in_=rrow.rearrange("o (b i) -> o b i", b=16)[:, :, :],
                    axis=mybir.AxisListType.X)
                c.inc(ins, "v")
            seg_done = c.cnt["v"]

            if wi == 0:
                @block.scalar
                def _(sc):
                    c.w(sc, "v", seg_done)
                    ins = sc.activation(out=smrow[0:1, 0:16], in_=rrow2[0:1, 0:16],
                                        func=AF.Sigmoid)
                    c.inc(ins, "c")

                @block.vector
                def _(v):
                    c.w(v, "c")
                    ins = v.tensor_scalar(out=smrow[0:1, 0:16], in0=smrow[0:1, 0:16],
                                          scalar1=BASE_LR, scalar2=None, op0=ALU.mult)
                    c.inc(ins, "v")
            else:
                @block.vector
                def _(v):
                    v.drain()
                    ins = v.reduce_sum(out=rrow2[0:1, 16:17], in_=rrow2[0:1, 0:16],
                                       axis=mybir.AxisListType.X)
                    v.drain()
                    c.inc(ins, "v")

                @block.scalar
                def _(sc):
                    c.w(sc, "v")
                    pass

                @block.vector
                def _(v):
                    v.drain()
                    ins = v.tensor_scalar(out=rrow2[0:1, 16:17], in0=rrow2[0:1, 16:17],
                                          scalar1=1.0 / 16.0, scalar2=None, op0=ALU.mult)
                    v.drain()
                    c.inc(ins, "v")

                @block.scalar
                def _(sc):
                    c.w(sc, "v")
                    ins = sc.activation(out=rrow2[0:1, 17:18], in_=rrow2[0:1, 16:17],
                                        func=AF.Sigmoid)
                    c.inc(ins, "c")

                @block.vector
                def _(v):
                    c.w(v, "c")
                    if wi == 1:
                        ins = v.tensor_scalar(out=smrow[0:1, 16:17],
                                              in0=rrow2[0:1, 17:18],
                                              scalar1=-1.0, scalar2=1.0,
                                              op0=ALU.mult, op1=ALU.add)
                    else:
                        ins = v.tensor_copy(out=smrow[0:1, 17:18], in_=rrow2[0:1, 17:18])
                    c.inc(ins, "v")
        sm_done = c.cnt["v"]

        @block.gpsimd
        def _(g):
            c.w(g, "v", sm_done)
            c.dma(g, "g", d["sm_in"][s:s + 1, :], smrow[:])

    @block.tensor
    def _(t):
        c.w(t, "g")
        ins = t.matmul(P[3][:, 0:512], st.ones_r[:], st.knrow_bf[0:1, 0:512],
                       start=True, stop=True)
        ins = t.matmul(P[3][:, 512:1024], st.ones_r[:], st.knrow_bf[0:1, 512:1024],
                       start=True, stop=True)
        c.inc(ins, "t")

    @block.vector
    def _(v):
        c.w(v, "t")
        ins = v.tensor_copy(out=KNB[:], in_=P[3][:])
        c.inc(ins, "v")
    knb_done = c.cnt["v"]

    for proj in ("q", "k", "v"):
        wsrc = {"q": "wqT", "k": "wkT", "v": "wvT"}[proj]

        @block.gpsimd
        def _(g):
            c.w(g, "t")
            c.w(g, "v")
            c.w(g, "c")
            c.dma(g, "g", wT_bf[:].rearrange("p (k h) -> p k h", k=8), d[wsrc].rearrange("(k p) h -> p k h", p=128))
        w_ld = c.cnt["g"]

        for bb in range(2):
            @block.gpsimd
            def _(g):
                if bb == 0:
                    c.w(g, "g", w_ld)
                else:
                    c.w(g, "t")
                c.dma(g, "g", xbk[:].rearrange("p (k t) -> p k t", k=8),
                      d["xfb"][bb * H:(bb + 1) * H, :].rearrange("(k p) t -> p k t", p=128))
            xb_ld = c.cnt["g"]

            for tt in range(4):
                if proj == "q":
                    for m in range(8):
                        pm = P[m % 2]

                        @block.tensor
                        def _(t):
                            if m == 0 and tt == 0:
                                c.w(t, "g", xb_ld)
                            if m >= 2 or tt > 0:
                                c.w(t, "c")
                            ins = None
                            for k in range(8):
                                ins = t.matmul(
                                    pm[:, 0:512],
                                    wT_bf[:, k * H + m * 128:k * H + m * 128 + 128],
                                    xbk[:, k * S + tt * 512:k * S + tt * 512 + 512],
                                    start=(k == 0), stop=(k == 7))
                            c.inc(ins, "t")
                        mm_done = c.cnt["t"]

                        @block.scalar
                        def _(sc):
                            c.w(sc, "t", mm_done)
                            ins = sc.activation(out=silk[:, m * 512:(m + 1) * 512],
                                                in_=pm[:, 0:512], func=AF.Silu)
                            c.inc(ins, "c")
                    sil_done = c.cnt["c"]

                    for m in range(8):
                        @block.scalar
                        def _(sc):
                            if m == 0:
                                c.w(sc, "c", sil_done)
                            else:
                                c.w(sc, "t")
                            ins = sc.activation(out=sqs[:, 0:512],
                                                in_=silk[:, m * 512:(m + 1) * 512],
                                                func=AF.Square)
                            c.inc(ins, "c")
                        sq_done = c.cnt["c"]

                        @block.tensor
                        def _(t):
                            c.w(t, "c", sq_done)
                            ins = t.matmul(P[2][0:1, 0:512], st.ones_c[:], sqs[:, 0:512],
                                           start=(m == 0), stop=(m == 7),
                                           skip_group_check=True)
                            c.inc(ins, "t")

                    @block.vector
                    def _(v):
                        c.w(v, "t")
                        ins = v.tensor_scalar(out=rrow[0:1, 0:512], in0=P[2][0:1, 0:512],
                                              scalar1=1.0 / float(H), scalar2=EPS,
                                              op0=ALU.mult, op1=ALU.add)
                        c.inc(ins, "v")

                    @block.scalar
                    def _(sc):
                        c.w(sc, "v")
                        ins = sc.activation(out=rrow[0:1, 0:512], in_=rrow[0:1, 0:512],
                                            func=AF.Sqrt)
                        c.inc(ins, "c")

                    @block.vector
                    def _(v):
                        c.w(v, "c")
                        v.reciprocal(out=rrowR[0:1, 0:512], in_=rrow[0:1, 0:512])
                        v.drain()
                        ins = v.tensor_copy(out=rbf[0:1, 0:512], in_=rrowR[0:1, 0:512])
                        c.inc(ins, "v")

                    @block.tensor
                    def _(t):
                        c.w(t, "v")
                        ins = t.matmul(P[3][:, 0:512], st.ones_r[:], rbf[0:1, 0:512],
                                       start=True, stop=True)
                        c.inc(ins, "t")

                    @block.vector
                    def _(v):
                        c.w(v, "t")
                        v.tensor_copy(out=RBs[:, 0:512], in_=P[3][:, 0:512])
                        ins = None
                        for m in range(8):
                            ins = v.scalar_tensor_tensor(
                                out=kout[:, m * 512:(m + 1) * 512],
                                in0=silk[:, m * 512:(m + 1) * 512],
                                scalar=qnq[:, m:m + 1], in1=RBs[:, 0:512],
                                op0=ALU.mult, op1=ALU.mult)
                        c.inc(ins, "v")
                    q_done = c.cnt["v"]

                    @block.sync
                    def _(sy):
                        c.w(sy, "v", q_done)
                        c.dma(sy, "q",
                              d["q_dram"][:, bb * S + tt * 512:bb * S + tt * 512 + 512]
                              .rearrange("(k p) t -> p k t", p=128),
                              kout[:, 0:8 * 512].rearrange("p (k t) -> p k t", k=8))
                else:
                    for m in range(4):
                        pm = P[m % 2]

                        @block.tensor
                        def _(t):
                            if m == 0 and tt == 0:
                                c.w(t, "g", xb_ld)
                            if m >= 2 or tt > 0:
                                c.w(t, "c")
                                c.w(t, "v")
                            ins = None
                            for k in range(8):
                                for nh in range(2):
                                    ins = t.matmul(
                                        pm[:, nh * 512:nh * 512 + 512],
                                        xbk[:, k * S + tt * 512 + m * 128:
                                            k * S + tt * 512 + m * 128 + 128],
                                        wT_bf[:, k * H + nh * 512:k * H + nh * 512 + 512],
                                        start=(k == 0), stop=(k == 7))
                            c.inc(ins, "t")
                        mm_done = c.cnt["t"]

                        @block.scalar
                        def _(sc):
                            c.w(sc, "t", mm_done)
                            if proj == "k":
                                sc.activation(out=silt[:], in_=pm[:], func=AF.Silu)
                                ins = sc.activation(out=sqs[:], in_=silt[:],
                                                    func=AF.Square, accum_out=ssc[:])
                                c.inc(ins, "c")
                            else:
                                c.w(sc, "q")
                                ins = sc.activation(out=vtok[:], in_=pm[:], func=AF.Silu)
                                c.inc(ins, "c")
                        act_done = c.cnt["c"]

                        if proj == "k":
                            @block.vector
                            def _(v):
                                c.w(v, "c", act_done)
                                ins = v.tensor_scalar(out=rstc[:], in0=ssc[:],
                                                      scalar1=1.0 / float(H), scalar2=EPS,
                                                      op0=ALU.mult, op1=ALU.add)
                                c.inc(ins, "v")

                            @block.scalar
                            def _(sc):
                                c.w(sc, "v")
                                ins = sc.activation(out=rstc[:], in_=rstc[:], func=AF.Sqrt)
                                c.inc(ins, "c")

                            @block.vector
                            def _(v):
                                c.w(v, "c")
                                c.w(v, "q")
                                v.reciprocal(out=rstcR[:], in_=rstc[:])
                                v.drain()
                                ins = v.scalar_tensor_tensor(
                                    out=ktok[:], in0=silt[:], scalar=rstcR[:],
                                    in1=KNB[:], op0=ALU.mult, op1=ALU.mult)
                                c.inc(ins, "v")
                        ops_done = c.cnt["v"]

                        @block.sync
                        def _(sy):
                            row0 = (bb * S + tt * 512 + m * 128)
                            if proj == "k":
                                c.w(sy, "v", ops_done)
                                c.dma(sy, "q", d["ktok_dram"][row0:row0 + 128, :], ktok[:])
                            else:
                                c.w(sy, "c", act_done)
                                c.dma(sy, "q", d["vtok_dram"][row0:row0 + 128, :], vtok[:])

    dstream = sb("dstream", [128, 1024])
    kstream = sb("kstream", [128, 1024])

    for bb in range(2):
        for tt in range(4):
            for m in range(4):
                row0 = bb * S + tt * 512 + m * 128

                @block.sync
                def _(sy):
                    c.w(sy, "q")
                    c.w(sy, "v")
                    c.dma(sy, "q", kstream[:], d["ktok_dram"][row0:row0 + 128, :])
                    c.dma(sy, "q", dstream[:], d["vtok_dram"][row0:row0 + 128, :])
                ld2 = c.cnt["q"]

                @block.vector
                def _(v):
                    c.w(v, "q", ld2)
                    ins = v.tensor_tensor(out=dstream[:], in0=kstream[:], in1=dstream[:],
                                          op=ALU.subtract)
                    c.inc(ins, "v")
                d_done = c.cnt["v"]

                @block.sync
                def _(sy):
                    c.w(sy, "v", d_done)
                    base_tok = tt * 512 + m * 128
                    for seg in range(2):
                        ch = (base_tok + seg * 64) // CH
                        c.dma(sy, "q",
                              d["d_own"][ch * 128 + bb * CH:ch * 128 + bb * CH + CH, :],
                              dstream[seg * 64:(seg + 1) * 64, :])

    @block.gpsimd
    def _(g):
        c.w(g, "g")
        c.w(g, "q")
        c.w(g, "v")
        ins = g.collective_compute("AllGather", ALU.bypass, RG,
                                   ins=[d["kd_in"].opt()],
                                   outs=[d["kd_all"].opt()])
        c.inc(ins, "g", 1)
        ins = g.collective_compute("AllGather", ALU.bypass, RG,
                                   ins=[d["sm_in"].opt()],
                                   outs=[d["sm_all"].opt()])
        c.inc(ins, "g", 1)
    st.ag_done = c.cnt["g"]

    th_sb = sb("th_sb", [16, 32])
    esel_bf = sb("esel_bf", [16, 128])
    berow = sb("berow", [1, 64])

    @block.gpsimd
    def _(g):
        c.w(g, "g", st.ag_done)
        c.dma(g, "g", esel_bf[:], d["eselh"])
        c.dma(g, "g", th_sb[:], d["sm_all"][:, 0:16].rearrange("ch b -> b ch"),
              allow_slow_non_contiguous=True)
        c.dma(g, "g", berow[0:1, 0:32], d["sm_all"][:, 16:17].rearrange("ch o -> o ch"),
              allow_slow_non_contiguous=True)
        c.dma(g, "g", berow[0:1, 32:64], d["sm_all"][:, 17:18].rearrange("ch o -> o ch"),
              allow_slow_non_contiguous=True)
    tbl_ld = c.cnt["g"]

    @block.tensor
    def _(t):
        c.w(t, "g", tbl_ld)
        ins = t.matmul(P[0][:, 0:32], esel_bf[:], th_sb[:], start=True, stop=True)
        c.inc(ins, "t")
        ins = t.matmul(P[1][:, 0:64], st.ones_r[:], berow[0:1, :], start=True, stop=True)
        c.inc(ins, "t")

    @block.vector
    def _(v):
        c.w(v, "t")
        v.tensor_scalar(out=st.GS[:], in0=P[0][:, 0:32], scalar1=2.0 / NLOSS,
                        scalar2=None, op0=ALU.mult)
        v.tensor_copy(out=st.BT[:], in_=P[1][:, 0:32])
        v.tensor_copy(out=st.ET[:], in_=P[1][:, 32:64])
        ins = v.tensor_copy(out=st.BErow[:], in_=P[1][0:1, 0:64])
        c.inc(ins, "v")
    st.tbl_done = c.cnt["v"]


def scan(st):
    nc, c, block, d = st.nc, st.c, st.block, st.dram
    sb = st.sb
    P = st.P
    dbg = st.dbg

    kf = [sb(f"kf{i}", [128, 8 * T]) for i in range(2)]
    kt = sb("kt", [128, 8 * H], is_transpose_dest=True)
    dblk = [sb(f"dblk{i}", [128, H]) for i in range(2)]
    h_j = sb("h_j", [128, 2 * T])
    U_j = sb("U_j", [128, 2 * T])
    h_tok = sb("h_tok", [128, 8 * JS], is_transpose_dest=True)
    U_tok = sb("U_tok", [128, 8 * JS], is_transpose_dest=True)
    ypp = [sb(f"ypp{i}", [128, H]) for i in range(2)]
    yblk = sb("yblk", [128, H])
    LNB = sb("LNB", [128, H])
    t1s = sb("t1s", [128, H], F32)
    esb = sb("esb", [128, H], F32)
    Qsb = sb("Qsb", [128, H], F32)
    Qw = sb("Qw", [128, H])
    dyb = sb("dyb", [128, H])
    dy_tok = sb("dy_tok", [128, 8 * H])
    df2 = sb("df2", [128, 2 * T], is_transpose_dest=True)
    dW1sb = sb("dW1sb", [128, 2 * H], F32)
    dW0sb = sb("dW0sb", [128, 8 * JS], F32)
    dPre = sb("dPre", [128, 8 * JS])
    sscol = sb("sscol", [128, 1], F32)
    rstdc = sb("rstdc", [128, 1], F32)
    rstdR = sb("rstdR", [128, 1], F32)
    ccol = sb("ccol", [128, 1], F32)
    wcol = sb("wcol", [128, 1], F32)
    bneg = sb("bneg", [128, 1], F32)
    nrmc = sb("nrmc", [128, 12], F32)
    nrmb = sb("nrmb", [128, 12])
    nrow = sb("nrow", [1, 1536], F32)
    n2row = sb("n2row", [1, 1536], F32)
    scr_bf = sb("scr_bf", [128, H])
    clipb = sb("clipb", [1, 2])
    ncol = sb("ncol", [128, 1], F32)

    w0f, w1j, s0f, s1j = st.w0f, st.w1j, st.s0f, st.s1j
    w0f_bf, w1j_bf, w1hc_bf = st.w0f_bf, st.w1j_bf, st.w1hc_bf
    ln_row, sln_row, ln_bfr = st.ln_row, st.sln_row, st.ln_bfr

    @block.sync
    def _(sy):
        c.w(sy, "g", st.ag_done)
        c.dma(sy, "q", w0f[:], d["w0fs"])
        c.dma(sy, "q", w1j[:], d["w1js"])
        c.dma(sy, "q", ln_row[:], d["lnr"])
    st_ld = c.cnt["q"]

    @block.vector
    def _(v):
        c.w(v, "q", st_ld)
        v.memset(s0f[:], 0.0)
        v.memset(s1j[:], 0.0)
        v.memset(sln_row[:], 0.0)
        v.memset(nrow[:], 0.0)
        v.tensor_copy(out=w0f_bf[:], in_=w0f[:])
        v.tensor_copy(out=w1j_bf[:], in_=w1j[:])
        ins = v.tensor_copy(out=ln_bfr[:], in_=ln_row[:])
        c.inc(ins, "v")
    init_done = c.cnt["v"]

    def emit_w1hc(wait_v):
        @block.sync
        def _(sy):
            c.w(sy, "v", wait_v)
            c.dma(sy, "q", d["w1scr"][:, :].rearrange("(k p) h -> p k h", p=128),
                  w1j_bf[:].rearrange("p (k h) -> p k h", k=2))
            c.w(sy, "q")
            for m in range(8):
                c.tr(sy, "q", w1hc_bf[:, m * JS:(m + 1) * JS],
                     d["w1scr"][:, m * 128:(m + 1) * 128])
        return c.cnt["q"]

    w1hc_ready = emit_w1hc(init_done)

    def kd_base(ch):
        return (ch // CPC) * CPC * H + (ch % CPC) * H

    def emit_kf(ch, buf, wait_t):
        kb = kd_base(ch)

        @block.sync
        def _(sy):
            c.w(sy, "g", st.ag_done)
            c.w(sy, "t", wait_t)
            c.dma(sy, "q", kf[buf][:].rearrange("p (k t) -> p k t", k=8),
                  d["kd_all"][kb:kb + H, :].rearrange("(k p) t -> p k t", p=128))
        return c.cnt["q"]

    def emit_kt(ch, wait_t):
        kb = kd_base(ch)

        @block.sync
        def _(sy):
            c.w(sy, "g", st.ag_done)
            c.w(sy, "t", wait_t)
            for m in range(8):
                c.tr(sy, "q", kt[:, m * H:(m + 1) * H],
                     d["kd_all"][kb:kb + H, m * 128:(m + 1) * 128])
        return c.cnt["q"]

    def emit_dblk(ch, buf, wait_v):
        @block.sync
        def _(sy):
            c.w(sy, "v", wait_v)
            c.dma(sy, "q", dblk[buf][:], d["d_own"][ch * 128:(ch + 1) * 128, :])
        return c.cnt["q"]

    kf_rdy = emit_kf(0, 0, 0)
    kt_rdy = emit_kt(0, 0)
    db_rdy = emit_dblk(0, 0, 0)
    gbf_done_box = [0]
    gbf_done = 0
    kf_rdy_next = 0
    m1_done_prev = 0
    m5_done_prev = 0
    ew_done_prev = 0
    upd_done = init_done

    for ch in range(NCH):
        buf = ch % 2
        nbuf = 1 - buf

        if ch + 1 < NCH:
            kf_rdy_next = emit_kf(ch + 1, nbuf, m1_done_prev)
            db_rdy_next = emit_dblk(ch + 1, nbuf, ew_done_prev)

        @block.tensor
        def _(t):
            c.w(t, "q", kf_rdy)
            if ch == 0:
                c.w(t, "v", upd_done)
                for jm in range(2):
                    ins = None
                    for nh in range(2):
                        for k in range(8):
                            ins = t.matmul(
                                P[jm][:, nh * 512:nh * 512 + 512],
                                w0f_bf[:, k * JS + jm * 128:k * JS + jm * 128 + 128],
                                kf[buf][:, k * T + nh * 512:k * T + nh * 512 + 512],
                                start=(k == 0), stop=(k == 7))
                    c.inc(ins, "t")
            else:
                c.w(t, "v", gbf_done)
                for jm in range(2):
                    ins = None
                    for nh in range(2):
                        for k in range(8):
                            ins = t.matmul(
                                P[jm][:, nh * 512:nh * 512 + 512],
                                dPre[:, k * JS + jm * 128:k * JS + jm * 128 + 128],
                                kf[buf][:, k * T + nh * 512:k * T + nh * 512 + 512],
                                start=False, stop=(k == 7), skip_group_check=True)
                    c.inc(ins, "t")
        m1_done = c.cnt["t"]
        m1_done_prev = m1_done

        @block.scalar
        def _(sc):
            c.w(sc, "t", m1_done)
            for jm in range(2):
                sc.activation(out=h_j[:, jm * T:(jm + 1) * T], in_=P[jm][:], func=AF.Silu)
                ins = sc.activation(out=U_j[:, jm * T:(jm + 1) * T], in_=P[jm][:],
                                    func=AF.Derivative_silu)
            c.inc(ins, "c")
        hu_done = c.cnt["c"]

        @block.scalar
        def _(sc):
            c.w(sc, "c", hu_done)
            c.dma(sc, "s", d["hU_dram"][0:JS, :].rearrange("(k p) t -> p k t", p=128),
                  h_j[:].rearrange("p (k t) -> p k t", k=2))
            c.dma(sc, "s", d["hU_dram"][JS:2 * JS, :].rearrange("(k p) t -> p k t", p=128),
                  U_j[:].rearrange("p (k t) -> p k t", k=2))
            c.w(sc, "s")
            for m in range(8):
                c.tr(sc, "s", h_tok[:, m * JS:(m + 1) * JS],
                     d["hU_dram"][0:JS, m * 128:(m + 1) * 128])
                c.tr(sc, "s", U_tok[:, m * JS:(m + 1) * JS],
                     d["hU_dram"][JS:2 * JS, m * 128:(m + 1) * 128])
        hU_tok_rdy = c.cnt["s"]

        gv_y = []
        for tm in range(8):
            pm = P[2 + (tm % 2)]

            @block.tensor
            def _(t):
                if tm == 0:
                    c.w(t, "c", hu_done)
                if tm >= 2:
                    c.w(t, "v")
                ins = None
                for nh in range(2):
                    for k in range(2):
                        ins = t.matmul(
                            pm[:, nh * 512:nh * 512 + 512],
                            h_j[:, k * T + tm * 128:k * T + tm * 128 + 128],
                            w1j_bf[:, k * H + nh * 512:k * H + nh * 512 + 512],
                            start=(k == 0), stop=(k == 1))
                c.inc(ins, "t")
            y_mm = c.cnt["t"]

            @block.vector
            def _(v):
                c.w(v, "t", y_mm)
                if tm >= 2:
                    c.w(v, "g")
                ins = v.tensor_copy(out=ypp[tm % 2][:], in_=pm[:])
                c.inc(ins, "v")
            y_ev = c.cnt["v"]

            @block.gpsimd
            def _(g):
                c.w(g, "v", y_ev)
                if ch > 0 and tm == 0:
                    c.w(g, "g")
                c.dma(g, "g", d["yrs_in"][tm * 128:(tm + 1) * 128, :], ypp[tm % 2][:])
            gv_y.append(c.cnt["g"])

        @block.gpsimd
        def _(g):
            c.w(g, "g", gv_y[-1])
            ins = g.collective_compute("ReduceScatter", ALU.add, RG,
                                       ins=[d["yrs_in"].opt()],
                                       outs=[d["yrs_out"].opt()])
            c.inc(ins, "g", 1)
        rs_done = c.cnt["g"]

        @block.sync
        def _(sy):
            c.w(sy, "g", rs_done)
            c.dma(sy, "q", yblk[:], d["yrs_out"][:, :])
        yblk_rdy = c.cnt["q"]

        @block.tensor
        def _(t):
            c.w(t, "v", upd_done)
            ins = None
            for nh in range(2):
                ins = t.matmul(P[0][:, nh * 512:nh * 512 + 512], st.ones_r[:],
                               ln_bfr[0:1, nh * 512:nh * 512 + 512],
                               start=True, stop=True)
            c.inc(ins, "t")
        lnb_mm = c.cnt["t"]

        @block.vector
        def _(v):
            c.w(v, "t", lnb_mm)
            ins = v.tensor_copy(out=LNB[:], in_=P[0][:])
            c.inc(ins, "v")
        lnb_done = c.cnt["v"]

        @block.scalar
        def _(sc):
            c.w(sc, "q", yblk_rdy)
            ins = sc.activation(out=esb[:], in_=yblk[:], func=AF.Square,
                                accum_out=sscol[:])
            c.inc(ins, "c")
        ssq_done = c.cnt["c"]

        @block.vector
        def _(v):
            c.w(v, "c", ssq_done)
            ins = v.tensor_scalar(out=rstdc[:], in0=sscol[:], scalar1=1.0 / float(H),
                                  scalar2=EPS, op0=ALU.mult, op1=ALU.add)
            c.inc(ins, "v")

        @block.scalar
        def _(sc):
            c.w(sc, "v")
            ins = sc.activation(out=rstdc[:], in_=rstdc[:], func=AF.Sqrt)
            c.inc(ins, "c")
        rstd_sc = c.cnt["c"]

        @block.vector
        def _(v):
            c.w(v, "c", rstd_sc)
            c.w(v, "v", lnb_done)
            c.w(v, "q", db_rdy)
            v.reciprocal(out=rstdR[:], in_=rstdc[:])
            v.drain()
            v.scalar_tensor_tensor(out=t1s[:], in0=yblk[:], scalar=rstdR[:],
                                   in1=LNB[:], op0=ALU.mult, op1=ALU.mult)
            v.tensor_tensor(out=esb[:], in0=t1s[:], in1=dblk[buf][:], op=ALU.add)
            v.tensor_tensor(out=Qsb[:], in0=yblk[:], in1=esb[:], op=ALU.mult)
            ins = v.tensor_tensor(out=t1s[:], in0=Qsb[:], in1=LNB[:], op=ALU.mult)
            c.inc(ins, "v")
        r4_done = c.cnt["v"]

        @block.scalar
        def _(sc):
            c.w(sc, "v", r4_done)
            ins = sc.activation(out=dW1sb[:, 0:H], in_=t1s[:], func=AF.Identity,
                                accum_out=ccol[:])
            c.inc(ins, "c")
        ccol_done = c.cnt["c"]

        @block.vector
        def _(v):
            c.w(v, "c", ccol_done)
            v.tensor_tensor(out=wcol[:], in0=st.GS[:, ch:ch + 1], in1=rstdR[:],
                            op=ALU.mult)
            v.drain()
            ins = v.tensor_scalar(out=Qw[:], in0=Qsb[:], scalar1=wcol[:], scalar2=None,
                                  op0=ALU.mult)
            c.inc(ins, "v")
            c.vsw(v, v.tensor_tensor(out=bneg[:], in0=rstdR[:], in1=rstdR[:], op=ALU.mult))
            c.vsw(v, v.tensor_tensor(out=bneg[:], in0=bneg[:], in1=wcol[:], op=ALU.mult))
            c.vsw(v, v.tensor_tensor(out=bneg[:], in0=bneg[:], in1=ccol[:], op=ALU.mult))
            v.tensor_scalar(out=bneg[:], in0=bneg[:], scalar1=-1.0 / float(H),
                            scalar2=None, op0=ALU.mult)
            v.drain()
            v.tensor_tensor(out=t1s[:], in0=esb[:], in1=LNB[:], op=ALU.mult)
            v.tensor_scalar(out=t1s[:], in0=t1s[:], scalar1=wcol[:], scalar2=None,
                            op0=ALU.mult)
            v.drain()
            ins = v.scalar_tensor_tensor(out=dyb[:], in0=yblk[:], scalar=bneg[:],
                                         in1=t1s[:], op0=ALU.mult, op1=ALU.add)
            c.inc(ins, "v")
        ew_done = c.cnt["v"]
        ew_done_prev = ew_done
        qw_v = ew_done

        @block.tensor
        def _(t):
            c.w(t, "v", qw_v)
            ins = None
            for nh in range(2):
                ins = t.matmul(P[1][0:1, nh * 512:nh * 512 + 512], st.ones_c[:],
                               Qw[:, nh * 512:nh * 512 + 512], start=True, stop=True)
            c.inc(ins, "t")
        gln_mm = c.cnt["t"]

        @block.vector
        def _(v):
            c.w(v, "t", gln_mm)
            ins = v.tensor_copy(out=nrow[0:1, 0:1024], in_=P[1][0:1, :])
            c.inc(ins, "v")
        gln_ev = c.cnt["v"]

        @block.gpsimd
        def _(g):
            c.w(g, "v", ew_done)
            c.dma(g, "g", d["dy_in"][:, :], dyb[:])
            c.w(g, "g")
            ins = g.collective_compute("AllGather", ALU.bypass, RG,
                                       ins=[d["dy_in"].opt()],
                                       outs=[d["dy_all"].opt()])
            c.inc(ins, "g", 1)
        ag_dy = c.cnt["g"]

        @block.sync
        def _(sy):
            c.w(sy, "g", ag_dy)
            c.dma(sy, "q", dy_tok[:].rearrange("p (k h) -> p k h", k=8),
                  d["dy_all"][:, :].rearrange("(k p) h -> p k h", p=128))
        dytok_rdy = c.cnt["q"]

        @block.tensor
        def _(t):
            c.w(t, "q", dytok_rdy)
            c.w(t, "s", hU_tok_rdy)
            c.w(t, "v", gln_ev)
            for jm in range(2):
                ins = None
                for nh in range(2):
                    for k in range(8):
                        ins = t.matmul(
                            P[jm][:, nh * 512:nh * 512 + 512],
                            h_tok[:, k * JS + jm * 128:k * JS + jm * 128 + 128],
                            dy_tok[:, k * H + nh * 512:k * H + nh * 512 + 512],
                            start=(k == 0), stop=(k == 7))
                c.inc(ins, "t")
        m3_done = c.cnt["t"]

        @block.scalar
        def _(sc):
            c.w(sc, "t", m3_done)
            for jm in range(2):
                ins = sc.activation(out=dW0sb[:, 0:H], in_=P[jm][:], func=AF.Square,
                                    accum_out=nrmc[:, jm:jm + 1])
            c.inc(ins, "c")
        n1_done = c.cnt["c"]

        @block.vector
        def _(v):
            c.w(v, "c", n1_done)
            for jm in range(2):
                ins = v.tensor_copy(out=dW1sb[:, jm * H:(jm + 1) * H], in_=P[jm][:])
            c.inc(ins, "v")
        dw1_ev = c.cnt["v"]

        df_rdy = []
        for khc in range(0, 8, 2):
            @block.sync
            def _(sy):
                c.w(sy, "g", ag_dy)
                if khc >= 2:
                    c.w(sy, "t")
                for kk in range(2):
                    c.tr(sy, "q", df2[:, kk * T:(kk + 1) * T],
                         d["dy_all"][:, (khc + kk) * 128:(khc + kk) * 128 + 128])
            df_rdy.append(c.cnt["q"])

            @block.tensor
            def _(t):
                c.w(t, "q", df_rdy[-1])
                if khc == 0:
                    c.w(t, "q", w1hc_ready)
                    c.w(t, "v", dw1_ev)
                ins = None
                for kk in range(2):
                    kh = khc + kk
                    for tm in range(8):
                        pq = P[2 + tm // 4]
                        ins = t.matmul(
                            pq[:, (tm % 4) * 256:(tm % 4) * 256 + 256],
                            df2[:, kk * T + tm * 128:kk * T + tm * 128 + 128],
                            w1hc_bf[:, kh * JS:(kh + 1) * JS],
                            start=(kh == 0), stop=(kh == 7))
                c.inc(ins, "t")
        m4_done = c.cnt["t"]

        @block.vector
        def _(v):
            c.w(v, "t", m4_done)
            c.w(v, "s", hU_tok_rdy)
            ins = None
            for tm in range(8):
                pq = P[2 + tm // 4]
                ins = v.tensor_tensor(out=dPre[:, tm * JS:(tm + 1) * JS],
                                      in0=pq[:, (tm % 4) * 256:(tm % 4) * 256 + 256],
                                      in1=U_tok[:, tm * JS:(tm + 1) * JS], op=ALU.mult)
            c.inc(ins, "v")
        dpre_done = c.cnt["v"]

        @block.tensor
        def _(t):
            c.w(t, "v", dpre_done)
            c.w(t, "q", kt_rdy)
            for hm in range(8):
                pq = P[hm // 4]
                ins = None
                for k in range(8):
                    ins = t.matmul(
                        pq[:, (hm % 4) * 256:(hm % 4) * 256 + 256],
                        kt[:, k * H + hm * 128:k * H + hm * 128 + 128],
                        dPre[:, k * JS:(k + 1) * JS],
                        start=(k == 0), stop=(k == 7))
                c.inc(ins, "t")
        m5_done = c.cnt["t"]
        m5_done_prev = m5_done

        @block.scalar
        def _(sc):
            c.w(sc, "t", m5_done)
            for hm in range(8):
                pq = P[hm // 4]
                ins = sc.activation(out=esb[:, 0:256],
                                    in_=pq[:, (hm % 4) * 256:(hm % 4) * 256 + 256],
                                    func=AF.Square, accum_out=nrmc[:, 2 + hm:3 + hm])
            c.inc(ins, "c")
        n0_done = c.cnt["c"]

        @block.vector
        def _(v):
            c.w(v, "c", n0_done)
            for hm in range(8):
                pq = P[hm // 4]
                ins = v.tensor_copy(out=dW0sb[:, hm * JS:(hm + 1) * JS],
                                    in_=pq[:, (hm % 4) * 256:(hm % 4) * 256 + 256])
            ins = v.tensor_copy(out=nrmb[:, 0:10], in_=nrmc[:, 0:10])
            c.inc(ins, "v")
        dw0_ev = c.cnt["v"]

        if ch + 1 < NCH:
            kt_rdy_next = emit_kt(ch + 1, m5_done)

        @block.tensor
        def _(t):
            c.w(t, "v", dw0_ev)
            ins = t.matmul(P[2][0:1, 0:10], st.ones_c[:], nrmb[:, 0:10],
                           start=True, stop=True)
            c.inc(ins, "t")
        nsum_mm = c.cnt["t"]

        @block.vector
        def _(v):
            c.w(v, "t", nsum_mm)
            ins = v.reduce_sum(out=nrow[0:1, 1024:1025], in_=P[2][0:1, 0:10],
                               axis=mybir.AxisListType.X)
            c.inc(ins, "v")
        nrow_done = c.cnt["v"]

        @block.vector
        def _(v):
            bcol = st.BT[:, ch:ch + 1]
            ecol = st.ET[:, ch:ch + 1]
            v.tensor_scalar(out=s0f[:], in0=s0f[:], scalar1=ecol, scalar2=None,
                            op0=ALU.mult)
            v.tensor_scalar(out=s1j[:], in0=s1j[:], scalar1=ecol, scalar2=None,
                            op0=ALU.mult)
            v.drain()
            v.scalar_tensor_tensor(out=w0f[:], in0=w0f[:], scalar=bcol, in1=s0f[:],
                                   op0=ALU.mult, op1=ALU.add)
            v.scalar_tensor_tensor(out=w1j[:], in0=w1j[:], scalar=bcol, in1=s1j[:],
                                   op0=ALU.mult, op1=ALU.add)
            v.drain()
            ins = v.tensor_copy(out=w0f_bf[:], in_=w0f[:])
            c.inc(ins, "v")
        upre_done = c.cnt["v"]

        @block.gpsimd
        def _(g):
            c.w(g, "v", nrow_done)
            c.dma(g, "g", d["nr_in"][:, :], nrow[:])
            c.w(g, "g")
            ins = g.collective_compute("AllReduce", ALU.add, RG,
                                       ins=[d["nr_in"].opt()],
                                       outs=[d["nr_out"].opt()])
            c.inc(ins, "g", 1)
            c.w(g, "g")
            c.dma(g, "g", n2row[:], d["nr_out"][:, :])
        ar2_done = c.cnt["g"]

        @block.gpsimd
        def _(g):
            r0w = ch * (H // NCH)
            r0d = ch * (KDE // NCH)
            c.dma(g, "g", d["wgB"][r0w:r0w + H // NCH, :], d["wgS"][r0w:r0w + H // NCH, :])
            c.dma(g, "g", d["wuB"][r0w:r0w + H // NCH, :], d["wuS"][r0w:r0w + H // NCH, :])
            c.dma(g, "g", d["wdB"][r0d:r0d + KDE // NCH, :], d["wdS"][r0d:r0d + KDE // NCH, :])

        if ch + 1 < NCH:
            @block.tensor
            def _(t):
                c.w(t, "v", upre_done)
                c.w(t, "q", kf_rdy_next)
                for jm in range(2):
                    ins = None
                    for nh in range(2):
                        for k in range(8):
                            ins = t.matmul(
                                P[jm][:, nh * 512:nh * 512 + 512],
                                w0f_bf[:, k * JS + jm * 128:k * JS + jm * 128 + 128],
                                kf[1 - buf][:, k * T + nh * 512:k * T + nh * 512 + 512],
                                start=(k == 0), stop=False, skip_group_check=True)
                    c.inc(ins, "t")

        @block.scalar
        def _(sc):
            c.w(sc, "g", ar2_done)
            ins = sc.activation(out=esb[0:1, 0:1024], in_=n2row[0:1, 0:1024],
                                func=AF.Square, accum_out=n2row[0:1, 1025:1026])
            c.inc(ins, "c")
        glnsq = c.cnt["c"]

        @block.vector
        def _(v):
            c.w(v, "c", glnsq)
            ins = v.tensor_tensor(out=n2row[0:1, 1025:1026], in0=n2row[0:1, 1025:1026],
                                  in1=n2row[0:1, 1024:1025], op=ALU.add)
            c.inc(ins, "v")

        @block.scalar
        def _(sc):
            c.w(sc, "v")
            ins = sc.activation(out=n2row[0:1, 1025:1026], in_=n2row[0:1, 1025:1026],
                                func=AF.Sqrt)
            c.inc(ins, "c")
        tot_done = c.cnt["c"]

        @block.vector
        def _(v):
            c.w(v, "c", tot_done)
            v.tensor_scalar(out=n2row[0:1, 1025:1026], in0=n2row[0:1, 1025:1026],
                            scalar1=1e-6, scalar2=None, op0=ALU.add)
            ins = v.reciprocal(out=n2row[0:1, 1024:1025], in_=n2row[0:1, 1025:1026])
            c.vsw(v, ins)
            c.vsw(v, v.tensor_scalar(out=n2row[0:1, 1024:1025], in0=n2row[0:1, 1024:1025],
                                     scalar1=1.0, scalar2=None, op0=ALU.min))
            v.tensor_copy(out=clipb[0:1, 0:1], in_=n2row[0:1, 1024:1025])
            ins = v.tensor_scalar(out=clipb[0:1, 1:2], in0=n2row[0:1, 1024:1025],
                                  scalar1=-1.0, scalar2=None, op0=ALU.mult)
            c.inc(ins, "v")
        clip_done = c.cnt["v"]

        @block.tensor
        def _(t):
            c.w(t, "v", clip_done)
            ins = t.matmul(P[2][:, 0:2], st.ones_r[:], clipb[0:1, 0:2],
                           start=True, stop=True)
            c.inc(ins, "t")
        clip_mm = c.cnt["t"]

        @block.vector
        def _(v):
            c.w(v, "t", clip_mm)
            v.tensor_copy(out=ncol[:], in_=P[2][:, 1:2])
            v.drain()

            ins = v.tensor_scalar(out=dPre[:], in0=dW0sb[:], scalar1=ncol[:],
                                  scalar2=None, op0=ALU.mult)
            c.inc(ins, "v")
            gbf_done_box[0] = c.cnt["v"]
            v.scalar_tensor_tensor(out=w1j[:], in0=dW1sb[:], scalar=ncol[:],
                                   in1=w1j[:], op0=ALU.mult, op1=ALU.add)
            v.drain()
            v.tensor_copy(out=w1j_bf[:], in_=w1j[:])
            v.scalar_tensor_tensor(out=s0f[:], in0=dW0sb[:], scalar=ncol[:],
                                   in1=s0f[:], op0=ALU.mult, op1=ALU.add)
            v.scalar_tensor_tensor(out=s1j[:], in0=dW1sb[:], scalar=ncol[:],
                                   in1=s1j[:], op0=ALU.mult, op1=ALU.add)
            v.scalar_tensor_tensor(out=w0f[:], in0=dW0sb[:], scalar=ncol[:],
                                   in1=w0f[:], op0=ALU.mult, op1=ALU.add)
            brow = st.BErow[0:1, ch:ch + 1]
            erow = st.BErow[0:1, 32 + ch:33 + ch]
            ins = v.tensor_scalar(out=sln_row[:], in0=sln_row[:], scalar1=erow,
                                  scalar2=None, op0=ALU.mult)
            c.vsw(v, ins)
            ins = v.scalar_tensor_tensor(out=sln_row[:], in0=n2row[0:1, 0:1024],
                                         scalar=n2row[0:1, 1024:1025], in1=sln_row[:],
                                         op0=ALU.mult, op1=ALU.subtract)
            c.vsw(v, ins)
            ins = v.tensor_scalar(out=sln_row[:], in0=sln_row[:], scalar1=-1.0,
                                  scalar2=None, op0=ALU.mult)
            c.vsw(v, ins)
            ins = v.tensor_scalar(out=ln_row[:], in0=ln_row[:], scalar1=brow,
                                  scalar2=None, op0=ALU.mult)
            c.vsw(v, ins)
            ins = v.tensor_tensor(out=ln_row[:], in0=ln_row[:], in1=sln_row[:], op=ALU.add)
            c.vsw(v, ins)
            ins = v.tensor_copy(out=ln_bfr[:], in_=ln_row[:])
            c.inc(ins, "v")
        upd_done = c.cnt["v"]
        gbf_done = gbf_done_box[0]

        if ch + 1 < NCH:
            w1hc_ready = emit_w1hc(upd_done)
            kf_rdy = kf_rdy_next
            kt_rdy = kt_rdy_next
            db_rdy = db_rdy_next

        if dbg.get("scan_ch") == ch:
            @block.gpsimd
            def _(g):
                c.w(g, "v", upd_done)
                c.dma(g, "g", st.dbg_ap["w0f"], w0f[:])
                c.dma(g, "g", st.dbg_ap["w1j"], w1j[:])
                c.dma(g, "g", st.dbg_ap["lnrow"], ln_row[:])
                c.dma(g, "g", st.dbg_ap["yblk"], yblk[:])
                c.dma(g, "g", st.dbg_ap["dyall"], d["dy_all"][:, :])
                c.dma(g, "g", st.dbg_ap["n2row"], n2row[:])
                c.dma(g, "g", st.dbg_ap["GS"], st.GS[:])
                c.dma(g, "g", st.dbg_ap["Qw"], Qw[:])
                c.dma(g, "g", st.dbg_ap["ccol"], ccol[:])
                c.dma(g, "g", st.dbg_ap["esb"], esb[:])
                c.dma(g, "g", st.dbg_ap["t1s"], t1s[:])
                c.dma(g, "g", st.dbg_ap["wcol"], wcol[:])
                c.dma(g, "g", st.dbg_ap["nrow"], nrow[:])
                c.dma(g, "g", st.dbg_ap["nri"], d["nr_in"][:, :])
                c.dma(g, "g", st.dbg_ap["nro"], d["nr_out"][:, :])

    st.upd_done = upd_done
    st.wcast_done = c.cnt["g"]




def final_attn(st):
    nc, c, block, d = st.nc, st.c, st.block, st.dram
    sb = st.sb
    P = st.P

    @block.gpsimd
    def _(g):
        c.w(g, "v", st.upd_done)
        c.dma(g, "g", d["w0_in"][:, :].rearrange("(k p) j -> p k j", p=128),
              st.w0f_bf[:].rearrange("p (k j) -> p k j", k=8))
        c.dma(g, "g", d["w1_in"][:, :].rearrange("(k p) h -> p k h", p=128),
              st.w1j_bf[:].rearrange("p (k h) -> p k h", k=2))
        c.w(g, "g")
        ins = g.collective_compute("AllGather", ALU.bypass, RG,
                                   ins=[d["w0_in"].opt()], outs=[d["w0_all"].opt()])
        c.inc(ins, "g", 1)
        ins = g.collective_compute("AllGather", ALU.bypass, RG,
                                   ins=[d["w1_in"].opt()], outs=[d["w1_all"].opt()])
        c.inc(ins, "g", 1)
    mem_ag = c.cnt["g"]

    w0a = sb("w0a", [128, 8 * IN])
    w1a = sb("w1a", [128, 16 * H])
    qt = sb("qt", [128, 8 * 512])
    h2 = sb("h2", [128, 16 * 512])
    y2sb = sb("y2sb", [128, 8 * 512])
    zb = sb("zb", [128, 512])
    sqb = sb("sqb", [128, 512])
    srow = sb("srow", [1, 1024], F32)
    srow_bf = sb("srow_bf", [1, 1024])
    RB2 = sb("RB2", [128, 512])
    MB2 = sb("MB2", [128, 512])
    lnq = sb("lnq", [128, 8], F32)
    lnbf32 = sb("lnbf32", [128, 1024], F32)
    xstr = sb("xstr", [128, 512], F32)
    vrow = sb("vrow", [1, 512], F32)
    zfp = sb("zfp", [128, 8 * 512], F32)
    x1o = sb("x1o", [128, 512], F32)
    x1b2 = sb("x1b2", [128, 512])
    x1l2 = sb("x1l2", [128, 512])

    @block.sync
    def _(sy):
        c.w(sy, "g", mem_ag)
        for rr in range(8):
            c.dma(sy, "q",
                  w0a[:].rearrange("p (k r j) -> p k r j", k=8, r=8)[:, :, rr, :],
                  d["w0_all"][rr * H:(rr + 1) * H, :]
                  .rearrange("(k p) j -> p k j", p=128))
        c.dma(sy, "q", w1a[:].rearrange("p (k h) -> p k h", k=16),
              d["w1_all"].rearrange("(k p) h -> p k h", p=128))
    mem_ld = c.cnt["q"]

    @block.tensor
    def _(t):
        c.w(t, "v", st.upd_done)
        ins = None
        for nh in range(2):
            ins = t.matmul(P[0][:, nh * 512:nh * 512 + 512], st.ones_r[:],
                           st.ln_bfr[0:1, nh * 512:nh * 512 + 512], start=True, stop=True)
        c.inc(ins, "t")

    @block.vector
    def _(v):
        c.w(v, "t")
        ins = v.tensor_copy(out=lnbf32[:], in_=P[0][:])
        c.inc(ins, "v")
    lnb32_done = c.cnt["v"]

    @block.tensor
    def _(t):
        c.w(t, "v", lnb32_done)
        ins = None
        for m in range(8):
            ins = t.transpose(P[1][:, m * 128:(m + 1) * 128],
                              lnbf32[:, m * 128:(m + 1) * 128], st.eye_f32[:])
        c.inc(ins, "t")
    lnq_mm = c.cnt["t"]

    @block.vector
    def _(v):
        c.w(v, "t", lnq_mm)
        ins = None
        for m in range(8):
            ins = v.tensor_copy(out=lnq[:, m:m + 1], in_=P[1][:, m * 128:m * 128 + 1])
        c.inc(ins, "v")
    lnq_done = c.cnt["v"]

    for tt in range(8):
        bb = tt // 4

        @block.sync
        def _(sy):
            c.w(sy, "q", mem_ld)
            if tt > 0:
                c.w(sy, "v")
            c.dma(sy, "q", qt[:].rearrange("p (k t) -> p k t", k=8),
                  d["q_dram"][:, tt * 512:(tt + 1) * 512]
                  .rearrange("(k p) t -> p k t", p=128))
        q_ld = c.cnt["q"]

        for im in range(16):
            qtr = [P[0], P[0], P[1], P[1]][im % 4]
            half = (im % 2) * 512

            @block.tensor
            def _(t):
                if im == 0:
                    c.w(t, "q", q_ld)
                    c.w(t, "v", lnq_done)
                if im >= 4:
                    c.w(t, "c")
                ins = None
                for k in range(8):
                    ins = t.matmul(
                        qtr[:, half:half + 512],
                        w0a[:, k * IN + im * 128:k * IN + im * 128 + 128],
                        qt[:, k * 512:(k + 1) * 512],
                        start=(k == 0), stop=(k == 7))
                c.inc(ins, "t")
            mm_done = c.cnt["t"]

            @block.scalar
            def _(sc):
                c.w(sc, "t", mm_done)
                ins = sc.activation(out=h2[:, im * 512:(im + 1) * 512],
                                    in_=qtr[:, half:half + 512], func=AF.Silu)
                c.inc(ins, "c")
        h2_done = c.cnt["c"]

        for hm in range(8):
            half = (hm % 2) * 512

            @block.tensor
            def _(t):
                if hm == 0:
                    c.w(t, "c", h2_done)
                if hm >= 2:
                    c.w(t, "v")
                    c.w(t, "c")
                ins = None
                for k in range(16):
                    ins = t.matmul(
                        P[2][:, half:half + 512],
                        w1a[:, k * H + hm * 128:k * H + hm * 128 + 128],
                        h2[:, k * 512:(k + 1) * 512],
                        start=(k == 0), stop=(k == 15))
                c.inc(ins, "t")
            mm_done = c.cnt["t"]

            @block.scalar
            def _(sc):
                c.w(sc, "t", mm_done)
                c.w(sc, "t")
                ins = sc.activation(out=sqb[:], in_=P[2][:, half:half + 512],
                                    func=AF.Square)
                c.inc(ins, "c")
            sq_done = c.cnt["c"]

            @block.vector
            def _(v):
                c.w(v, "t", mm_done)
                ins = v.tensor_copy(out=y2sb[:, hm * 512:(hm + 1) * 512],
                                    in_=P[2][:, half:half + 512])
                c.inc(ins, "v")

            @block.tensor
            def _(t):
                c.w(t, "c", sq_done)
                ins = t.matmul(P[3][0:1, 0:512], st.ones_c[:], sqb[:],
                               start=(hm == 0), stop=(hm == 7), skip_group_check=True)
                c.inc(ins, "t")
        ss_mm = c.cnt["t"]

        @block.vector
        def _(v):
            c.w(v, "t", ss_mm)
            ins = v.tensor_scalar(out=srow[0:1, 0:512], in0=P[3][0:1, 0:512],
                                  scalar1=1.0 / float(H), scalar2=EPS,
                                  op0=ALU.mult, op1=ALU.add)
            c.inc(ins, "v")

        @block.scalar
        def _(sc):
            c.w(sc, "v")
            ins = sc.activation(out=srow[0:1, 0:512], in_=srow[0:1, 0:512],
                                func=AF.Sqrt)
            c.inc(ins, "c")

        @block.vector
        def _(v):
            c.w(v, "c")
            v.reciprocal(out=srow[0:1, 512:1024], in_=srow[0:1, 0:512])
            v.drain()
            ins = v.tensor_copy(out=srow_bf[0:1, 0:512], in_=srow[0:1, 512:1024])
            c.inc(ins, "v")

        @block.tensor
        def _(t):
            c.w(t, "v")
            ins = t.matmul(P[3][:, 512:1024], st.ones_r[:], srow_bf[0:1, 0:512],
                           start=True, stop=True)
            c.inc(ins, "t")

        @block.vector
        def _(v):
            c.w(v, "t")
            ins = v.tensor_copy(out=RB2[:], in_=P[3][:, 512:1024])
            c.inc(ins, "v")
        rb2_done = c.cnt["v"]

        for hm in range(8):
            @block.sync
            def _(sy):
                c.w(sy, "v", rb2_done if hm == 0 else c.cnt["v"])
                c.dma(sy, "q", xstr[:],
                      d["xfb"][bb * H + hm * 128:bb * H + hm * 128 + 128,
                               (tt % 4) * 512:(tt % 4) * 512 + 512])
            x_ld = c.cnt["q"]

            @block.vector
            def _(v):
                c.w(v, "q", x_ld)
                sl = slice(hm * 512, (hm + 1) * 512)
                v.scalar_tensor_tensor(out=zfp[:, sl], in0=y2sb[:, sl],
                                       scalar=lnq[:, hm:hm + 1], in1=RB2[:],
                                       op0=ALU.mult, op1=ALU.mult)
                v.tensor_tensor(out=zfp[:, sl], in0=zfp[:, sl], in1=qt[:, sl],
                                op=ALU.add)
                ins = v.tensor_tensor(out=zfp[:, sl], in0=zfp[:, sl], in1=xstr[:],
                                      op=ALU.add)
                c.inc(ins, "v")
            z_done = c.cnt["v"]

            @block.scalar
            def _(sc):
                c.w(sc, "v", z_done)
                c.w(sc, "t")
                sl = slice(hm * 512, (hm + 1) * 512)
                sc.activation(out=zb[:], in_=zfp[:, sl], func=AF.Identity)
                ins = sc.activation(out=sqb[:], in_=zfp[:, sl], func=AF.Square)
                c.inc(ins, "c")
            zb_done = c.cnt["c"]

            @block.tensor
            def _(t):
                c.w(t, "c", zb_done)
                t.matmul(P[0][0:1, 0:512], st.ones_c[:], zb[:],
                         start=(hm == 0), stop=(hm == 7), skip_group_check=True)
                ins = t.matmul(P[0][0:1, 512:1024], st.ones_c[:], sqb[:],
                               start=(hm == 0), stop=(hm == 7), skip_group_check=True)
                c.inc(ins, "t")
        st_mm = c.cnt["t"]

        @block.vector
        def _(v):
            c.w(v, "t", st_mm)
            v.tensor_scalar(out=srow[0:1, 0:512], in0=P[0][0:1, 0:512],
                            scalar1=1.0 / float(H), scalar2=None, op0=ALU.mult)
            ins = v.tensor_scalar(out=srow[0:1, 512:1024], in0=P[0][0:1, 512:1024],
                                  scalar1=1.0 / float(H), scalar2=None, op0=ALU.mult)
            c.vsw(v, ins)
            ins = v.tensor_tensor(out=vrow[0:1, 0:512], in0=srow[0:1, 0:512],
                                  in1=srow[0:1, 0:512], op=ALU.mult)
            c.vsw(v, ins)
            ins = v.tensor_tensor(out=vrow[0:1, 0:512], in0=srow[0:1, 512:1024],
                                  in1=vrow[0:1, 0:512], op=ALU.subtract)
            c.inc(ins, "v")
        var_done = c.cnt["v"]

        @block.vector
        def _(v):
            c.w(v, "v", var_done)
            ins = v.tensor_scalar(out=vrow[0:1, 0:512], in0=vrow[0:1, 0:512],
                                  scalar1=LNEPS, scalar2=None, op0=ALU.add)
            c.inc(ins, "v")

        @block.scalar
        def _(sc):
            c.w(sc, "v")
            ins = sc.activation(out=vrow[0:1, 0:512], in_=vrow[0:1, 0:512],
                                func=AF.Sqrt)
            c.inc(ins, "c")

        @block.vector
        def _(v):
            c.w(v, "c")
            v.reciprocal(out=srow[0:1, 512:1024], in_=vrow[0:1, 0:512])
            v.drain()
            v.tensor_copy(out=srow_bf[0:1, 0:512], in_=srow[0:1, 0:512])
            ins = v.tensor_copy(out=srow_bf[0:1, 512:1024], in_=srow[0:1, 512:1024])
            c.inc(ins, "v")
        rows_done = c.cnt["v"]

        @block.tensor
        def _(t):
            c.w(t, "v", rows_done)
            t.matmul(P[1][:, 0:512], st.ones_r[:], srow_bf[0:1, 0:512],
                     start=True, stop=True)
            ins = t.matmul(P[1][:, 512:1024], st.ones_r[:], srow_bf[0:1, 512:1024],
                           start=True, stop=True)
            c.inc(ins, "t")
        bc_mm = c.cnt["t"]

        @block.vector
        def _(v):
            c.w(v, "t", bc_mm)
            v.tensor_copy(out=MB2[:], in_=P[1][:, 0:512])
            ins = v.tensor_copy(out=RB2[:], in_=P[1][:, 512:1024])
            c.inc(ins, "v")
        bc_done = c.cnt["v"]

        for hm in range(8):
            @block.vector
            def _(v):
                if hm == 0:
                    c.w(v, "v", bc_done)
                c.w(v, "q")
                sl = slice(hm * 512, (hm + 1) * 512)
                v.tensor_tensor(out=x1o[:], in0=zfp[:, sl], in1=MB2[:], op=ALU.subtract)
                ins = v.tensor_tensor(out=x1o[:], in0=x1o[:], in1=RB2[:], op=ALU.mult)
                c.vsw(v, ins)
                v.tensor_copy(out=x1b2[:], in_=x1o[:])
                v.drain()
                ins = v.tensor_tensor(out=x1l2[:], in0=x1o[:], in1=x1b2[:],
                                      op=ALU.subtract)
                c.inc(ins, "v")
            x1_done = c.cnt["v"]

            @block.sync
            def _(sy):
                c.w(sy, "v", x1_done)
                c.dma(sy, "q",
                      d["x1scr"][hm * 128:(hm + 1) * 128,
                                 tt * 512:(tt + 1) * 512], x1o[:])
                c.dma(sy, "q",
                      d["x1bfd"][hm * 128:(hm + 1) * 128,
                                 tt * 512:(tt + 1) * 512], x1b2[:])
                c.dma(sy, "q",
                      d["x1lod"][hm * 128:(hm + 1) * 128,
                                 tt * 512:(tt + 1) * 512], x1l2[:])
    st.x1_stores = c.cnt["q"]

    if st.dbg.get("x1"):
        @block.gpsimd
        def _(g):
            c.w(g, "q", st.x1_stores)
            c.dma(g, "g", st.dbg_ap["x1"], d["x1scr"][:, :])


def ffn(st):
    nc, c, block, d = st.nc, st.c, st.block, st.dram
    sb = st.sb
    P = st.P

    x1bf = sb("x1bf", [128, 8 * 1024])
    xlo = sb("xlo", [128, 8 * 1024])
    ghi = sb("ghi", [128, 8 * 16])
    glo = sb("glo", [128, 8 * 16])
    g32 = sb("g32", [128, 8 * 16], F32)
    brows = sb("brows", [1, 48])
    b32 = sb("b32", [1, 16], F32)
    rsigall = sb("rsigall", [128, 128], F32)
    rtmp = sb("rtmp", [128, 16], F32)
    eq1 = sb("eq1", [128, 16], F32)
    eq2 = sb("eq2", [128, 16], F32)
    m1c = sb("m1c", [128, 1], F32)
    m2c = sb("m2c", [128, 1], F32)
    mRc = sb("mRc", [128, 1], F32)
    wts = sb("wts", [128, 8 * NE], F32)
    wg_sb = [sb(f"wg_sb{i}", [128, 8 * 512]) for i in range(2)]
    wu_sb = [sb(f"wu_sb{i}", [128, 8 * 512]) for i in range(2)]
    sgl = sb("sgl", [128, 512])
    huw = sb("huw", [128, 512])
    act_sb = sb("act_sb", [128, 8 * 512])
    wd_t = [sb(f"wd_t{i}", [128, 1024]) for i in range(2)]
    ac_t = [sb(f"ac_t{i}", [128, 512], is_transpose_dest=True) for i in range(2)]
    zfp2 = sb("zfp2", [128, 8 * 512], F32)
    zb2 = sb("zb2", [128, 512])
    sqb2 = sb("sqb2", [128, 512])
    sr2 = sb("sr2", [1, 1024], F32)
    sr2b = sb("sr2b", [1, 1024])
    vr2 = sb("vr2", [1, 512], F32)
    MB3 = sb("MB3", [128, 512])
    RB3 = sb("RB3", [128, 512])
    xfp3 = sb("xfp3", [128, 512], F32)
    outt = sb("outt", [128, 512], F32)

    @block.gpsimd
    def _(g):
        c.w(g, "q", st.x1_stores)
        c.dma(g, "g", ghi[:].rearrange("p (k e) -> p k e", k=8), d["gwH"].rearrange("(k p) e -> p k e", p=128))
        c.dma(g, "g", g32[:].rearrange("p (k e) -> p k e", k=8), d["gwH"].rearrange("(k p) e -> p k e", p=128))
        c.dma(g, "g", brows[0:1, 0:16], d["gbH"])
        c.dma(g, "g", b32[:], d["gbH"])
    g_ld = c.cnt["g"]

    @block.vector
    def _(v):
        c.w(v, "g", g_ld)
        v.tensor_tensor(out=glo[:], in0=g32[:], in1=ghi[:], op=ALU.subtract)
        ins = v.scalar_tensor_tensor(out=brows[0:1, 16:32], in0=brows[0:1, 0:16],
                                     scalar=-1.0, in1=b32[:],
                                     op0=ALU.mult, op1=ALU.add)
        c.inc(ins, "v")
    glo_done = c.cnt["v"]

    for cc in range(4):
        col0 = cc * 1024

        @block.sync
        def _(sy):
            c.w(sy, "q", st.x1_stores)
            if cc > 0:
                c.w(sy, "t")
            c.dma(sy, "q", x1bf[:].rearrange("p (k t) -> p k t", k=8),
                  d["x1bfd"][:, col0:col0 + 1024].rearrange("(k p) t -> p k t", p=128))
            c.dma(sy, "q", xlo[:].rearrange("p (k t) -> p k t", k=8),
                  d["x1lod"][:, col0:col0 + 1024].rearrange("(k p) t -> p k t", p=128))
        xlo_done = c.cnt["q"]

        @block.vector
        def _(v):
            ins = v.memset(wts[:], 1.0)
            c.inc(ins, "v")
        wts_init = c.cnt["v"]

        for tm in range(8):
            @block.tensor
            def _(t):
                if tm == 0:
                    c.w(t, "q", xlo_done)
                    c.w(t, "v", wts_init)
                first = True
                for k in range(8):
                    t.matmul(P[2][:, tm * 16:tm * 16 + 16],
                             x1bf[:, k * 1024 + tm * 128:k * 1024 + tm * 128 + 128],
                             ghi[:, k * 16:(k + 1) * 16], start=first, stop=False)
                    first = False
                    t.matmul(P[2][:, tm * 16:tm * 16 + 16],
                             x1bf[:, k * 1024 + tm * 128:k * 1024 + tm * 128 + 128],
                             glo[:, k * 16:(k + 1) * 16], start=False, stop=False)
                    t.matmul(P[2][:, tm * 16:tm * 16 + 16],
                             xlo[:, k * 1024 + tm * 128:k * 1024 + tm * 128 + 128],
                             ghi[:, k * 16:(k + 1) * 16], start=False, stop=False)
                t.matmul(P[2][:, tm * 16:tm * 16 + 16], st.ones_r[:], brows[0:1, 0:16],
                         start=False, stop=False)
                ins = t.matmul(P[2][:, tm * 16:tm * 16 + 16], st.ones_r[:],
                               brows[0:1, 16:32], start=False, stop=(True))
                if tm == 7:
                    c.inc(ins, "t")
        r_mm = c.cnt["t"]

        @block.scalar
        def _(sc):
            c.w(sc, "t", r_mm)
            ins = sc.activation(out=rsigall[:], in_=P[2][:, 0:128], func=AF.Sigmoid)
            c.inc(ins, "c")
        sig_done = c.cnt["c"]

        for tm in range(8):
            @block.vector
            def _(v):
                if tm == 0:
                    c.w(v, "c", sig_done)
                rsig_t = rsigall[:, tm * 16:(tm + 1) * 16]
                v.reduce_max(out=m1c[:], in_=rsig_t, axis=mybir.AxisListType.X)
                v.drain()
                v.tensor_scalar(out=eq1[:], in0=rsig_t, scalar1=m1c[:], scalar2=None,
                                op0=ALU.is_equal)
                v.drain()
                v.scalar_tensor_tensor(out=rtmp[:], in0=eq1[:], scalar=-1e30,
                                       in1=rsig_t, op0=ALU.mult, op1=ALU.add)
                v.drain()
                v.reduce_max(out=m2c[:], in_=rtmp[:], axis=mybir.AxisListType.X)
                v.drain()
                v.tensor_scalar(out=eq2[:], in0=rtmp[:], scalar1=m2c[:],
                                scalar2=None, op0=ALU.is_equal)
                v.drain()
                v.tensor_tensor(out=eq1[:], in0=eq1[:], in1=eq2[:], op=ALU.add)
                v.tensor_tensor(out=m1c[:], in0=m1c[:], in1=m2c[:], op=ALU.add)
                v.drain()
                v.reciprocal(out=mRc[:], in_=m1c[:])
                v.drain()
                v.tensor_scalar(out=rtmp[:], in0=rsig_t, scalar1=mRc[:], scalar2=None,
                                op0=ALU.mult)
                v.drain()
                ins = v.tensor_tensor(out=wts[:, tm * NE:tm * NE + 16], in0=rtmp[:],
                                      in1=eq1[:], op=ALU.mult)
                c.inc(ins, "v")
        wts_done = c.cnt["v"]

        st_act = []
        up_hist_x = []
        for e in range(NE):
            ebuf = e % 2

            @block.sync
            def _(sy):
                if e >= 2:
                    c.w(sy, "t", up_hist_x[e - 1])
                c.dma(sy, "q", wg_sb[ebuf][:].rearrange("p (k i) -> p k i", k=8),
                      d["wgB"][:, e * 512:(e + 1) * 512]
                      .rearrange("(k p) i -> p k i", p=128))
                c.dma(sy, "q", wu_sb[ebuf][:].rearrange("p (k i) -> p k i", k=8),
                      d["wuB"][:, e * 512:(e + 1) * 512]
                      .rearrange("(k p) i -> p k i", p=128))
            w_ld = c.cnt["q"]

            for tm in range(8):
                qg = [P[0], P[2]][tm % 2]
                qu = [P[1], P[3]][tm % 2]
                half = ((tm // 2) % 2) * 512

                @block.tensor
                def _(t):
                    if tm == 0:
                        c.w(t, "q", w_ld)
                        c.w(t, "v", wts_done)
                        c.w(t, "g", st.wcast_done)
                    if tm >= 4:
                        c.w(t, "v")
                        c.w(t, "c")
                    ins = None
                    for k in range(8):
                        t.matmul(qg[:, half:half + 512],
                                 x1bf[:, k * 1024 + tm * 128:k * 1024 + tm * 128 + 128],
                                 wg_sb[ebuf][:, k * 512:(k + 1) * 512],
                                 start=(k == 0), stop=(k == 7))
                        ins = t.matmul(qu[:, half:half + 512],
                                       x1bf[:, k * 1024 + tm * 128:
                                            k * 1024 + tm * 128 + 128],
                                       wu_sb[ebuf][:, k * 512:(k + 1) * 512],
                                       start=(k == 0), stop=(k == 7))
                    c.inc(ins, "t")
                up_mm = c.cnt["t"]

                @block.scalar
                def _(sc):
                    c.w(sc, "t", up_mm)
                    if tm >= 1 or e >= 1:
                        c.w(sc, "v")
                    ins = sc.activation(out=sgl[:], in_=qg[:, half:half + 512],
                                        func=AF.Silu)
                    c.inc(ins, "c")
                sg_done = c.cnt["c"]

                @block.vector
                def _(v):
                    c.w(v, "c", sg_done)
                    if tm == 0:
                        c.w(v, "s")
                        v.drain()
                    v.tensor_scalar(out=huw[:], in0=qu[:, half:half + 512],
                                    scalar1=wts[:, tm * NE + e:tm * NE + e + 1],
                                    scalar2=None, op0=ALU.mult)
                    ins = v.tensor_tensor(out=act_sb[:, tm * 512:(tm + 1) * 512],
                                          in0=sgl[:], in1=huw[:], op=ALU.mult)
                    c.inc(ins, "v")
            a_done = c.cnt["v"]
            up_hist_x.append(c.cnt["t"])

            @block.scalar
            def _(sc):
                c.w(sc, "v", a_done)
                c.dma(sc, "s",
                      d["act_dram"][e * 1024:(e + 1) * 1024, :]
                      .rearrange("(tm p) i -> p tm i", p=128),
                      act_sb[:].rearrange("p (tm i) -> p tm i", tm=8))
            st_act.append(c.cnt["s"])

        for th in range(2):
            nk = KDE // 128
            ld_hist = []
            mm_hist = []
            for k in range(nk):
                kb = k % 2
                e_of_k = k // 4

                @block.sync
                def _(sy):
                    if k == 0:
                        c.w(sy, "s", st_act[-1])
                        c.w(sy, "t")
                    if k >= 2:
                        c.w(sy, "t", mm_hist[k - 2])
                    c.dma(sy, "q", wd_t[kb][:], d["wdB"][k * 128:(k + 1) * 128, :])
                ld_w = c.cnt["q"]

                @block.scalar
                def _(sc):
                    if k == 0:
                        c.w(sc, "s", st_act[-1])
                        c.w(sc, "t")
                    if k >= 2:
                        c.w(sc, "t", mm_hist[k - 2])
                    c.tr(sc, "s", ac_t[kb][:],
                         d["act_dram"][e_of_k * 1024 + th * 512:
                                       e_of_k * 1024 + th * 512 + 512,
                                       (k % 4) * 128:(k % 4) * 128 + 128])
                ld_hist.append((ld_w, c.cnt["s"]))

                @block.tensor
                def _(t):
                    c.w(t, "q", ld_hist[-1][0])
                    c.w(t, "s", ld_hist[-1][1])
                    ins = None
                    for m in range(8):
                        ins = t.matmul(P[m // 2][:, (m % 2) * 512:(m % 2) * 512 + 512],
                                       wd_t[kb][:, m * 128:(m + 1) * 128],
                                       ac_t[kb][:],
                                       start=(k == 0), stop=(k == nk - 1))
                    c.inc(ins, "t")
                mm_hist.append(c.cnt["t"])
            dn_mm = c.cnt["t"]

            for m in range(8):
                @block.sync
                def _(sy):
                    if m == 0:
                        c.w(sy, "t", dn_mm)
                    c.w(sy, "v")
                    c.dma(sy, "q", xfp3[:],
                          d["x1scr"][m * 128:(m + 1) * 128,
                                     col0 + th * 512:col0 + th * 512 + 512])
                xl = c.cnt["q"]

                @block.vector
                def _(v):
                    c.w(v, "q", xl)
                    ins = v.scalar_tensor_tensor(
                        out=zfp2[:, m * 512:(m + 1) * 512], in0=xfp3[:], scalar=2.0,
                        in1=P[m // 2][:, (m % 2) * 512:(m % 2) * 512 + 512],
                        op0=ALU.mult, op1=ALU.add)
                    c.inc(ins, "v")
            z2_done = c.cnt["v"]

            if cc == 0 and th == 0 and st.dbg.get("ffn0"):
                @block.gpsimd
                def _(g):
                    c.w(g, "v", z2_done)
                    c.dma(g, "g", st.dbg_ap["zfp2"], zfp2[:])

            st2_hist = []
            for m in range(8):
                @block.scalar
                def _(sc):
                    if m == 0:
                        c.w(sc, "v", z2_done)
                    else:
                        c.w(sc, "t", st2_hist[m - 1])
                    sl = slice(m * 512, (m + 1) * 512)
                    sc.activation(out=zb2[:], in_=zfp2[:, sl], func=AF.Identity)
                    ins = sc.activation(out=sqb2[:], in_=zfp2[:, sl], func=AF.Square)
                    c.inc(ins, "c")
                zb2_done = c.cnt["c"]

                @block.tensor
                def _(t):
                    c.w(t, "c", zb2_done)
                    t.matmul(P[0][0:1, 0:512], st.ones_c[:], zb2[:],
                             start=(m == 0), stop=(m == 7), skip_group_check=True)
                    ins = t.matmul(P[0][0:1, 512:1024], st.ones_c[:], sqb2[:],
                                   start=(m == 0), stop=(m == 7), skip_group_check=True)
                    c.inc(ins, "t")
                st2_hist.append(c.cnt["t"])
            st2_mm = c.cnt["t"]

            @block.vector
            def _(v):
                c.w(v, "t", st2_mm)
                v.tensor_scalar(out=sr2[0:1, 0:512], in0=P[0][0:1, 0:512],
                                scalar1=1.0 / float(H), scalar2=None, op0=ALU.mult)
                ins = v.tensor_scalar(out=sr2[0:1, 512:1024], in0=P[0][0:1, 512:1024],
                                      scalar1=1.0 / float(H), scalar2=None, op0=ALU.mult)
                c.vsw(v, ins)
                ins = v.tensor_tensor(out=vr2[0:1, 0:512], in0=sr2[0:1, 0:512],
                                      in1=sr2[0:1, 0:512], op=ALU.mult)
                c.vsw(v, ins)
                ins = v.tensor_tensor(out=vr2[0:1, 0:512], in0=sr2[0:1, 512:1024],
                                      in1=vr2[0:1, 0:512], op=ALU.subtract)
                c.inc(ins, "v")
            v2_done = c.cnt["v"]

            @block.vector
            def _(v):
                c.w(v, "v", v2_done)
                ins = v.tensor_scalar(out=vr2[0:1, 0:512], in0=vr2[0:1, 0:512],
                                      scalar1=LNEPS, scalar2=None, op0=ALU.add)
                c.inc(ins, "v")

            @block.scalar
            def _(sc):
                c.w(sc, "v")
                ins = sc.activation(out=vr2[0:1, 0:512], in_=vr2[0:1, 0:512],
                                    func=AF.Sqrt)
                c.inc(ins, "c")

            @block.vector
            def _(v):
                c.w(v, "c")
                v.reciprocal(out=sr2[0:1, 512:1024], in_=vr2[0:1, 0:512])
                v.drain()
                v.tensor_copy(out=sr2b[0:1, 0:512], in_=sr2[0:1, 0:512])
                ins = v.tensor_copy(out=sr2b[0:1, 512:1024], in_=sr2[0:1, 512:1024])
                c.inc(ins, "v")
            rows2 = c.cnt["v"]

            @block.tensor
            def _(t):
                c.w(t, "v", rows2)
                t.matmul(P[1][:, 0:512], st.ones_r[:], sr2b[0:1, 0:512],
                         start=True, stop=True)
                ins = t.matmul(P[1][:, 512:1024], st.ones_r[:], sr2b[0:1, 512:1024],
                               start=True, stop=True)
                c.inc(ins, "t")
            bc2_mm = c.cnt["t"]

            @block.vector
            def _(v):
                c.w(v, "t", bc2_mm)
                v.tensor_copy(out=MB3[:], in_=P[1][:, 0:512])
                ins = v.tensor_copy(out=RB3[:], in_=P[1][:, 512:1024])
                c.inc(ins, "v")
            bc2_done = c.cnt["v"]

            for m in range(8):
                @block.vector
                def _(v):
                    if m == 0:
                        c.w(v, "v", bc2_done)
                    c.w(v, "q")
                    sl = slice(m * 512, (m + 1) * 512)
                    v.tensor_tensor(out=outt[:], in0=zfp2[:, sl], in1=MB3[:],
                                    op=ALU.subtract)
                    ins = v.tensor_tensor(out=outt[:], in0=outt[:], in1=RB3[:],
                                          op=ALU.mult)
                    c.inc(ins, "v")
                o_done = c.cnt["v"]

                @block.sync
                def _(sy):
                    c.w(sy, "v", o_done)
                    c.dma(sy, "q",
                          d["yout"][m * 128:(m + 1) * 128,
                                    col0 + th * 512:col0 + th * 512 + 512], outt[:])

    @block.gpsimd
    def _(g):
        for sname in ("q", "g", "s", "v", "c", "t"):
            c.w(g, sname)


def build_nc(dbg=None):
    dbg = dict(dbg or {})
    nc = bass.Bass("TRN2", target_bir_lowering=False)
    d = {}

    def din(name, shape, dt=F32):
        d[name] = nc.dram_tensor(name, shape, dt, kind="ExternalInput").ap()

    def dou(name, shape, dt=F32):
        d[name] = nc.dram_tensor(name, shape, dt, kind="ExternalOutput").ap()

    def dit(name, shape, dt=BF16):
        d[name] = nc.dram_tensor(name, shape, dt).ap()

    din("xfc", [CPC * H, T]); din("xfb", [2 * H, S])
    din("wqT", [H, H]); din("wkT", [H, H]); din("wvT", [H, H])
    din("qn", [128, 8]); din("kn", [128, 8]); din("knrow", [1, H])
    din("twT", [H, T]); din("twA", [H, T]); din("twE", [H, T])
    din("w0fs", [128, 8 * JS]); din("w1js", [128, 2 * H]); din("lnr", [1, H])
    din("eselh", [16, 128]); din("onesh", [128, 128]); din("eyeh", [128, 128])
    din("gwH", [H, 16]); din("gbH", [1, 16])
    din("wgS", [H, KDE]); din("wuS", [H, KDE]); din("wdS", [KDE, H])
    dou("yout", [H, TPC])

    dit("kd_in", [CPC * H, T]); dit("kd_all", [NCORE * CPC * H, T])
    dit("sm_in", [CPC, 32], F32); dit("sm_all", [NCH, 32], F32)
    dit("q_dram", [H, TPC])
    dit("ktok_dram", [TPC, H]); dit("vtok_dram", [TPC, H])
    dit("d_own", [NCH * 128, H])
    dit("hU_dram", [2 * JS, T])
    dit("yrs_in", [T, H]); dit("yrs_out", [128, H])
    dit("dy_in", [128, H]); dit("dy_all", [T, H])
    dit("nr_in", [1, 1536], F32); dit("nr_out", [1, 1536], F32)
    dit("w1scr", [JS, H])
    dit("w0_in", [H, JS]); dit("w0_all", [NCORE * H, JS])
    dit("w1_in", [JS, H]); dit("w1_all", [IN, H])
    dit("x1scr", [H, TPC], F32)
    dit("x1bfd", [H, TPC]); dit("x1lod", [H, TPC])
    dit("wgB", [H, KDE]); dit("wuB", [H, KDE]); dit("wdB", [KDE, H])
    dit("act_dram", [NE * T, 512])

    dbg_shapes = {
        "kall": ([NCORE * CPC * H, T], BF16), "sm": ([NCH, 32], F32),
        "qd": ([H, TPC], BF16), "down": ([NCH * 128, H], BF16),
        "w0f": ([128, 8 * JS], F32), "w1j": ([128, 2 * H], F32),
        "lnrow": ([1, H], F32), "yblk": ([128, H], F32),
        "dyall": ([T, H], BF16), "n2row": ([1, 1536], F32),
        "x1": ([H, TPC], F32),
        "ktd": ([TPC, H], BF16), "vtd": ([TPC, H], BF16),
        "GS": ([128, NCH], F32), "Qw": ([128, H], BF16), "ccol": ([128, 1], F32),
        "esb": ([128, H], F32), "t1s": ([128, H], F32), "wcol": ([128, 1], F32), "nrow": ([1, 1536], F32),
        "nri": ([1, 1536], F32), "nro": ([1, 1536], F32),
        "wts": ([128, 8 * NE], F32), "actd": ([NE * T, 512], BF16),
        "zfp2": ([128, 8 * 512], F32),
    }
    dbg_ap = {}
    need = set()
    if dbg.get("pa"):
        need |= {"kall", "sm", "qd", "down", "ktd", "vtd"}
    if dbg.get("scan_ch") is not None:
        need |= {"w0f", "w1j", "lnrow", "yblk", "dyall", "n2row",
                 "GS", "Qw", "ccol", "esb", "t1s", "wcol", "nrow", "nri", "nro"}
    if dbg.get("x1"):
        need.add("x1")
    if dbg.get("ffn0"):
        need |= {"wts", "actd", "zfp2"}
    for name in need:
        shp, dt = dbg_shapes[name]
        dbg_ap[name] = nc.dram_tensor("dbg_" + name, shp, dt, kind="ExternalOutput").ap()

    es = contextlib.ExitStack()
    with es:
        c = Ctx(nc, es)
        block = es.enter_context(nc.Block())

        def mk_sb(stack):
            return lambda name, shape, dt=BF16, **kw: stack.enter_context(
                nc.sbuf_tensor(name, shape, dt, **kw))

        psb = mk_sb(es)
        st = types.SimpleNamespace(nc=nc, c=c, block=block, dram=d, dbg=dbg,
                                   dbg_ap=dbg_ap)
        st.ones_r = psb("ones_r", [1, 128])
        st.ones_c = psb("ones_c", [128, 1])
        st.eye_bf = psb("eye_bf", [128, 128])
        st.eye_f32 = psb("eye_f32", [128, 128], F32)
        st.knrow_bf = psb("knrow_bf", [1, H])
        st.GS = psb("GS", [128, NCH], F32)
        st.BT = psb("BT", [128, NCH], F32)
        st.ET = psb("ET", [128, NCH], F32)
        st.BErow = psb("BErow", [1, 64], F32)
        st.w0f = psb("w0f", [128, 8 * JS], F32)
        st.w1j = psb("w1j", [128, 2 * H], F32)
        st.s0f = psb("s0f", [128, 8 * JS], F32)
        st.s1j = psb("s1j", [128, 2 * H], F32)
        st.w0f_bf = psb("w0f_bf", [128, 8 * JS])
        st.w1j_bf = psb("w1j_bf", [128, 2 * H])
        st.w1hc_bf = psb("w1hc_bf", [128, 8 * JS], is_transpose_dest=True)
        st.ln_row = psb("ln_row", [1, H], F32)
        st.sln_row = psb("sln_row", [1, H], F32)
        st.ln_bfr = psb("ln_bfr", [1, H])
        st.P = [es.enter_context(nc.psum_tensor(f"P{i}", [128, 1024], F32))
                for i in range(4)]

        @block.sync
        def _(sy):
            c.dma(sy, "q", st.eye_f32[:], d["eyeh"])

        esA = contextlib.ExitStack()
        st.sb = mk_sb(esA)
        phase_a(st)
        if dbg.get("pa"):
            @block.gpsimd
            def _(g):
                for sname in ("q", "g", "s", "v", "c", "t"):
                    c.w(g, sname)
                c.dma(g, "g", dbg_ap["kall"], d["kd_all"][:, :])
                c.dma(g, "g", dbg_ap["sm"], d["sm_all"][:, :])
                c.dma(g, "g", dbg_ap["qd"], d["q_dram"][:, :])
                c.dma(g, "g", dbg_ap["down"], d["d_own"][:, :])
                c.dma(g, "g", dbg_ap["ktd"], d["ktok_dram"][:, :])
                c.dma(g, "g", dbg_ap["vtd"], d["vtok_dram"][:, :])
        c.barrier(block)
        esA.close()

        if not dbg.get("stop_pa"):
            esS = contextlib.ExitStack()
            st.sb = mk_sb(esS)
            scan(st)
            c.barrier(block)
            esS.close()

        if not dbg.get("stop_pa") and not dbg.get("stop_scan"):
            esF = contextlib.ExitStack()
            st.sb = mk_sb(esF)
            final_attn(st)
            c.barrier(block)
            esF.close()

            esN = contextlib.ExitStack()
            st.sb = mk_sb(esN)
            ffn(st)
            c.barrier(block)
            esN.close()

        @block.gpsimd
        def _(g):
            for sname in ("q", "g", "s", "v", "c", "t"):
                c.w(g, sname)
    return nc


def _host_maps(inputs):
    f32 = np.float32
    x = np.ascontiguousarray(np.asarray(inputs["L0_attn_x"], f32))
    wq = np.asarray(inputs["L0_attn_wq_w"], f32)
    wk = np.asarray(inputs["L0_attn_wk_w"], f32)
    wv = np.asarray(inputs["L0_attn_wv_w"], f32)
    qnw = np.asarray(inputs["L0_attn_q_norm_w"], f32)
    knw = np.asarray(inputs["L0_attn_k_norm_w"], f32)
    aw = np.asarray(inputs["L0_attn_alpha_w"], f32)
    tw = np.asarray(inputs["L0_attn_theta_w"], f32)
    ew = np.asarray(inputs["L0_attn_eta_w"], f32)
    m0 = np.asarray(inputs["L0_attn_mem_w0"], f32)
    m1 = np.asarray(inputs["L0_attn_mem_w1"], f32)
    mln = np.asarray(inputs["L0_attn_mem_ln_w"], f32)
    gw = np.asarray(inputs["L0_ffn_gate_w"], f32)
    gb = np.asarray(inputs["L0_ffn_gate_b"], f32)
    swg = np.asarray(inputs["L0_ffn_shared_wg"], f32)
    swu = np.asarray(inputs["L0_ffn_shared_wu"], f32)
    swd = np.asarray(inputs["L0_ffn_shared_wd"], f32)
    ewg = np.asarray(inputs["L0_ffn_experts_wg"], f32)
    ewu = np.asarray(inputs["L0_ffn_experts_wu"], f32)
    ewd = np.asarray(inputs["L0_ffn_experts_wd"], f32)

    CC = np.ascontiguousarray
    shared = {
        "wqT": CC(wq.T), "wkT": CC(wk.T), "wvT": CC(wv.T),
        "qn": CC(qnw.reshape(8, 128).T), "kn": CC(knw.reshape(8, 128).T),
        "knrow": CC(knw.reshape(1, H)),
        "twT": CC(np.tile(tw.reshape(CH, H).T, (1, B))),
        "twA": CC(np.tile(aw.reshape(CH, H).T, (1, B))),
        "twE": CC(np.tile(ew.reshape(CH, H).T, (1, B))),
        "lnr": CC(mln.reshape(1, H)),
        "onesh": np.ones((128, 128), f32),
        "eyeh": np.eye(128, dtype=f32),
        "gwH": CC(gw), "gbH": CC(gb.reshape(1, 16)),
        "wgS": CC(np.concatenate(
            [ewg.transpose(1, 0, 2).reshape(H, 16 * 512), swg], axis=1)),
        "wuS": CC(np.concatenate(
            [ewu.transpose(1, 0, 2).reshape(H, 16 * 512), swu], axis=1)),
        "wdS": CC(np.concatenate([ewd.reshape(16 * 512, H), swd], axis=0)),
    }
    w0T = m0.T
    xc = x.reshape(B, NCH, CH, H)
    in_maps = []
    for r in range(NCORE):
        esel = np.zeros((16, 128), f32)
        for p in range(128):
            esel[2 * r + p // CH, p] = 1.0
        m = dict(shared)
        m["xfc"] = CC(xc[:, 4 * r:4 * r + 4].transpose(1, 3, 0, 2)
                      .reshape(CPC * H, T))
        m["xfb"] = CC(x[2 * r:2 * r + 2].transpose(0, 2, 1).reshape(2 * H, S))
        m["w0fs"] = CC(w0T[:, JS * r:JS * (r + 1)].reshape(8, 128, JS)
                       .transpose(1, 0, 2).reshape(128, 8 * JS))
        m["w1js"] = CC(m1[:, JS * r:JS * (r + 1)].T.reshape(2, 128, H)
                       .transpose(1, 0, 2).reshape(128, 2 * H))
        m["eselh"] = esel
        in_maps.append(m)
    return in_maps


def kernel(**inputs):
    in_maps = _host_maps(inputs)
    nc = build_nc()
    res = run_bass_kernel_spmd(nc, in_maps, core_ids=list(range(NCORE)))
    outs = []
    for r in range(NCORE):
        y = np.asarray(res.results[r]["yout"], dtype=np.float32)
        outs.append(y.T.reshape(2, S, H))
    return np.ascontiguousarray(np.concatenate(outs, axis=0))


# version 05 (final)
# speedup vs baseline: 1.0118x; 1.0073x over previous
import contextlib
import types
import numpy as np
import ml_dtypes
import concourse.bass as bass
import concourse.mybir as mybir
from concourse.bass_utils import run_bass_kernel_spmd

F32 = mybir.dt.float32
BF16 = mybir.dt.bfloat16
AF = mybir.ActivationFunctionType
ALU = mybir.AluOpType
bfd = ml_dtypes.bfloat16

RG = [[0, 1, 2, 3, 4, 5, 6, 7]]
NCORE = 8
B, S, H = 16, 2048, 1024
CH = 64
NCH = 32
IN = 2048
JS = IN // NCORE
T = B * CH
NLOSS = float(B * CH * H)
EPS = 1e-6
LNEPS = 1e-5
BASE_LR = 0.01
CPC = NCH // NCORE
TPC = 2 * S
NE = 17
KDE = NE * 512


class Ctx:
    def __init__(self, nc, es):
        self.nc = nc
        self.sems = {}
        self.cnt = {}
        for n in ("q", "g", "s", "v", "c", "t"):
            self.sems[n] = es.enter_context(nc.semaphore("sem_" + n))
            self.cnt[n] = 0

    def inc(self, instr, name, k=1):
        instr.then_inc(self.sems[name], k)
        self.cnt[name] += k
        return self.cnt[name]

    def dma(self, eng, qn, out, in_, **kw):
        return self.inc(eng.dma_start(out=out, in_=in_, **kw), qn, 16)

    def tr(self, eng, qn, out, in_):
        return self.inc(eng.dma_start_transpose(out=out, in_=in_), qn, 16)

    def w(self, eng, name, val=None):
        eng.wait_ge(self.sems[name], self.cnt[name] if val is None else val)

    def vsw(self, eng, ins, name="v"):
        eng.drain()

    def barrier(self, block):
        snap = dict(self.cnt)
        for ename in ("sync", "gpsimd", "scalar", "vector", "tensor"):
            dec = getattr(block, ename)

            @dec
            def _(e):
                for sname, v in snap.items():
                    e.wait_ge(self.sems[sname], v)


def _mmslice(buf, col0, width):
    return buf[:, col0:col0 + width]


def phase_a(st):
    nc, c, block, d = st.nc, st.c, st.block, st.dram
    sb = st.sb
    P = st.P

    knq = sb("knq", [128, 8], F32)
    qnq = sb("qnq", [128, 8], F32)
    wT_bf = sb("wT_bf", [128, 8 * H])
    tw_bf = sb("tw_bf", [128, 8 * T])
    xck = sb("xck", [128, 8 * T])
    silk = sb("silk", [128, 8 * T])
    kout = sb("kout", [128, 8 * T])
    sqs = sb("sqs", [128, 1024])
    rrow = sb("rrow", [1, 1024], F32)
    rrow2 = sb("rrow2", [1, 64], F32)
    rrowR = sb("rrowR", [1, 1024], F32)
    rstcR = sb("rstcR", [128, 1], F32)
    rbf = sb("rbf", [1, 1024])
    RBs = sb("RBs", [128, 1024])
    smrow = sb("smrow", [1, 32], F32)
    KNB = sb("KNB", [128, 1024])
    xbk = sb("xbk", [128, 8 * S])
    silt = sb("silt", [128, 1024])
    vtok = sb("vtok", [128, 1024])
    ktok = sb("ktok", [128, 1024])
    ssc = sb("ssc", [128, 1], F32)
    rstc = sb("rstc", [128, 1], F32)

    @block.sync
    def _(sy):
        c.dma(sy, "q", knq[:], d["kn"])
        c.dma(sy, "q", qnq[:], d["qn"])

    @block.vector
    def _(v):
        ins = v.memset(smrow[:], 0.0)
        c.inc(ins, "v")

    @block.gpsimd
    def _(g):
        c.dma(g, "g", st.ones_r[:], d["onesh"][0:1, :])
        c.dma(g, "g", st.ones_c[:], d["onesh"][:, 0:1],
              allow_slow_non_contiguous=True)
        c.dma(g, "g", st.eye_bf[:], d["eyeh"])
        c.dma(g, "g", st.knrow_bf[:], d["knrow"])
        c.dma(g, "g", wT_bf[:].rearrange("p (k h) -> p k h", k=8), d["wkT"].rearrange("(k p) h -> p k h", p=128))
    g_static = c.cnt["g"]

    for s in range(CPC):
        @block.gpsimd
        def _(g):
            if s == 0:
                c.w(g, "g", g_static)
            else:
                c.w(g, "g")
                c.w(g, "v")
                c.w(g, "t")
            c.dma(g, "g", xck[:].rearrange("p (k t) -> p k t", k=8),
                  d["xfc"][s * H:(s + 1) * H, :].rearrange("(k p) t -> p k t", p=128))
        ld_x = c.cnt["g"]

        for m in range(8):
            pm = P[m % 2]

            @block.tensor
            def _(t):
                if m == 0:
                    c.w(t, "g", ld_x)
                if m >= 2:
                    c.w(t, "c")
                ins = None
                for k in range(8):
                    for nh in range(2):
                        ins = t.matmul(
                            pm[:, nh * 512:nh * 512 + 512],
                            wT_bf[:, k * H + m * 128:k * H + m * 128 + 128],
                            xck[:, k * T + nh * 512:k * T + nh * 512 + 512],
                            start=(k == 0), stop=(k == 7))
                c.inc(ins, "t")
            mm_done = c.cnt["t"]

            @block.scalar
            def _(sc):
                c.w(sc, "t", mm_done)
                ins = sc.activation(out=silk[:, m * T:(m + 1) * T], in_=pm[:],
                                    func=AF.Silu)
                c.inc(ins, "c")
        sil_done = c.cnt["c"]

        for m in range(8):
            @block.scalar
            def _(sc):
                if m == 0:
                    c.w(sc, "c", sil_done)
                else:
                    c.w(sc, "t")
                ins = sc.activation(out=sqs[:], in_=silk[:, m * T:(m + 1) * T],
                                    func=AF.Square)
                c.inc(ins, "c")
            sq_done = c.cnt["c"]

            @block.tensor
            def _(t):
                c.w(t, "c", sq_done)
                ins = None
                for nh in range(2):
                    ins = t.matmul(P[2][0:1, nh * 512:nh * 512 + 512], st.ones_c[:],
                                   sqs[:, nh * 512:nh * 512 + 512],
                                   start=(m == 0), stop=(m == 7), skip_group_check=True)
                c.inc(ins, "t")
        ss_done = c.cnt["t"]

        @block.scalar
        def _(sc):
            c.w(sc, "t", ss_done)
            ins = sc.copy(out=rrow[:], in_=P[2][0:1, :])
            c.inc(ins, "c")

        @block.vector
        def _(v):
            c.w(v, "c")
            ins = v.tensor_scalar(out=rrow[:], in0=rrow[:], scalar1=1.0 / float(H),
                                  scalar2=EPS, op0=ALU.mult, op1=ALU.add)
            c.inc(ins, "v")

        @block.scalar
        def _(sc):
            c.w(sc, "v")
            ins = sc.activation(out=rrow[:], in_=rrow[:], func=AF.Sqrt)
            c.inc(ins, "c")

        @block.vector
        def _(v):
            c.w(v, "c")
            v.reciprocal(out=rrowR[:], in_=rrow[:])
            v.drain()
            ins = v.tensor_copy(out=rbf[:], in_=rrowR[:])
            c.inc(ins, "v")

        @block.tensor
        def _(t):
            c.w(t, "v")
            ins = None
            for nh in range(2):
                ins = t.matmul(P[3][:, nh * 512:nh * 512 + 512], st.ones_r[:],
                               rbf[0:1, nh * 512:nh * 512 + 512], start=True, stop=True)
            c.inc(ins, "t")

        @block.vector
        def _(v):
            c.w(v, "t")
            v.tensor_copy(out=RBs[:], in_=P[3][:])
            ins = None
            for m in range(8):
                ins = v.scalar_tensor_tensor(
                    out=kout[:, m * T:(m + 1) * T],
                    in0=silk[:, m * T:(m + 1) * T], scalar=knq[:, m:m + 1],
                    in1=RBs[:], op0=ALU.mult, op1=ALU.mult)
            c.inc(ins, "v")
        k_done = c.cnt["v"]

        @block.gpsimd
        def _(g):
            c.w(g, "v", k_done)
            c.dma(g, "g",
                  d["kd_in"][s * H:(s + 1) * H, :].rearrange("(k p) t -> p k t", p=128),
                  kout[:].rearrange("p (k t) -> p k t", k=8))

        for wi, wname in enumerate(("twT", "twA", "twE")):
            @block.gpsimd
            def _(g):
                if wi > 0:
                    c.w(g, "v")
                c.dma(g, "g", tw_bf[:].rearrange("p (k t) -> p k t", k=8),
                      d[wname].rearrange("(k p) t -> p k t", p=128))
            tw_ld = c.cnt["g"]

            @block.vector
            def _(v):
                c.w(v, "g", tw_ld)
                if wi == 0:
                    c.w(v, "v", k_done)
                ins = None
                for m in range(8):
                    ins = v.tensor_tensor(
                        out=silk[:, m * T:(m + 1) * T],
                        in0=xck[:, m * T:(m + 1) * T],
                        in1=tw_bf[:, m * T:(m + 1) * T], op=ALU.mult)
                c.inc(ins, "v")
            z_done = c.cnt["v"]

            @block.tensor
            def _(t):
                c.w(t, "v", z_done)
                ins = None
                for m in range(8):
                    for nh in range(2):
                        ins = t.matmul(P[2][0:1, nh * 512:nh * 512 + 512], st.ones_c[:],
                                       silk[:, m * T + nh * 512:m * T + nh * 512 + 512],
                                       start=(m == 0), stop=(m == 7))
                c.inc(ins, "t")
            dot_done = c.cnt["t"]

            @block.vector
            def _(v):
                c.w(v, "t", dot_done)
                v.tensor_copy(out=rrow[:], in_=P[2][0:1, :])
                v.drain()
                ins = v.reduce_sum(
                    out=rrow2[0:1, 0:16],
                    in_=rrow.rearrange("o (b i) -> o b i", b=16)[:, :, :],
                    axis=mybir.AxisListType.X)
                c.inc(ins, "v")
            seg_done = c.cnt["v"]

            if wi == 0:
                @block.scalar
                def _(sc):
                    c.w(sc, "v", seg_done)
                    ins = sc.activation(out=smrow[0:1, 0:16], in_=rrow2[0:1, 0:16],
                                        func=AF.Sigmoid)
                    c.inc(ins, "c")

                @block.vector
                def _(v):
                    c.w(v, "c")
                    ins = v.tensor_scalar(out=smrow[0:1, 0:16], in0=smrow[0:1, 0:16],
                                          scalar1=BASE_LR, scalar2=None, op0=ALU.mult)
                    c.inc(ins, "v")
            else:
                @block.vector
                def _(v):
                    v.drain()
                    ins = v.reduce_sum(out=rrow2[0:1, 16:17], in_=rrow2[0:1, 0:16],
                                       axis=mybir.AxisListType.X)
                    v.drain()
                    c.inc(ins, "v")

                @block.scalar
                def _(sc):
                    c.w(sc, "v")
                    pass

                @block.vector
                def _(v):
                    v.drain()
                    ins = v.tensor_scalar(out=rrow2[0:1, 16:17], in0=rrow2[0:1, 16:17],
                                          scalar1=1.0 / 16.0, scalar2=None, op0=ALU.mult)
                    v.drain()
                    c.inc(ins, "v")

                @block.scalar
                def _(sc):
                    c.w(sc, "v")
                    ins = sc.activation(out=rrow2[0:1, 17:18], in_=rrow2[0:1, 16:17],
                                        func=AF.Sigmoid)
                    c.inc(ins, "c")

                @block.vector
                def _(v):
                    c.w(v, "c")
                    if wi == 1:
                        ins = v.tensor_scalar(out=smrow[0:1, 16:17],
                                              in0=rrow2[0:1, 17:18],
                                              scalar1=-1.0, scalar2=1.0,
                                              op0=ALU.mult, op1=ALU.add)
                    else:
                        ins = v.tensor_copy(out=smrow[0:1, 17:18], in_=rrow2[0:1, 17:18])
                    c.inc(ins, "v")
        sm_done = c.cnt["v"]

        @block.gpsimd
        def _(g):
            c.w(g, "v", sm_done)
            c.dma(g, "g", d["sm_in"][s:s + 1, :], smrow[:])

    @block.tensor
    def _(t):
        c.w(t, "g")
        ins = t.matmul(P[3][:, 0:512], st.ones_r[:], st.knrow_bf[0:1, 0:512],
                       start=True, stop=True)
        ins = t.matmul(P[3][:, 512:1024], st.ones_r[:], st.knrow_bf[0:1, 512:1024],
                       start=True, stop=True)
        c.inc(ins, "t")

    @block.vector
    def _(v):
        c.w(v, "t")
        ins = v.tensor_copy(out=KNB[:], in_=P[3][:])
        c.inc(ins, "v")
    knb_done = c.cnt["v"]

    for proj in ("q", "k", "v"):
        wsrc = {"q": "wqT", "k": "wkT", "v": "wvT"}[proj]

        @block.gpsimd
        def _(g):
            c.w(g, "t")
            c.w(g, "v")
            c.w(g, "c")
            c.dma(g, "g", wT_bf[:].rearrange("p (k h) -> p k h", k=8), d[wsrc].rearrange("(k p) h -> p k h", p=128))
        w_ld = c.cnt["g"]

        for bb in range(2):
            @block.gpsimd
            def _(g):
                if bb == 0:
                    c.w(g, "g", w_ld)
                else:
                    c.w(g, "t")
                c.dma(g, "g", xbk[:].rearrange("p (k t) -> p k t", k=8),
                      d["xfb"][bb * H:(bb + 1) * H, :].rearrange("(k p) t -> p k t", p=128))
            xb_ld = c.cnt["g"]

            for tt in range(4):
                if proj == "q":
                    for m in range(8):
                        pm = P[m % 2]

                        @block.tensor
                        def _(t):
                            if m == 0 and tt == 0:
                                c.w(t, "g", xb_ld)
                            if m >= 2 or tt > 0:
                                c.w(t, "c")
                            ins = None
                            for k in range(8):
                                ins = t.matmul(
                                    pm[:, 0:512],
                                    wT_bf[:, k * H + m * 128:k * H + m * 128 + 128],
                                    xbk[:, k * S + tt * 512:k * S + tt * 512 + 512],
                                    start=(k == 0), stop=(k == 7))
                            c.inc(ins, "t")
                        mm_done = c.cnt["t"]

                        @block.scalar
                        def _(sc):
                            c.w(sc, "t", mm_done)
                            ins = sc.activation(out=silk[:, m * 512:(m + 1) * 512],
                                                in_=pm[:, 0:512], func=AF.Silu)
                            c.inc(ins, "c")
                    sil_done = c.cnt["c"]

                    for m in range(8):
                        @block.scalar
                        def _(sc):
                            if m == 0:
                                c.w(sc, "c", sil_done)
                            else:
                                c.w(sc, "t")
                            ins = sc.activation(out=sqs[:, 0:512],
                                                in_=silk[:, m * 512:(m + 1) * 512],
                                                func=AF.Square)
                            c.inc(ins, "c")
                        sq_done = c.cnt["c"]

                        @block.tensor
                        def _(t):
                            c.w(t, "c", sq_done)
                            ins = t.matmul(P[2][0:1, 0:512], st.ones_c[:], sqs[:, 0:512],
                                           start=(m == 0), stop=(m == 7),
                                           skip_group_check=True)
                            c.inc(ins, "t")

                    @block.vector
                    def _(v):
                        c.w(v, "t")
                        ins = v.tensor_scalar(out=rrow[0:1, 0:512], in0=P[2][0:1, 0:512],
                                              scalar1=1.0 / float(H), scalar2=EPS,
                                              op0=ALU.mult, op1=ALU.add)
                        c.inc(ins, "v")

                    @block.scalar
                    def _(sc):
                        c.w(sc, "v")
                        ins = sc.activation(out=rrow[0:1, 0:512], in_=rrow[0:1, 0:512],
                                            func=AF.Sqrt)
                        c.inc(ins, "c")

                    @block.vector
                    def _(v):
                        c.w(v, "c")
                        v.reciprocal(out=rrowR[0:1, 0:512], in_=rrow[0:1, 0:512])
                        v.drain()
                        ins = v.tensor_copy(out=rbf[0:1, 0:512], in_=rrowR[0:1, 0:512])
                        c.inc(ins, "v")

                    @block.tensor
                    def _(t):
                        c.w(t, "v")
                        ins = t.matmul(P[3][:, 0:512], st.ones_r[:], rbf[0:1, 0:512],
                                       start=True, stop=True)
                        c.inc(ins, "t")

                    @block.vector
                    def _(v):
                        c.w(v, "t")
                        v.tensor_copy(out=RBs[:, 0:512], in_=P[3][:, 0:512])
                        ins = None
                        for m in range(8):
                            ins = v.scalar_tensor_tensor(
                                out=kout[:, m * 512:(m + 1) * 512],
                                in0=silk[:, m * 512:(m + 1) * 512],
                                scalar=qnq[:, m:m + 1], in1=RBs[:, 0:512],
                                op0=ALU.mult, op1=ALU.mult)
                        c.inc(ins, "v")
                    q_done = c.cnt["v"]

                    @block.sync
                    def _(sy):
                        c.w(sy, "v", q_done)
                        c.dma(sy, "q",
                              d["q_dram"][:, bb * S + tt * 512:bb * S + tt * 512 + 512]
                              .rearrange("(k p) t -> p k t", p=128),
                              kout[:, 0:8 * 512].rearrange("p (k t) -> p k t", k=8))
                else:
                    for m in range(4):
                        pm = P[m % 2]

                        @block.tensor
                        def _(t):
                            if m == 0 and tt == 0:
                                c.w(t, "g", xb_ld)
                            if m >= 2 or tt > 0:
                                c.w(t, "c")
                                c.w(t, "v")
                            ins = None
                            for k in range(8):
                                for nh in range(2):
                                    ins = t.matmul(
                                        pm[:, nh * 512:nh * 512 + 512],
                                        xbk[:, k * S + tt * 512 + m * 128:
                                            k * S + tt * 512 + m * 128 + 128],
                                        wT_bf[:, k * H + nh * 512:k * H + nh * 512 + 512],
                                        start=(k == 0), stop=(k == 7))
                            c.inc(ins, "t")
                        mm_done = c.cnt["t"]

                        @block.scalar
                        def _(sc):
                            c.w(sc, "t", mm_done)
                            if proj == "k":
                                sc.activation(out=silt[:], in_=pm[:], func=AF.Silu)
                                ins = sc.activation(out=sqs[:], in_=silt[:],
                                                    func=AF.Square, accum_out=ssc[:])
                                c.inc(ins, "c")
                            else:
                                c.w(sc, "q")
                                ins = sc.activation(out=vtok[:], in_=pm[:], func=AF.Silu)
                                c.inc(ins, "c")
                        act_done = c.cnt["c"]

                        if proj == "k":
                            @block.vector
                            def _(v):
                                c.w(v, "c", act_done)
                                ins = v.tensor_scalar(out=rstc[:], in0=ssc[:],
                                                      scalar1=1.0 / float(H), scalar2=EPS,
                                                      op0=ALU.mult, op1=ALU.add)
                                c.inc(ins, "v")

                            @block.scalar
                            def _(sc):
                                c.w(sc, "v")
                                ins = sc.activation(out=rstc[:], in_=rstc[:], func=AF.Sqrt)
                                c.inc(ins, "c")

                            @block.vector
                            def _(v):
                                c.w(v, "c")
                                c.w(v, "q")
                                v.reciprocal(out=rstcR[:], in_=rstc[:])
                                v.drain()
                                ins = v.scalar_tensor_tensor(
                                    out=ktok[:], in0=silt[:], scalar=rstcR[:],
                                    in1=KNB[:], op0=ALU.mult, op1=ALU.mult)
                                c.inc(ins, "v")
                        ops_done = c.cnt["v"]

                        @block.sync
                        def _(sy):
                            row0 = (bb * S + tt * 512 + m * 128)
                            if proj == "k":
                                c.w(sy, "v", ops_done)
                                c.dma(sy, "q", d["ktok_dram"][row0:row0 + 128, :], ktok[:])
                            else:
                                c.w(sy, "c", act_done)
                                c.dma(sy, "q", d["vtok_dram"][row0:row0 + 128, :], vtok[:])

    dstream = sb("dstream", [128, 1024])
    kstream = sb("kstream", [128, 1024])

    for bb in range(2):
        for tt in range(4):
            for m in range(4):
                row0 = bb * S + tt * 512 + m * 128

                @block.sync
                def _(sy):
                    c.w(sy, "q")
                    c.w(sy, "v")
                    c.dma(sy, "q", kstream[:], d["ktok_dram"][row0:row0 + 128, :])
                    c.dma(sy, "q", dstream[:], d["vtok_dram"][row0:row0 + 128, :])
                ld2 = c.cnt["q"]

                @block.vector
                def _(v):
                    c.w(v, "q", ld2)
                    ins = v.tensor_tensor(out=dstream[:], in0=kstream[:], in1=dstream[:],
                                          op=ALU.subtract)
                    c.inc(ins, "v")
                d_done = c.cnt["v"]

                @block.sync
                def _(sy):
                    c.w(sy, "v", d_done)
                    base_tok = tt * 512 + m * 128
                    for seg in range(2):
                        ch = (base_tok + seg * 64) // CH
                        c.dma(sy, "q",
                              d["d_own"][ch * 128 + bb * CH:ch * 128 + bb * CH + CH, :],
                              dstream[seg * 64:(seg + 1) * 64, :])

    @block.gpsimd
    def _(g):
        c.w(g, "g")
        c.w(g, "q")
        c.w(g, "v")
        ins = g.collective_compute("AllGather", ALU.bypass, RG,
                                   ins=[d["kd_in"].opt()],
                                   outs=[d["kd_all"].opt()])
        c.inc(ins, "g", 1)
        ins = g.collective_compute("AllGather", ALU.bypass, RG,
                                   ins=[d["sm_in"].opt()],
                                   outs=[d["sm_all"].opt()])
        c.inc(ins, "g", 1)
    st.ag_done = c.cnt["g"]

    th_sb = sb("th_sb", [16, 32])
    esel_bf = sb("esel_bf", [16, 128])
    berow = sb("berow", [1, 64])

    @block.gpsimd
    def _(g):
        c.w(g, "g", st.ag_done)
        c.dma(g, "g", esel_bf[:], d["eselh"])
        c.dma(g, "g", th_sb[:], d["sm_all"][:, 0:16].rearrange("ch b -> b ch"),
              allow_slow_non_contiguous=True)
        c.dma(g, "g", berow[0:1, 0:32], d["sm_all"][:, 16:17].rearrange("ch o -> o ch"),
              allow_slow_non_contiguous=True)
        c.dma(g, "g", berow[0:1, 32:64], d["sm_all"][:, 17:18].rearrange("ch o -> o ch"),
              allow_slow_non_contiguous=True)
    tbl_ld = c.cnt["g"]

    @block.tensor
    def _(t):
        c.w(t, "g", tbl_ld)
        ins = t.matmul(P[0][:, 0:32], esel_bf[:], th_sb[:], start=True, stop=True)
        c.inc(ins, "t")
        ins = t.matmul(P[1][:, 0:64], st.ones_r[:], berow[0:1, :], start=True, stop=True)
        c.inc(ins, "t")

    @block.vector
    def _(v):
        c.w(v, "t")
        v.tensor_scalar(out=st.GS[:], in0=P[0][:, 0:32], scalar1=2.0 / NLOSS,
                        scalar2=None, op0=ALU.mult)
        v.tensor_copy(out=st.BT[:], in_=P[1][:, 0:32])
        v.tensor_copy(out=st.ET[:], in_=P[1][:, 32:64])
        ins = v.tensor_copy(out=st.BErow[:], in_=P[1][0:1, 0:64])
        c.inc(ins, "v")
    st.tbl_done = c.cnt["v"]


def scan(st):
    nc, c, block, d = st.nc, st.c, st.block, st.dram
    sb = st.sb
    P = st.P
    dbg = st.dbg

    kf = [sb(f"kf{i}", [128, 8 * T]) for i in range(2)]
    kt = sb("kt", [128, 8 * H], is_transpose_dest=True)
    dblk = [sb(f"dblk{i}", [128, H]) for i in range(2)]
    h_j = sb("h_j", [128, 2 * T])
    U_j = sb("U_j", [128, 2 * T])
    h_tok = sb("h_tok", [128, 8 * JS], is_transpose_dest=True)
    U_tok = sb("U_tok", [128, 8 * JS], is_transpose_dest=True)
    ypp = [sb(f"ypp{i}", [128, H]) for i in range(2)]
    yblk = sb("yblk", [128, H])
    LNB = sb("LNB", [128, H])
    t1s = sb("t1s", [128, H], F32)
    esb = sb("esb", [128, H], F32)
    Qsb = sb("Qsb", [128, H], F32)
    Qw = sb("Qw", [128, H])
    dyb = sb("dyb", [128, H])
    dy_tok = sb("dy_tok", [128, 8 * H])
    df2 = sb("df2", [128, 2 * T], is_transpose_dest=True)
    dW1sb = sb("dW1sb", [128, 2 * H], F32)
    dW0sb = sb("dW0sb", [128, 8 * JS], F32)
    dPre = sb("dPre", [128, 8 * JS])
    sscol = sb("sscol", [128, 1], F32)
    rstdc = sb("rstdc", [128, 1], F32)
    rstdR = sb("rstdR", [128, 1], F32)
    ccol = sb("ccol", [128, 1], F32)
    wcol = sb("wcol", [128, 1], F32)
    bneg = sb("bneg", [128, 1], F32)
    nrmc = sb("nrmc", [128, 12], F32)
    nrmb = sb("nrmb", [128, 12])
    nrow = sb("nrow", [1, 1536], F32)
    n2row = sb("n2row", [1, 1536], F32)
    scr_bf = sb("scr_bf", [128, H])
    clipb = sb("clipb", [1, 2])
    ncol = sb("ncol", [128, 1], F32)

    w0f, w1j, s0f, s1j = st.w0f, st.w1j, st.s0f, st.s1j
    w0f_bf, w1j_bf, w1hc_bf = st.w0f_bf, st.w1j_bf, st.w1hc_bf
    ln_row, sln_row, ln_bfr = st.ln_row, st.sln_row, st.ln_bfr

    @block.sync
    def _(sy):
        c.w(sy, "g", st.ag_done)
        c.dma(sy, "q", w0f[:], d["w0fs"])
        c.dma(sy, "q", w1j[:], d["w1js"])
        c.dma(sy, "q", ln_row[:], d["lnr"])
    st_ld = c.cnt["q"]

    @block.vector
    def _(v):
        c.w(v, "q", st_ld)
        v.memset(s0f[:], 0.0)
        v.memset(s1j[:], 0.0)
        v.memset(sln_row[:], 0.0)
        v.memset(nrow[:], 0.0)
        v.tensor_copy(out=w0f_bf[:], in_=w0f[:])
        v.tensor_copy(out=w1j_bf[:], in_=w1j[:])
        ins = v.tensor_copy(out=ln_bfr[:], in_=ln_row[:])
        c.inc(ins, "v")
    init_done = c.cnt["v"]

    def emit_w1hc(wait_v):
        @block.sync
        def _(sy):
            c.w(sy, "v", wait_v)
            c.dma(sy, "q", d["w1scr"][:, :].rearrange("(k p) h -> p k h", p=128),
                  w1j_bf[:].rearrange("p (k h) -> p k h", k=2))
            c.w(sy, "q")
            for m in range(8):
                c.tr(sy, "q", w1hc_bf[:, m * JS:(m + 1) * JS],
                     d["w1scr"][:, m * 128:(m + 1) * 128])
        return c.cnt["q"]

    w1hc_ready = emit_w1hc(init_done)

    def kd_base(ch):
        return (ch // CPC) * CPC * H + (ch % CPC) * H

    def emit_kf(ch, buf, wait_t):
        kb = kd_base(ch)

        @block.sync
        def _(sy):
            c.w(sy, "g", st.ag_done)
            c.w(sy, "t", wait_t)
            c.dma(sy, "q", kf[buf][:].rearrange("p (k t) -> p k t", k=8),
                  d["kd_all"][kb:kb + H, :].rearrange("(k p) t -> p k t", p=128))
        return c.cnt["q"]

    def emit_kt(ch, wait_t):
        kb = kd_base(ch)

        @block.sync
        def _(sy):
            c.w(sy, "g", st.ag_done)
            c.w(sy, "t", wait_t)
            for m in range(8):
                c.tr(sy, "q", kt[:, m * H:(m + 1) * H],
                     d["kd_all"][kb:kb + H, m * 128:(m + 1) * 128])
        return c.cnt["q"]

    def emit_dblk(ch, buf, wait_v):
        @block.sync
        def _(sy):
            c.w(sy, "v", wait_v)
            c.dma(sy, "q", dblk[buf][:], d["d_own"][ch * 128:(ch + 1) * 128, :])
        return c.cnt["q"]

    kf_rdy = emit_kf(0, 0, 0)
    kt_rdy = emit_kt(0, 0)
    db_rdy = emit_dblk(0, 0, 0)
    gbf_done_box = [0]
    gbf_done = 0
    kf_rdy_next = 0
    m1_done_prev = 0
    m5_done_prev = 0
    ew_done_prev = 0
    upd_done = init_done

    for ch in range(NCH):
        buf = ch % 2
        nbuf = 1 - buf

        if ch + 1 < NCH:
            kf_rdy_next = emit_kf(ch + 1, nbuf, m1_done_prev)
            db_rdy_next = emit_dblk(ch + 1, nbuf, ew_done_prev)

        @block.tensor
        def _(t):
            c.w(t, "q", kf_rdy)
            if ch == 0:
                c.w(t, "v", upd_done)
                for jm in range(2):
                    ins = None
                    for nh in range(2):
                        for k in range(8):
                            ins = t.matmul(
                                P[jm][:, nh * 512:nh * 512 + 512],
                                w0f_bf[:, k * JS + jm * 128:k * JS + jm * 128 + 128],
                                kf[buf][:, k * T + nh * 512:k * T + nh * 512 + 512],
                                start=(k == 0), stop=(k == 7))
                    c.inc(ins, "t")
            else:
                c.w(t, "v", gbf_done)
                for jm in range(2):
                    ins = None
                    for nh in range(2):
                        for k in range(8):
                            ins = t.matmul(
                                P[jm][:, nh * 512:nh * 512 + 512],
                                dPre[:, k * JS + jm * 128:k * JS + jm * 128 + 128],
                                kf[buf][:, k * T + nh * 512:k * T + nh * 512 + 512],
                                start=False, stop=(k == 7), skip_group_check=True)
                    c.inc(ins, "t")
        m1_done = c.cnt["t"]
        m1_done_prev = m1_done

        @block.scalar
        def _(sc):
            c.w(sc, "t", m1_done)
            for jm in range(2):
                sc.activation(out=h_j[:, jm * T:(jm + 1) * T], in_=P[jm][:], func=AF.Silu)
                ins = sc.activation(out=U_j[:, jm * T:(jm + 1) * T], in_=P[jm][:],
                                    func=AF.Derivative_silu)
            c.inc(ins, "c")
        hu_done = c.cnt["c"]

        @block.scalar
        def _(sc):
            c.w(sc, "c", hu_done)
            c.dma(sc, "s", d["hU_dram"][0:JS, :].rearrange("(k p) t -> p k t", p=128),
                  h_j[:].rearrange("p (k t) -> p k t", k=2))
            c.dma(sc, "s", d["hU_dram"][JS:2 * JS, :].rearrange("(k p) t -> p k t", p=128),
                  U_j[:].rearrange("p (k t) -> p k t", k=2))
            c.w(sc, "s")
            for m in range(8):
                c.tr(sc, "s", h_tok[:, m * JS:(m + 1) * JS],
                     d["hU_dram"][0:JS, m * 128:(m + 1) * 128])
                c.tr(sc, "s", U_tok[:, m * JS:(m + 1) * JS],
                     d["hU_dram"][JS:2 * JS, m * 128:(m + 1) * 128])
        hU_tok_rdy = c.cnt["s"]

        gv_y = []
        for tm in range(8):
            pm = P[2 + (tm % 2)]

            @block.tensor
            def _(t):
                if tm == 0:
                    c.w(t, "c", hu_done)
                if tm >= 2:
                    c.w(t, "v")
                ins = None
                for nh in range(2):
                    for k in range(2):
                        ins = t.matmul(
                            pm[:, nh * 512:nh * 512 + 512],
                            h_j[:, k * T + tm * 128:k * T + tm * 128 + 128],
                            w1j_bf[:, k * H + nh * 512:k * H + nh * 512 + 512],
                            start=(k == 0), stop=(k == 1))
                c.inc(ins, "t")
            y_mm = c.cnt["t"]

            @block.vector
            def _(v):
                c.w(v, "t", y_mm)
                if tm >= 2:
                    c.w(v, "g")
                ins = v.tensor_copy(out=ypp[tm % 2][:], in_=pm[:])
                c.inc(ins, "v")
            y_ev = c.cnt["v"]

            @block.gpsimd
            def _(g):
                c.w(g, "v", y_ev)
                if ch > 0 and tm == 0:
                    c.w(g, "g")
                c.dma(g, "g", d["yrs_in"][tm * 128:(tm + 1) * 128, :], ypp[tm % 2][:])
            gv_y.append(c.cnt["g"])

        @block.gpsimd
        def _(g):
            c.w(g, "g", gv_y[-1])
            ins = g.collective_compute("ReduceScatter", ALU.add, RG,
                                       ins=[d["yrs_in"].opt()],
                                       outs=[d["yrs_out"].opt()])
            c.inc(ins, "g", 1)
        rs_done = c.cnt["g"]

        @block.sync
        def _(sy):
            c.w(sy, "g", rs_done)
            c.dma(sy, "q", yblk[:], d["yrs_out"][:, :])
        yblk_rdy = c.cnt["q"]

        @block.tensor
        def _(t):
            c.w(t, "v", upd_done)
            ins = None
            for nh in range(2):
                ins = t.matmul(P[0][:, nh * 512:nh * 512 + 512], st.ones_r[:],
                               ln_bfr[0:1, nh * 512:nh * 512 + 512],
                               start=True, stop=True)
            c.inc(ins, "t")
        lnb_mm = c.cnt["t"]

        @block.vector
        def _(v):
            c.w(v, "t", lnb_mm)
            ins = v.tensor_copy(out=LNB[:], in_=P[0][:])
            c.inc(ins, "v")
        lnb_done = c.cnt["v"]

        @block.scalar
        def _(sc):
            c.w(sc, "q", yblk_rdy)
            ins = sc.activation(out=esb[:], in_=yblk[:], func=AF.Square,
                                accum_out=sscol[:])
            c.inc(ins, "c")
        ssq_done = c.cnt["c"]

        @block.vector
        def _(v):
            c.w(v, "c", ssq_done)
            ins = v.tensor_scalar(out=rstdc[:], in0=sscol[:], scalar1=1.0 / float(H),
                                  scalar2=EPS, op0=ALU.mult, op1=ALU.add)
            c.inc(ins, "v")

        @block.scalar
        def _(sc):
            c.w(sc, "v")
            ins = sc.activation(out=rstdc[:], in_=rstdc[:], func=AF.Sqrt)
            c.inc(ins, "c")
        rstd_sc = c.cnt["c"]

        @block.vector
        def _(v):
            c.w(v, "c", rstd_sc)
            c.w(v, "v", lnb_done)
            c.w(v, "q", db_rdy)
            v.reciprocal(out=rstdR[:], in_=rstdc[:])
            v.drain()
            v.scalar_tensor_tensor(out=t1s[:], in0=yblk[:], scalar=rstdR[:],
                                   in1=LNB[:], op0=ALU.mult, op1=ALU.mult)
            v.tensor_tensor(out=esb[:], in0=t1s[:], in1=dblk[buf][:], op=ALU.add)
            v.tensor_tensor(out=Qsb[:], in0=yblk[:], in1=esb[:], op=ALU.mult)
            ins = v.tensor_tensor(out=t1s[:], in0=Qsb[:], in1=LNB[:], op=ALU.mult)
            c.inc(ins, "v")
        r4_done = c.cnt["v"]

        @block.scalar
        def _(sc):
            c.w(sc, "v", r4_done)
            ins = sc.activation(out=dW1sb[:, 0:H], in_=t1s[:], func=AF.Identity,
                                accum_out=ccol[:])
            c.inc(ins, "c")
        ccol_done = c.cnt["c"]

        @block.vector
        def _(v):
            c.w(v, "c", ccol_done)
            v.tensor_tensor(out=wcol[:], in0=st.GS[:, ch:ch + 1], in1=rstdR[:],
                            op=ALU.mult)
            v.drain()
            ins = v.tensor_scalar(out=Qw[:], in0=Qsb[:], scalar1=wcol[:], scalar2=None,
                                  op0=ALU.mult)
            c.inc(ins, "v")
            c.vsw(v, v.tensor_tensor(out=bneg[:], in0=rstdR[:], in1=rstdR[:], op=ALU.mult))
            c.vsw(v, v.tensor_tensor(out=bneg[:], in0=bneg[:], in1=wcol[:], op=ALU.mult))
            c.vsw(v, v.tensor_tensor(out=bneg[:], in0=bneg[:], in1=ccol[:], op=ALU.mult))
            v.tensor_scalar(out=bneg[:], in0=bneg[:], scalar1=-1.0 / float(H),
                            scalar2=None, op0=ALU.mult)
            v.drain()
            v.tensor_tensor(out=t1s[:], in0=esb[:], in1=LNB[:], op=ALU.mult)
            v.tensor_scalar(out=t1s[:], in0=t1s[:], scalar1=wcol[:], scalar2=None,
                            op0=ALU.mult)
            v.drain()
            ins = v.scalar_tensor_tensor(out=dyb[:], in0=yblk[:], scalar=bneg[:],
                                         in1=t1s[:], op0=ALU.mult, op1=ALU.add)
            c.inc(ins, "v")
        ew_done = c.cnt["v"]
        ew_done_prev = ew_done
        qw_v = ew_done

        @block.tensor
        def _(t):
            c.w(t, "v", qw_v)
            ins = None
            for nh in range(2):
                ins = t.matmul(P[1][0:1, nh * 512:nh * 512 + 512], st.ones_c[:],
                               Qw[:, nh * 512:nh * 512 + 512], start=True, stop=True)
            c.inc(ins, "t")
        gln_mm = c.cnt["t"]

        @block.vector
        def _(v):
            c.w(v, "t", gln_mm)
            ins = v.tensor_copy(out=nrow[0:1, 0:1024], in_=P[1][0:1, :])
            c.inc(ins, "v")
        gln_ev = c.cnt["v"]

        @block.gpsimd
        def _(g):
            c.w(g, "v", ew_done)
            c.dma(g, "g", d["dy_in"][:, :], dyb[:])
            c.w(g, "g")
            ins = g.collective_compute("AllGather", ALU.bypass, RG,
                                       ins=[d["dy_in"].opt()],
                                       outs=[d["dy_all"].opt()])
            c.inc(ins, "g", 1)
        ag_dy = c.cnt["g"]

        @block.sync
        def _(sy):
            c.w(sy, "g", ag_dy)
            c.dma(sy, "q", dy_tok[:].rearrange("p (k h) -> p k h", k=8),
                  d["dy_all"][:, :].rearrange("(k p) h -> p k h", p=128))
        dytok_rdy = c.cnt["q"]

        @block.tensor
        def _(t):
            c.w(t, "q", dytok_rdy)
            c.w(t, "s", hU_tok_rdy)
            c.w(t, "v", gln_ev)
            for jm in range(2):
                ins = None
                for nh in range(2):
                    for k in range(8):
                        ins = t.matmul(
                            P[jm][:, nh * 512:nh * 512 + 512],
                            h_tok[:, k * JS + jm * 128:k * JS + jm * 128 + 128],
                            dy_tok[:, k * H + nh * 512:k * H + nh * 512 + 512],
                            start=(k == 0), stop=(k == 7))
                c.inc(ins, "t")
        m3_done = c.cnt["t"]

        @block.scalar
        def _(sc):
            c.w(sc, "t", m3_done)
            for jm in range(2):
                ins = sc.activation(out=dW0sb[:, 0:H], in_=P[jm][:], func=AF.Square,
                                    accum_out=nrmc[:, jm:jm + 1])
            c.inc(ins, "c")
        n1_done = c.cnt["c"]

        @block.vector
        def _(v):
            c.w(v, "c", n1_done)
            for jm in range(2):
                ins = v.tensor_copy(out=dW1sb[:, jm * H:(jm + 1) * H], in_=P[jm][:])
            c.inc(ins, "v")
        dw1_ev = c.cnt["v"]

        df_rdy = []
        for khc in range(0, 8, 2):
            @block.sync
            def _(sy):
                c.w(sy, "g", ag_dy)
                if khc >= 2:
                    c.w(sy, "t")
                for kk in range(2):
                    c.tr(sy, "q", df2[:, kk * T:(kk + 1) * T],
                         d["dy_all"][:, (khc + kk) * 128:(khc + kk) * 128 + 128])
            df_rdy.append(c.cnt["q"])

            @block.tensor
            def _(t):
                c.w(t, "q", df_rdy[-1])
                if khc == 0:
                    c.w(t, "q", w1hc_ready)
                    c.w(t, "v", dw1_ev)
                ins = None
                for kk in range(2):
                    kh = khc + kk
                    for tm in range(8):
                        pq = P[2 + tm // 4]
                        ins = t.matmul(
                            pq[:, (tm % 4) * 256:(tm % 4) * 256 + 256],
                            df2[:, kk * T + tm * 128:kk * T + tm * 128 + 128],
                            w1hc_bf[:, kh * JS:(kh + 1) * JS],
                            start=(kh == 0), stop=(kh == 7))
                c.inc(ins, "t")
        m4_done = c.cnt["t"]

        @block.vector
        def _(v):
            c.w(v, "t", m4_done)
            c.w(v, "s", hU_tok_rdy)
            ins = None
            for tm in range(8):
                pq = P[2 + tm // 4]
                ins = v.tensor_tensor(out=dPre[:, tm * JS:(tm + 1) * JS],
                                      in0=pq[:, (tm % 4) * 256:(tm % 4) * 256 + 256],
                                      in1=U_tok[:, tm * JS:(tm + 1) * JS], op=ALU.mult)
            c.inc(ins, "v")
        dpre_done = c.cnt["v"]

        @block.tensor
        def _(t):
            c.w(t, "v", dpre_done)
            c.w(t, "q", kt_rdy)
            for hm in range(8):
                pq = P[hm // 4]
                ins = None
                for k in range(8):
                    ins = t.matmul(
                        pq[:, (hm % 4) * 256:(hm % 4) * 256 + 256],
                        kt[:, k * H + hm * 128:k * H + hm * 128 + 128],
                        dPre[:, k * JS:(k + 1) * JS],
                        start=(k == 0), stop=(k == 7))
                c.inc(ins, "t")
        m5_done = c.cnt["t"]
        m5_done_prev = m5_done

        @block.scalar
        def _(sc):
            c.w(sc, "t", m5_done)
            for hm in range(8):
                pq = P[hm // 4]
                ins = sc.activation(out=esb[:, 0:256],
                                    in_=pq[:, (hm % 4) * 256:(hm % 4) * 256 + 256],
                                    func=AF.Square, accum_out=nrmc[:, 2 + hm:3 + hm])
            c.inc(ins, "c")
        n0_done = c.cnt["c"]

        @block.vector
        def _(v):
            c.w(v, "c", n0_done)
            for hm in range(8):
                pq = P[hm // 4]
                ins = v.tensor_copy(out=dW0sb[:, hm * JS:(hm + 1) * JS],
                                    in_=pq[:, (hm % 4) * 256:(hm % 4) * 256 + 256])
            ins = v.tensor_copy(out=nrmb[:, 0:10], in_=nrmc[:, 0:10])
            c.inc(ins, "v")
        dw0_ev = c.cnt["v"]

        if ch + 1 < NCH:
            kt_rdy_next = emit_kt(ch + 1, m5_done)

        @block.tensor
        def _(t):
            c.w(t, "v", dw0_ev)
            ins = t.matmul(P[2][0:1, 0:10], st.ones_c[:], nrmb[:, 0:10],
                           start=True, stop=True)
            c.inc(ins, "t")
        nsum_mm = c.cnt["t"]

        @block.vector
        def _(v):
            c.w(v, "t", nsum_mm)
            ins = v.reduce_sum(out=nrow[0:1, 1024:1025], in_=P[2][0:1, 0:10],
                               axis=mybir.AxisListType.X)
            c.inc(ins, "v")
        nrow_done = c.cnt["v"]

        @block.vector
        def _(v):
            bcol = st.BT[:, ch:ch + 1]
            ecol = st.ET[:, ch:ch + 1]
            v.tensor_scalar(out=s0f[:], in0=s0f[:], scalar1=ecol, scalar2=None,
                            op0=ALU.mult)
            v.tensor_scalar(out=s1j[:], in0=s1j[:], scalar1=ecol, scalar2=None,
                            op0=ALU.mult)
            v.drain()
            v.scalar_tensor_tensor(out=w0f[:], in0=w0f[:], scalar=bcol, in1=s0f[:],
                                   op0=ALU.mult, op1=ALU.add)
            v.scalar_tensor_tensor(out=w1j[:], in0=w1j[:], scalar=bcol, in1=s1j[:],
                                   op0=ALU.mult, op1=ALU.add)
            v.drain()
            ins = v.tensor_copy(out=w0f_bf[:], in_=w0f[:])
            c.inc(ins, "v")
        upre_done = c.cnt["v"]

        @block.gpsimd
        def _(g):
            c.w(g, "v", nrow_done)
            c.dma(g, "g", d["nr_in"][:, :], nrow[:])
            c.w(g, "g")
            ins = g.collective_compute("AllReduce", ALU.add, RG,
                                       ins=[d["nr_in"].opt()],
                                       outs=[d["nr_out"].opt()])
            c.inc(ins, "g", 1)
            c.w(g, "g")
            c.dma(g, "g", n2row[:], d["nr_out"][:, :])
        ar2_done = c.cnt["g"]

        @block.gpsimd
        def _(g):
            r0w = ch * (H // NCH)
            r0d = ch * (KDE // NCH)
            c.dma(g, "g", d["wgB"][r0w:r0w + H // NCH, :], d["wgS"][r0w:r0w + H // NCH, :])
            c.dma(g, "g", d["wuB"][r0w:r0w + H // NCH, :], d["wuS"][r0w:r0w + H // NCH, :])
            c.dma(g, "g", d["wdB"][r0d:r0d + KDE // NCH, :], d["wdS"][r0d:r0d + KDE // NCH, :])

        if ch + 1 < NCH:
            @block.tensor
            def _(t):
                c.w(t, "v", upre_done)
                c.w(t, "q", kf_rdy_next)
                for jm in range(2):
                    ins = None
                    for nh in range(2):
                        for k in range(8):
                            ins = t.matmul(
                                P[jm][:, nh * 512:nh * 512 + 512],
                                w0f_bf[:, k * JS + jm * 128:k * JS + jm * 128 + 128],
                                kf[1 - buf][:, k * T + nh * 512:k * T + nh * 512 + 512],
                                start=(k == 0), stop=False, skip_group_check=True)
                    c.inc(ins, "t")

        @block.scalar
        def _(sc):
            c.w(sc, "g", ar2_done)
            ins = sc.activation(out=esb[0:1, 0:1024], in_=n2row[0:1, 0:1024],
                                func=AF.Square, accum_out=n2row[0:1, 1025:1026])
            c.inc(ins, "c")
        glnsq = c.cnt["c"]

        @block.vector
        def _(v):
            c.w(v, "c", glnsq)
            ins = v.tensor_tensor(out=n2row[0:1, 1025:1026], in0=n2row[0:1, 1025:1026],
                                  in1=n2row[0:1, 1024:1025], op=ALU.add)
            c.inc(ins, "v")

        @block.scalar
        def _(sc):
            c.w(sc, "v")
            ins = sc.activation(out=n2row[0:1, 1025:1026], in_=n2row[0:1, 1025:1026],
                                func=AF.Sqrt)
            c.inc(ins, "c")
        tot_done = c.cnt["c"]

        @block.vector
        def _(v):
            c.w(v, "c", tot_done)
            v.tensor_scalar(out=n2row[0:1, 1025:1026], in0=n2row[0:1, 1025:1026],
                            scalar1=1e-6, scalar2=None, op0=ALU.add)
            ins = v.reciprocal(out=n2row[0:1, 1024:1025], in_=n2row[0:1, 1025:1026])
            c.vsw(v, ins)
            c.vsw(v, v.tensor_scalar(out=n2row[0:1, 1024:1025], in0=n2row[0:1, 1024:1025],
                                     scalar1=1.0, scalar2=None, op0=ALU.min))
            v.tensor_copy(out=clipb[0:1, 0:1], in_=n2row[0:1, 1024:1025])
            ins = v.tensor_scalar(out=clipb[0:1, 1:2], in0=n2row[0:1, 1024:1025],
                                  scalar1=-1.0, scalar2=None, op0=ALU.mult)
            c.inc(ins, "v")
        clip_done = c.cnt["v"]

        @block.tensor
        def _(t):
            c.w(t, "v", clip_done)
            ins = t.matmul(P[2][:, 0:2], st.ones_r[:], clipb[0:1, 0:2],
                           start=True, stop=True)
            c.inc(ins, "t")
        clip_mm = c.cnt["t"]

        @block.vector
        def _(v):
            c.w(v, "t", clip_mm)
            v.tensor_copy(out=ncol[:], in_=P[2][:, 1:2])
            v.drain()

            ins = v.tensor_scalar(out=dPre[:], in0=dW0sb[:], scalar1=ncol[:],
                                  scalar2=None, op0=ALU.mult)
            c.inc(ins, "v")
            gbf_done_box[0] = c.cnt["v"]
            v.scalar_tensor_tensor(out=w1j[:], in0=dW1sb[:], scalar=ncol[:],
                                   in1=w1j[:], op0=ALU.mult, op1=ALU.add)
            v.drain()
            v.tensor_copy(out=w1j_bf[:], in_=w1j[:])
            v.scalar_tensor_tensor(out=s0f[:], in0=dW0sb[:], scalar=ncol[:],
                                   in1=s0f[:], op0=ALU.mult, op1=ALU.add)
            v.scalar_tensor_tensor(out=s1j[:], in0=dW1sb[:], scalar=ncol[:],
                                   in1=s1j[:], op0=ALU.mult, op1=ALU.add)
            v.scalar_tensor_tensor(out=w0f[:], in0=dW0sb[:], scalar=ncol[:],
                                   in1=w0f[:], op0=ALU.mult, op1=ALU.add)
            brow = st.BErow[0:1, ch:ch + 1]
            erow = st.BErow[0:1, 32 + ch:33 + ch]
            ins = v.tensor_scalar(out=sln_row[:], in0=sln_row[:], scalar1=erow,
                                  scalar2=None, op0=ALU.mult)
            c.vsw(v, ins)
            ins = v.scalar_tensor_tensor(out=sln_row[:], in0=n2row[0:1, 0:1024],
                                         scalar=n2row[0:1, 1024:1025], in1=sln_row[:],
                                         op0=ALU.mult, op1=ALU.subtract)
            c.vsw(v, ins)
            ins = v.tensor_scalar(out=sln_row[:], in0=sln_row[:], scalar1=-1.0,
                                  scalar2=None, op0=ALU.mult)
            c.vsw(v, ins)
            ins = v.tensor_scalar(out=ln_row[:], in0=ln_row[:], scalar1=brow,
                                  scalar2=None, op0=ALU.mult)
            c.vsw(v, ins)
            ins = v.tensor_tensor(out=ln_row[:], in0=ln_row[:], in1=sln_row[:], op=ALU.add)
            c.vsw(v, ins)
            ins = v.tensor_copy(out=ln_bfr[:], in_=ln_row[:])
            c.inc(ins, "v")
        upd_done = c.cnt["v"]
        gbf_done = gbf_done_box[0]

        if ch + 1 < NCH:
            w1hc_ready = emit_w1hc(upd_done)
            kf_rdy = kf_rdy_next
            kt_rdy = kt_rdy_next
            db_rdy = db_rdy_next

        if dbg.get("scan_ch") == ch:
            @block.gpsimd
            def _(g):
                c.w(g, "v", upd_done)
                c.dma(g, "g", st.dbg_ap["w0f"], w0f[:])
                c.dma(g, "g", st.dbg_ap["w1j"], w1j[:])
                c.dma(g, "g", st.dbg_ap["lnrow"], ln_row[:])
                c.dma(g, "g", st.dbg_ap["yblk"], yblk[:])
                c.dma(g, "g", st.dbg_ap["dyall"], d["dy_all"][:, :])
                c.dma(g, "g", st.dbg_ap["n2row"], n2row[:])
                c.dma(g, "g", st.dbg_ap["GS"], st.GS[:])
                c.dma(g, "g", st.dbg_ap["Qw"], Qw[:])
                c.dma(g, "g", st.dbg_ap["ccol"], ccol[:])
                c.dma(g, "g", st.dbg_ap["esb"], esb[:])
                c.dma(g, "g", st.dbg_ap["t1s"], t1s[:])
                c.dma(g, "g", st.dbg_ap["wcol"], wcol[:])
                c.dma(g, "g", st.dbg_ap["nrow"], nrow[:])
                c.dma(g, "g", st.dbg_ap["nri"], d["nr_in"][:, :])
                c.dma(g, "g", st.dbg_ap["nro"], d["nr_out"][:, :])

    st.upd_done = upd_done
    st.wcast_done = c.cnt["g"]




def final_attn(st):
    nc, c, block, d = st.nc, st.c, st.block, st.dram
    sb = st.sb
    P = st.P

    @block.gpsimd
    def _(g):
        c.w(g, "v", st.upd_done)
        c.dma(g, "g", d["w0_in"][:, :].rearrange("(k p) j -> p k j", p=128),
              st.w0f_bf[:].rearrange("p (k j) -> p k j", k=8))
        c.dma(g, "g", d["w1_in"][:, :].rearrange("(k p) h -> p k h", p=128),
              st.w1j_bf[:].rearrange("p (k h) -> p k h", k=2))
        c.w(g, "g")
        ins = g.collective_compute("AllGather", ALU.bypass, RG,
                                   ins=[d["w0_in"].opt()], outs=[d["w0_all"].opt()])
        c.inc(ins, "g", 1)
        ins = g.collective_compute("AllGather", ALU.bypass, RG,
                                   ins=[d["w1_in"].opt()], outs=[d["w1_all"].opt()])
        c.inc(ins, "g", 1)
    mem_ag = c.cnt["g"]

    w0a = sb("w0a", [128, 8 * IN])
    w1a = sb("w1a", [128, 16 * H])
    qt = sb("qt", [128, 8 * 512])
    h2 = sb("h2", [128, 16 * 512])
    y2sb = sb("y2sb", [128, 8 * 512])
    zb = sb("zb", [128, 512])
    sqb = sb("sqb", [128, 512])
    srow = sb("srow", [1, 1024], F32)
    srow_bf = sb("srow_bf", [1, 1024])
    RB2 = sb("RB2", [128, 512])
    MB2 = sb("MB2", [128, 512])
    lnq = sb("lnq", [128, 8], F32)
    lnbf32 = sb("lnbf32", [128, 1024], F32)
    xstr = sb("xstr", [128, 512], F32)
    vrow = sb("vrow", [1, 512], F32)
    zfp = sb("zfp", [128, 8 * 512], F32)
    x1o = sb("x1o", [128, 512], F32)
    x1b2 = sb("x1b2", [128, 512])
    x1l2 = sb("x1l2", [128, 512])

    @block.sync
    def _(sy):
        c.w(sy, "g", mem_ag)
        for rr in range(8):
            c.dma(sy, "q",
                  w0a[:].rearrange("p (k r j) -> p k r j", k=8, r=8)[:, :, rr, :],
                  d["w0_all"][rr * H:(rr + 1) * H, :]
                  .rearrange("(k p) j -> p k j", p=128))
        c.dma(sy, "q", w1a[:].rearrange("p (k h) -> p k h", k=16),
              d["w1_all"].rearrange("(k p) h -> p k h", p=128))
    mem_ld = c.cnt["q"]

    @block.tensor
    def _(t):
        c.w(t, "v", st.upd_done)
        ins = None
        for nh in range(2):
            ins = t.matmul(P[0][:, nh * 512:nh * 512 + 512], st.ones_r[:],
                           st.ln_bfr[0:1, nh * 512:nh * 512 + 512], start=True, stop=True)
        c.inc(ins, "t")

    @block.vector
    def _(v):
        c.w(v, "t")
        ins = v.tensor_copy(out=lnbf32[:], in_=P[0][:])
        c.inc(ins, "v")
    lnb32_done = c.cnt["v"]

    @block.tensor
    def _(t):
        c.w(t, "v", lnb32_done)
        ins = None
        for m in range(8):
            ins = t.transpose(P[1][:, m * 128:(m + 1) * 128],
                              lnbf32[:, m * 128:(m + 1) * 128], st.eye_f32[:])
        c.inc(ins, "t")
    lnq_mm = c.cnt["t"]

    @block.vector
    def _(v):
        c.w(v, "t", lnq_mm)
        ins = None
        for m in range(8):
            ins = v.tensor_copy(out=lnq[:, m:m + 1], in_=P[1][:, m * 128:m * 128 + 1])
        c.inc(ins, "v")
    lnq_done = c.cnt["v"]

    for tt in range(8):
        bb = tt // 4

        @block.sync
        def _(sy):
            c.w(sy, "q", mem_ld)
            if tt > 0:
                c.w(sy, "v")
            c.dma(sy, "q", qt[:].rearrange("p (k t) -> p k t", k=8),
                  d["q_dram"][:, tt * 512:(tt + 1) * 512]
                  .rearrange("(k p) t -> p k t", p=128))
        q_ld = c.cnt["q"]

        for im in range(16):
            qtr = [P[0], P[0], P[1], P[1]][im % 4]
            half = (im % 2) * 512

            @block.tensor
            def _(t):
                if im == 0:
                    c.w(t, "q", q_ld)
                    c.w(t, "v", lnq_done)
                if im >= 4:
                    c.w(t, "c")
                ins = None
                for k in range(8):
                    ins = t.matmul(
                        qtr[:, half:half + 512],
                        w0a[:, k * IN + im * 128:k * IN + im * 128 + 128],
                        qt[:, k * 512:(k + 1) * 512],
                        start=(k == 0), stop=(k == 7))
                c.inc(ins, "t")
            mm_done = c.cnt["t"]

            @block.scalar
            def _(sc):
                c.w(sc, "t", mm_done)
                ins = sc.activation(out=h2[:, im * 512:(im + 1) * 512],
                                    in_=qtr[:, half:half + 512], func=AF.Silu)
                c.inc(ins, "c")
        h2_done = c.cnt["c"]

        for hm in range(8):
            half = (hm % 2) * 512

            @block.tensor
            def _(t):
                if hm == 0:
                    c.w(t, "c", h2_done)
                if hm >= 2:
                    c.w(t, "v")
                    c.w(t, "c")
                ins = None
                for k in range(16):
                    ins = t.matmul(
                        P[2][:, half:half + 512],
                        w1a[:, k * H + hm * 128:k * H + hm * 128 + 128],
                        h2[:, k * 512:(k + 1) * 512],
                        start=(k == 0), stop=(k == 15))
                c.inc(ins, "t")
            mm_done = c.cnt["t"]

            @block.scalar
            def _(sc):
                c.w(sc, "t", mm_done)
                c.w(sc, "t")
                ins = sc.activation(out=sqb[:], in_=P[2][:, half:half + 512],
                                    func=AF.Square)
                c.inc(ins, "c")
            sq_done = c.cnt["c"]

            @block.vector
            def _(v):
                c.w(v, "t", mm_done)
                ins = v.tensor_copy(out=y2sb[:, hm * 512:(hm + 1) * 512],
                                    in_=P[2][:, half:half + 512])
                c.inc(ins, "v")

            @block.tensor
            def _(t):
                c.w(t, "c", sq_done)
                ins = t.matmul(P[3][0:1, 0:512], st.ones_c[:], sqb[:],
                               start=(hm == 0), stop=(hm == 7), skip_group_check=True)
                c.inc(ins, "t")
        ss_mm = c.cnt["t"]

        @block.vector
        def _(v):
            c.w(v, "t", ss_mm)
            ins = v.tensor_scalar(out=srow[0:1, 0:512], in0=P[3][0:1, 0:512],
                                  scalar1=1.0 / float(H), scalar2=EPS,
                                  op0=ALU.mult, op1=ALU.add)
            c.inc(ins, "v")

        @block.scalar
        def _(sc):
            c.w(sc, "v")
            ins = sc.activation(out=srow[0:1, 0:512], in_=srow[0:1, 0:512],
                                func=AF.Sqrt)
            c.inc(ins, "c")

        @block.vector
        def _(v):
            c.w(v, "c")
            v.reciprocal(out=srow[0:1, 512:1024], in_=srow[0:1, 0:512])
            v.drain()
            ins = v.tensor_copy(out=srow_bf[0:1, 0:512], in_=srow[0:1, 512:1024])
            c.inc(ins, "v")

        @block.tensor
        def _(t):
            c.w(t, "v")
            ins = t.matmul(P[3][:, 512:1024], st.ones_r[:], srow_bf[0:1, 0:512],
                           start=True, stop=True)
            c.inc(ins, "t")

        @block.vector
        def _(v):
            c.w(v, "t")
            ins = v.tensor_copy(out=RB2[:], in_=P[3][:, 512:1024])
            c.inc(ins, "v")
        rb2_done = c.cnt["v"]

        for hm in range(8):
            @block.sync
            def _(sy):
                c.w(sy, "v", rb2_done if hm == 0 else c.cnt["v"])
                c.dma(sy, "q", xstr[:],
                      d["xfb"][bb * H + hm * 128:bb * H + hm * 128 + 128,
                               (tt % 4) * 512:(tt % 4) * 512 + 512])
            x_ld = c.cnt["q"]

            @block.vector
            def _(v):
                c.w(v, "q", x_ld)
                sl = slice(hm * 512, (hm + 1) * 512)
                v.scalar_tensor_tensor(out=zfp[:, sl], in0=y2sb[:, sl],
                                       scalar=lnq[:, hm:hm + 1], in1=RB2[:],
                                       op0=ALU.mult, op1=ALU.mult)
                v.tensor_tensor(out=zfp[:, sl], in0=zfp[:, sl], in1=qt[:, sl],
                                op=ALU.add)
                ins = v.tensor_tensor(out=zfp[:, sl], in0=zfp[:, sl], in1=xstr[:],
                                      op=ALU.add)
                c.inc(ins, "v")
            z_done = c.cnt["v"]

            @block.scalar
            def _(sc):
                c.w(sc, "v", z_done)
                c.w(sc, "t")
                sl = slice(hm * 512, (hm + 1) * 512)
                sc.activation(out=zb[:], in_=zfp[:, sl], func=AF.Identity)
                ins = sc.activation(out=sqb[:], in_=zfp[:, sl], func=AF.Square)
                c.inc(ins, "c")
            zb_done = c.cnt["c"]

            @block.tensor
            def _(t):
                c.w(t, "c", zb_done)
                t.matmul(P[0][0:1, 0:512], st.ones_c[:], zb[:],
                         start=(hm == 0), stop=(hm == 7), skip_group_check=True)
                ins = t.matmul(P[0][0:1, 512:1024], st.ones_c[:], sqb[:],
                               start=(hm == 0), stop=(hm == 7), skip_group_check=True)
                c.inc(ins, "t")
        st_mm = c.cnt["t"]

        @block.vector
        def _(v):
            c.w(v, "t", st_mm)
            v.tensor_scalar(out=srow[0:1, 0:512], in0=P[0][0:1, 0:512],
                            scalar1=1.0 / float(H), scalar2=None, op0=ALU.mult)
            ins = v.tensor_scalar(out=srow[0:1, 512:1024], in0=P[0][0:1, 512:1024],
                                  scalar1=1.0 / float(H), scalar2=None, op0=ALU.mult)
            c.vsw(v, ins)
            ins = v.tensor_tensor(out=vrow[0:1, 0:512], in0=srow[0:1, 0:512],
                                  in1=srow[0:1, 0:512], op=ALU.mult)
            c.vsw(v, ins)
            ins = v.tensor_tensor(out=vrow[0:1, 0:512], in0=srow[0:1, 512:1024],
                                  in1=vrow[0:1, 0:512], op=ALU.subtract)
            c.inc(ins, "v")
        var_done = c.cnt["v"]

        @block.vector
        def _(v):
            c.w(v, "v", var_done)
            ins = v.tensor_scalar(out=vrow[0:1, 0:512], in0=vrow[0:1, 0:512],
                                  scalar1=LNEPS, scalar2=None, op0=ALU.add)
            c.inc(ins, "v")

        @block.scalar
        def _(sc):
            c.w(sc, "v")
            ins = sc.activation(out=vrow[0:1, 0:512], in_=vrow[0:1, 0:512],
                                func=AF.Sqrt)
            c.inc(ins, "c")

        @block.vector
        def _(v):
            c.w(v, "c")
            v.reciprocal(out=srow[0:1, 512:1024], in_=vrow[0:1, 0:512])
            v.drain()
            v.tensor_copy(out=srow_bf[0:1, 0:512], in_=srow[0:1, 0:512])
            ins = v.tensor_copy(out=srow_bf[0:1, 512:1024], in_=srow[0:1, 512:1024])
            c.inc(ins, "v")
        rows_done = c.cnt["v"]

        @block.tensor
        def _(t):
            c.w(t, "v", rows_done)
            t.matmul(P[1][:, 0:512], st.ones_r[:], srow_bf[0:1, 0:512],
                     start=True, stop=True)
            ins = t.matmul(P[1][:, 512:1024], st.ones_r[:], srow_bf[0:1, 512:1024],
                           start=True, stop=True)
            c.inc(ins, "t")
        bc_mm = c.cnt["t"]

        @block.vector
        def _(v):
            c.w(v, "t", bc_mm)
            v.tensor_copy(out=MB2[:], in_=P[1][:, 0:512])
            ins = v.tensor_copy(out=RB2[:], in_=P[1][:, 512:1024])
            c.inc(ins, "v")
        bc_done = c.cnt["v"]

        for hm in range(8):
            @block.vector
            def _(v):
                if hm == 0:
                    c.w(v, "v", bc_done)
                c.w(v, "q")
                sl = slice(hm * 512, (hm + 1) * 512)
                v.tensor_tensor(out=x1o[:], in0=zfp[:, sl], in1=MB2[:], op=ALU.subtract)
                ins = v.tensor_tensor(out=x1o[:], in0=x1o[:], in1=RB2[:], op=ALU.mult)
                c.vsw(v, ins)
                v.tensor_copy(out=x1b2[:], in_=x1o[:])
                v.drain()
                ins = v.tensor_tensor(out=x1l2[:], in0=x1o[:], in1=x1b2[:],
                                      op=ALU.subtract)
                c.inc(ins, "v")
            x1_done = c.cnt["v"]

            @block.sync
            def _(sy):
                c.w(sy, "v", x1_done)
                c.dma(sy, "q",
                      d["x1scr"][hm * 128:(hm + 1) * 128,
                                 tt * 512:(tt + 1) * 512], x1o[:])
                c.dma(sy, "q",
                      d["x1bfd"][hm * 128:(hm + 1) * 128,
                                 tt * 512:(tt + 1) * 512], x1b2[:])
                c.dma(sy, "q",
                      d["x1lod"][hm * 128:(hm + 1) * 128,
                                 tt * 512:(tt + 1) * 512], x1l2[:])
    st.x1_stores = c.cnt["q"]

    if st.dbg.get("x1"):
        @block.gpsimd
        def _(g):
            c.w(g, "q", st.x1_stores)
            c.dma(g, "g", st.dbg_ap["x1"], d["x1scr"][:, :])


def ffn(st):
    nc, c, block, d = st.nc, st.c, st.block, st.dram
    sb = st.sb
    P = st.P

    x1bf = sb("x1bf", [128, 8 * 1024])
    xlo = sb("xlo", [128, 8 * 1024])
    ghi = sb("ghi", [128, 8 * 16])
    glo = sb("glo", [128, 8 * 16])
    g32 = sb("g32", [128, 8 * 16], F32)
    brows = sb("brows", [1, 48])
    b32 = sb("b32", [1, 16], F32)
    rsigall = sb("rsigall", [128, 128], F32)
    rtmp = sb("rtmp", [128, 16], F32)
    eq1 = sb("eq1", [128, 16], F32)
    eq2 = sb("eq2", [128, 16], F32)
    m1c = sb("m1c", [128, 1], F32)
    m2c = sb("m2c", [128, 1], F32)
    mRc = sb("mRc", [128, 1], F32)
    wts = sb("wts", [128, 8 * NE], F32)
    wg_sb = [sb(f"wg_sb{i}", [128, 8 * 512]) for i in range(2)]
    wu_sb = [sb(f"wu_sb{i}", [128, 8 * 512]) for i in range(2)]
    sgl = sb("sgl", [128, 2 * 512])
    huw = sb("huw", [128, 512])
    act_sb = sb("act_sb", [128, 8 * 512])
    wd_t = [sb(f"wd_t{i}", [128, 1024]) for i in range(2)]
    ac_t = [sb(f"ac_t{i}", [128, 512], is_transpose_dest=True) for i in range(2)]
    zfp2 = sb("zfp2", [128, 8 * 512], F32)
    zb2 = sb("zb2", [128, 512])
    sqb2 = sb("sqb2", [128, 512])
    sr2 = sb("sr2", [1, 1024], F32)
    sr2b = sb("sr2b", [1, 1024])
    vr2 = sb("vr2", [1, 512], F32)
    MB3 = sb("MB3", [128, 512])
    RB3 = sb("RB3", [128, 512])
    xfp3 = sb("xfp3", [128, 512], F32)
    outt = sb("outt", [128, 512], F32)

    @block.gpsimd
    def _(g):
        c.w(g, "q", st.x1_stores)
        c.dma(g, "g", ghi[:].rearrange("p (k e) -> p k e", k=8), d["gwH"].rearrange("(k p) e -> p k e", p=128))
        c.dma(g, "g", g32[:].rearrange("p (k e) -> p k e", k=8), d["gwH"].rearrange("(k p) e -> p k e", p=128))
        c.dma(g, "g", brows[0:1, 0:16], d["gbH"])
        c.dma(g, "g", b32[:], d["gbH"])
    g_ld = c.cnt["g"]

    @block.vector
    def _(v):
        c.w(v, "g", g_ld)
        v.tensor_tensor(out=glo[:], in0=g32[:], in1=ghi[:], op=ALU.subtract)
        ins = v.scalar_tensor_tensor(out=brows[0:1, 16:32], in0=brows[0:1, 0:16],
                                     scalar=-1.0, in1=b32[:],
                                     op0=ALU.mult, op1=ALU.add)
        c.inc(ins, "v")
    glo_done = c.cnt["v"]

    for cc in range(4):
        col0 = cc * 1024

        @block.sync
        def _(sy):
            c.w(sy, "q", st.x1_stores)
            if cc > 0:
                c.w(sy, "t")
            c.dma(sy, "q", x1bf[:].rearrange("p (k t) -> p k t", k=8),
                  d["x1bfd"][:, col0:col0 + 1024].rearrange("(k p) t -> p k t", p=128))
            c.dma(sy, "q", xlo[:].rearrange("p (k t) -> p k t", k=8),
                  d["x1lod"][:, col0:col0 + 1024].rearrange("(k p) t -> p k t", p=128))
        xlo_done = c.cnt["q"]

        @block.vector
        def _(v):
            ins = v.memset(wts[:], 1.0)
            c.inc(ins, "v")
        wts_init = c.cnt["v"]

        for tm in range(8):
            @block.tensor
            def _(t):
                if tm == 0:
                    c.w(t, "q", xlo_done)
                    c.w(t, "v", wts_init)
                first = True
                for k in range(8):
                    t.matmul(P[2][:, tm * 16:tm * 16 + 16],
                             x1bf[:, k * 1024 + tm * 128:k * 1024 + tm * 128 + 128],
                             ghi[:, k * 16:(k + 1) * 16], start=first, stop=False)
                    first = False
                    t.matmul(P[2][:, tm * 16:tm * 16 + 16],
                             x1bf[:, k * 1024 + tm * 128:k * 1024 + tm * 128 + 128],
                             glo[:, k * 16:(k + 1) * 16], start=False, stop=False)
                    t.matmul(P[2][:, tm * 16:tm * 16 + 16],
                             xlo[:, k * 1024 + tm * 128:k * 1024 + tm * 128 + 128],
                             ghi[:, k * 16:(k + 1) * 16], start=False, stop=False)
                t.matmul(P[2][:, tm * 16:tm * 16 + 16], st.ones_r[:], brows[0:1, 0:16],
                         start=False, stop=False)
                ins = t.matmul(P[2][:, tm * 16:tm * 16 + 16], st.ones_r[:],
                               brows[0:1, 16:32], start=False, stop=(True))
                if tm == 7:
                    c.inc(ins, "t")
        r_mm = c.cnt["t"]

        @block.scalar
        def _(sc):
            c.w(sc, "t", r_mm)
            ins = sc.activation(out=rsigall[:], in_=P[2][:, 0:128], func=AF.Sigmoid)
            c.inc(ins, "c")
        sig_done = c.cnt["c"]

        for tm in range(8):
            @block.vector
            def _(v):
                if tm == 0:
                    c.w(v, "c", sig_done)
                rsig_t = rsigall[:, tm * 16:(tm + 1) * 16]
                v.reduce_max(out=m1c[:], in_=rsig_t, axis=mybir.AxisListType.X)
                v.drain()
                v.tensor_scalar(out=eq1[:], in0=rsig_t, scalar1=m1c[:], scalar2=None,
                                op0=ALU.is_equal)
                v.drain()
                v.scalar_tensor_tensor(out=rtmp[:], in0=eq1[:], scalar=-1e30,
                                       in1=rsig_t, op0=ALU.mult, op1=ALU.add)
                v.drain()
                v.reduce_max(out=m2c[:], in_=rtmp[:], axis=mybir.AxisListType.X)
                v.drain()
                v.tensor_scalar(out=eq2[:], in0=rtmp[:], scalar1=m2c[:],
                                scalar2=None, op0=ALU.is_equal)
                v.drain()
                v.tensor_tensor(out=eq1[:], in0=eq1[:], in1=eq2[:], op=ALU.add)
                v.tensor_tensor(out=m1c[:], in0=m1c[:], in1=m2c[:], op=ALU.add)
                v.drain()
                v.reciprocal(out=mRc[:], in_=m1c[:])
                v.drain()
                v.tensor_scalar(out=rtmp[:], in0=rsig_t, scalar1=mRc[:], scalar2=None,
                                op0=ALU.mult)
                v.drain()
                ins = v.tensor_tensor(out=wts[:, tm * NE:tm * NE + 16], in0=rtmp[:],
                                      in1=eq1[:], op=ALU.mult)
                c.inc(ins, "v")
        wts_done = c.cnt["v"]

        st_act = []
        up_hist_x = []
        prev_a_done = 0
        for e in range(NE):
            ebuf = e % 2

            @block.sync
            def _(sy):
                if e >= 2:
                    c.w(sy, "t", up_hist_x[e - 1])
                c.dma(sy, "q", wg_sb[ebuf][:].rearrange("p (k i) -> p k i", k=8),
                      d["wgB"][:, e * 512:(e + 1) * 512]
                      .rearrange("(k p) i -> p k i", p=128))
                c.dma(sy, "q", wu_sb[ebuf][:].rearrange("p (k i) -> p k i", k=8),
                      d["wuB"][:, e * 512:(e + 1) * 512]
                      .rearrange("(k p) i -> p k i", p=128))
            w_ld = c.cnt["q"]
            a_hist = []

            for tm in range(8):
                qg = [P[0], P[2]][tm % 2]
                qu = [P[1], P[3]][tm % 2]
                half = ((tm // 2) % 2) * 512

                @block.tensor
                def _(t):
                    if tm == 0:
                        c.w(t, "q", w_ld)
                        c.w(t, "v", wts_done)
                        c.w(t, "g", st.wcast_done)
                    if tm >= 4:
                        c.w(t, "v")
                        c.w(t, "c")
                    ins = None
                    for k in range(8):
                        t.matmul(qg[:, half:half + 512],
                                 x1bf[:, k * 1024 + tm * 128:k * 1024 + tm * 128 + 128],
                                 wg_sb[ebuf][:, k * 512:(k + 1) * 512],
                                 start=(k == 0), stop=(k == 7))
                        ins = t.matmul(qu[:, half:half + 512],
                                       x1bf[:, k * 1024 + tm * 128:
                                            k * 1024 + tm * 128 + 128],
                                       wu_sb[ebuf][:, k * 512:(k + 1) * 512],
                                       start=(k == 0), stop=(k == 7))
                    c.inc(ins, "t")
                up_mm = c.cnt["t"]

                sgh = (tm % 2) * 512

                @block.scalar
                def _(sc):
                    c.w(sc, "t", up_mm)
                    if tm >= 2:
                        c.w(sc, "v", a_hist[tm - 2])
                    elif e >= 1 or True:
                        c.w(sc, "v", prev_a_done)
                    ins = sc.activation(out=sgl[:, sgh:sgh + 512],
                                        in_=qg[:, half:half + 512], func=AF.Silu)
                    c.inc(ins, "c")
                sg_done = c.cnt["c"]

                @block.vector
                def _(v):
                    c.w(v, "c", sg_done)
                    if tm == 0:
                        c.w(v, "s")
                        v.drain()
                    v.tensor_scalar(out=huw[:], in0=qu[:, half:half + 512],
                                    scalar1=wts[:, tm * NE + e:tm * NE + e + 1],
                                    scalar2=None, op0=ALU.mult)
                    ins = v.tensor_tensor(out=act_sb[:, tm * 512:(tm + 1) * 512],
                                          in0=sgl[:, sgh:sgh + 512], in1=huw[:],
                                          op=ALU.mult)
                    c.inc(ins, "v")
                a_hist.append(c.cnt["v"])
            a_done = c.cnt["v"]
            prev_a_done = a_done
            up_hist_x.append(c.cnt["t"])

            @block.scalar
            def _(sc):
                c.w(sc, "v", a_done)
                c.dma(sc, "s",
                      d["act_dram"][e * 1024:(e + 1) * 1024, :]
                      .rearrange("(tm p) i -> p tm i", p=128),
                      act_sb[:].rearrange("p (tm i) -> p tm i", tm=8))
            st_act.append(c.cnt["s"])

        for th in range(2):
            nk = KDE // 128
            ld_hist = []
            mm_hist = []
            for k in range(nk):
                kb = k % 2
                e_of_k = k // 4

                @block.sync
                def _(sy):
                    if k == 0:
                        c.w(sy, "s", st_act[-1])
                        c.w(sy, "t")
                    if k >= 2:
                        c.w(sy, "t", mm_hist[k - 2])
                    c.dma(sy, "q", wd_t[kb][:], d["wdB"][k * 128:(k + 1) * 128, :])
                ld_w = c.cnt["q"]

                @block.scalar
                def _(sc):
                    if k == 0:
                        c.w(sc, "s", st_act[-1])
                        c.w(sc, "t")
                    if k >= 2:
                        c.w(sc, "t", mm_hist[k - 2])
                    c.tr(sc, "s", ac_t[kb][:],
                         d["act_dram"][e_of_k * 1024 + th * 512:
                                       e_of_k * 1024 + th * 512 + 512,
                                       (k % 4) * 128:(k % 4) * 128 + 128])
                ld_hist.append((ld_w, c.cnt["s"]))

                @block.tensor
                def _(t):
                    c.w(t, "q", ld_hist[-1][0])
                    c.w(t, "s", ld_hist[-1][1])
                    ins = None
                    for m in range(8):
                        ins = t.matmul(P[m // 2][:, (m % 2) * 512:(m % 2) * 512 + 512],
                                       wd_t[kb][:, m * 128:(m + 1) * 128],
                                       ac_t[kb][:],
                                       start=(k == 0), stop=(k == nk - 1))
                    c.inc(ins, "t")
                mm_hist.append(c.cnt["t"])
            dn_mm = c.cnt["t"]

            for m in range(8):
                @block.sync
                def _(sy):
                    if m == 0:
                        c.w(sy, "t", dn_mm)
                    c.w(sy, "v")
                    c.dma(sy, "q", xfp3[:],
                          d["x1scr"][m * 128:(m + 1) * 128,
                                     col0 + th * 512:col0 + th * 512 + 512])
                xl = c.cnt["q"]

                @block.vector
                def _(v):
                    c.w(v, "q", xl)
                    ins = v.scalar_tensor_tensor(
                        out=zfp2[:, m * 512:(m + 1) * 512], in0=xfp3[:], scalar=2.0,
                        in1=P[m // 2][:, (m % 2) * 512:(m % 2) * 512 + 512],
                        op0=ALU.mult, op1=ALU.add)
                    c.inc(ins, "v")
            z2_done = c.cnt["v"]

            if cc == 0 and th == 0 and st.dbg.get("ffn0"):
                @block.gpsimd
                def _(g):
                    c.w(g, "v", z2_done)
                    c.dma(g, "g", st.dbg_ap["zfp2"], zfp2[:])

            st2_hist = []
            for m in range(8):
                @block.scalar
                def _(sc):
                    if m == 0:
                        c.w(sc, "v", z2_done)
                    else:
                        c.w(sc, "t", st2_hist[m - 1])
                    sl = slice(m * 512, (m + 1) * 512)
                    sc.activation(out=zb2[:], in_=zfp2[:, sl], func=AF.Identity)
                    ins = sc.activation(out=sqb2[:], in_=zfp2[:, sl], func=AF.Square)
                    c.inc(ins, "c")
                zb2_done = c.cnt["c"]

                @block.tensor
                def _(t):
                    c.w(t, "c", zb2_done)
                    t.matmul(P[0][0:1, 0:512], st.ones_c[:], zb2[:],
                             start=(m == 0), stop=(m == 7), skip_group_check=True)
                    ins = t.matmul(P[0][0:1, 512:1024], st.ones_c[:], sqb2[:],
                                   start=(m == 0), stop=(m == 7), skip_group_check=True)
                    c.inc(ins, "t")
                st2_hist.append(c.cnt["t"])
            st2_mm = c.cnt["t"]

            @block.vector
            def _(v):
                c.w(v, "t", st2_mm)
                v.tensor_scalar(out=sr2[0:1, 0:512], in0=P[0][0:1, 0:512],
                                scalar1=1.0 / float(H), scalar2=None, op0=ALU.mult)
                ins = v.tensor_scalar(out=sr2[0:1, 512:1024], in0=P[0][0:1, 512:1024],
                                      scalar1=1.0 / float(H), scalar2=None, op0=ALU.mult)
                c.vsw(v, ins)
                ins = v.tensor_tensor(out=vr2[0:1, 0:512], in0=sr2[0:1, 0:512],
                                      in1=sr2[0:1, 0:512], op=ALU.mult)
                c.vsw(v, ins)
                ins = v.tensor_tensor(out=vr2[0:1, 0:512], in0=sr2[0:1, 512:1024],
                                      in1=vr2[0:1, 0:512], op=ALU.subtract)
                c.inc(ins, "v")
            v2_done = c.cnt["v"]

            @block.vector
            def _(v):
                c.w(v, "v", v2_done)
                ins = v.tensor_scalar(out=vr2[0:1, 0:512], in0=vr2[0:1, 0:512],
                                      scalar1=LNEPS, scalar2=None, op0=ALU.add)
                c.inc(ins, "v")

            @block.scalar
            def _(sc):
                c.w(sc, "v")
                ins = sc.activation(out=vr2[0:1, 0:512], in_=vr2[0:1, 0:512],
                                    func=AF.Sqrt)
                c.inc(ins, "c")

            @block.vector
            def _(v):
                c.w(v, "c")
                v.reciprocal(out=sr2[0:1, 512:1024], in_=vr2[0:1, 0:512])
                v.drain()
                v.tensor_copy(out=sr2b[0:1, 0:512], in_=sr2[0:1, 0:512])
                ins = v.tensor_copy(out=sr2b[0:1, 512:1024], in_=sr2[0:1, 512:1024])
                c.inc(ins, "v")
            rows2 = c.cnt["v"]

            @block.tensor
            def _(t):
                c.w(t, "v", rows2)
                t.matmul(P[1][:, 0:512], st.ones_r[:], sr2b[0:1, 0:512],
                         start=True, stop=True)
                ins = t.matmul(P[1][:, 512:1024], st.ones_r[:], sr2b[0:1, 512:1024],
                               start=True, stop=True)
                c.inc(ins, "t")
            bc2_mm = c.cnt["t"]

            @block.vector
            def _(v):
                c.w(v, "t", bc2_mm)
                v.tensor_copy(out=MB3[:], in_=P[1][:, 0:512])
                ins = v.tensor_copy(out=RB3[:], in_=P[1][:, 512:1024])
                c.inc(ins, "v")
            bc2_done = c.cnt["v"]

            for m in range(8):
                @block.vector
                def _(v):
                    if m == 0:
                        c.w(v, "v", bc2_done)
                    c.w(v, "q")
                    sl = slice(m * 512, (m + 1) * 512)
                    v.tensor_tensor(out=outt[:], in0=zfp2[:, sl], in1=MB3[:],
                                    op=ALU.subtract)
                    ins = v.tensor_tensor(out=outt[:], in0=outt[:], in1=RB3[:],
                                          op=ALU.mult)
                    c.inc(ins, "v")
                o_done = c.cnt["v"]

                @block.sync
                def _(sy):
                    c.w(sy, "v", o_done)
                    c.dma(sy, "q",
                          d["yout"][m * 128:(m + 1) * 128,
                                    col0 + th * 512:col0 + th * 512 + 512], outt[:])

    @block.gpsimd
    def _(g):
        for sname in ("q", "g", "s", "v", "c", "t"):
            c.w(g, sname)


def build_nc(dbg=None):
    dbg = dict(dbg or {})
    nc = bass.Bass("TRN2", target_bir_lowering=False)
    d = {}

    def din(name, shape, dt=F32):
        d[name] = nc.dram_tensor(name, shape, dt, kind="ExternalInput").ap()

    def dou(name, shape, dt=F32):
        d[name] = nc.dram_tensor(name, shape, dt, kind="ExternalOutput").ap()

    def dit(name, shape, dt=BF16):
        d[name] = nc.dram_tensor(name, shape, dt).ap()

    din("xfc", [CPC * H, T]); din("xfb", [2 * H, S])
    din("wqT", [H, H]); din("wkT", [H, H]); din("wvT", [H, H])
    din("qn", [128, 8]); din("kn", [128, 8]); din("knrow", [1, H])
    din("twT", [H, T]); din("twA", [H, T]); din("twE", [H, T])
    din("w0fs", [128, 8 * JS]); din("w1js", [128, 2 * H]); din("lnr", [1, H])
    din("eselh", [16, 128]); din("onesh", [128, 128]); din("eyeh", [128, 128])
    din("gwH", [H, 16]); din("gbH", [1, 16])
    din("wgS", [H, KDE]); din("wuS", [H, KDE]); din("wdS", [KDE, H])
    dou("yout", [H, TPC])

    dit("kd_in", [CPC * H, T]); dit("kd_all", [NCORE * CPC * H, T])
    dit("sm_in", [CPC, 32], F32); dit("sm_all", [NCH, 32], F32)
    dit("q_dram", [H, TPC])
    dit("ktok_dram", [TPC, H]); dit("vtok_dram", [TPC, H])
    dit("d_own", [NCH * 128, H])
    dit("hU_dram", [2 * JS, T])
    dit("yrs_in", [T, H]); dit("yrs_out", [128, H])
    dit("dy_in", [128, H]); dit("dy_all", [T, H])
    dit("nr_in", [1, 1536], F32); dit("nr_out", [1, 1536], F32)
    dit("w1scr", [JS, H])
    dit("w0_in", [H, JS]); dit("w0_all", [NCORE * H, JS])
    dit("w1_in", [JS, H]); dit("w1_all", [IN, H])
    dit("x1scr", [H, TPC], F32)
    dit("x1bfd", [H, TPC]); dit("x1lod", [H, TPC])
    dit("wgB", [H, KDE]); dit("wuB", [H, KDE]); dit("wdB", [KDE, H])
    dit("act_dram", [NE * T, 512])

    dbg_shapes = {
        "kall": ([NCORE * CPC * H, T], BF16), "sm": ([NCH, 32], F32),
        "qd": ([H, TPC], BF16), "down": ([NCH * 128, H], BF16),
        "w0f": ([128, 8 * JS], F32), "w1j": ([128, 2 * H], F32),
        "lnrow": ([1, H], F32), "yblk": ([128, H], F32),
        "dyall": ([T, H], BF16), "n2row": ([1, 1536], F32),
        "x1": ([H, TPC], F32),
        "ktd": ([TPC, H], BF16), "vtd": ([TPC, H], BF16),
        "GS": ([128, NCH], F32), "Qw": ([128, H], BF16), "ccol": ([128, 1], F32),
        "esb": ([128, H], F32), "t1s": ([128, H], F32), "wcol": ([128, 1], F32), "nrow": ([1, 1536], F32),
        "nri": ([1, 1536], F32), "nro": ([1, 1536], F32),
        "wts": ([128, 8 * NE], F32), "actd": ([NE * T, 512], BF16),
        "zfp2": ([128, 8 * 512], F32),
    }
    dbg_ap = {}
    need = set()
    if dbg.get("pa"):
        need |= {"kall", "sm", "qd", "down", "ktd", "vtd"}
    if dbg.get("scan_ch") is not None:
        need |= {"w0f", "w1j", "lnrow", "yblk", "dyall", "n2row",
                 "GS", "Qw", "ccol", "esb", "t1s", "wcol", "nrow", "nri", "nro"}
    if dbg.get("x1"):
        need.add("x1")
    if dbg.get("ffn0"):
        need |= {"wts", "actd", "zfp2"}
    for name in need:
        shp, dt = dbg_shapes[name]
        dbg_ap[name] = nc.dram_tensor("dbg_" + name, shp, dt, kind="ExternalOutput").ap()

    es = contextlib.ExitStack()
    with es:
        c = Ctx(nc, es)
        block = es.enter_context(nc.Block())

        def mk_sb(stack):
            return lambda name, shape, dt=BF16, **kw: stack.enter_context(
                nc.sbuf_tensor(name, shape, dt, **kw))

        psb = mk_sb(es)
        st = types.SimpleNamespace(nc=nc, c=c, block=block, dram=d, dbg=dbg,
                                   dbg_ap=dbg_ap)
        st.ones_r = psb("ones_r", [1, 128])
        st.ones_c = psb("ones_c", [128, 1])
        st.eye_bf = psb("eye_bf", [128, 128])
        st.eye_f32 = psb("eye_f32", [128, 128], F32)
        st.knrow_bf = psb("knrow_bf", [1, H])
        st.GS = psb("GS", [128, NCH], F32)
        st.BT = psb("BT", [128, NCH], F32)
        st.ET = psb("ET", [128, NCH], F32)
        st.BErow = psb("BErow", [1, 64], F32)
        st.w0f = psb("w0f", [128, 8 * JS], F32)
        st.w1j = psb("w1j", [128, 2 * H], F32)
        st.s0f = psb("s0f", [128, 8 * JS], F32)
        st.s1j = psb("s1j", [128, 2 * H], F32)
        st.w0f_bf = psb("w0f_bf", [128, 8 * JS])
        st.w1j_bf = psb("w1j_bf", [128, 2 * H])
        st.w1hc_bf = psb("w1hc_bf", [128, 8 * JS], is_transpose_dest=True)
        st.ln_row = psb("ln_row", [1, H], F32)
        st.sln_row = psb("sln_row", [1, H], F32)
        st.ln_bfr = psb("ln_bfr", [1, H])
        st.P = [es.enter_context(nc.psum_tensor(f"P{i}", [128, 1024], F32))
                for i in range(4)]

        @block.sync
        def _(sy):
            c.dma(sy, "q", st.eye_f32[:], d["eyeh"])

        esA = contextlib.ExitStack()
        st.sb = mk_sb(esA)
        phase_a(st)
        if dbg.get("pa"):
            @block.gpsimd
            def _(g):
                for sname in ("q", "g", "s", "v", "c", "t"):
                    c.w(g, sname)
                c.dma(g, "g", dbg_ap["kall"], d["kd_all"][:, :])
                c.dma(g, "g", dbg_ap["sm"], d["sm_all"][:, :])
                c.dma(g, "g", dbg_ap["qd"], d["q_dram"][:, :])
                c.dma(g, "g", dbg_ap["down"], d["d_own"][:, :])
                c.dma(g, "g", dbg_ap["ktd"], d["ktok_dram"][:, :])
                c.dma(g, "g", dbg_ap["vtd"], d["vtok_dram"][:, :])
        c.barrier(block)
        esA.close()

        if not dbg.get("stop_pa"):
            esS = contextlib.ExitStack()
            st.sb = mk_sb(esS)
            scan(st)
            c.barrier(block)
            esS.close()

        if not dbg.get("stop_pa") and not dbg.get("stop_scan"):
            esF = contextlib.ExitStack()
            st.sb = mk_sb(esF)
            final_attn(st)
            c.barrier(block)
            esF.close()

            esN = contextlib.ExitStack()
            st.sb = mk_sb(esN)
            ffn(st)
            c.barrier(block)
            esN.close()

        @block.gpsimd
        def _(g):
            for sname in ("q", "g", "s", "v", "c", "t"):
                c.w(g, sname)
    return nc


def _host_maps(inputs):
    f32 = np.float32
    x = np.ascontiguousarray(np.asarray(inputs["L0_attn_x"], f32))
    wq = np.asarray(inputs["L0_attn_wq_w"], f32)
    wk = np.asarray(inputs["L0_attn_wk_w"], f32)
    wv = np.asarray(inputs["L0_attn_wv_w"], f32)
    qnw = np.asarray(inputs["L0_attn_q_norm_w"], f32)
    knw = np.asarray(inputs["L0_attn_k_norm_w"], f32)
    aw = np.asarray(inputs["L0_attn_alpha_w"], f32)
    tw = np.asarray(inputs["L0_attn_theta_w"], f32)
    ew = np.asarray(inputs["L0_attn_eta_w"], f32)
    m0 = np.asarray(inputs["L0_attn_mem_w0"], f32)
    m1 = np.asarray(inputs["L0_attn_mem_w1"], f32)
    mln = np.asarray(inputs["L0_attn_mem_ln_w"], f32)
    gw = np.asarray(inputs["L0_ffn_gate_w"], f32)
    gb = np.asarray(inputs["L0_ffn_gate_b"], f32)
    swg = np.asarray(inputs["L0_ffn_shared_wg"], f32)
    swu = np.asarray(inputs["L0_ffn_shared_wu"], f32)
    swd = np.asarray(inputs["L0_ffn_shared_wd"], f32)
    ewg = np.asarray(inputs["L0_ffn_experts_wg"], f32)
    ewu = np.asarray(inputs["L0_ffn_experts_wu"], f32)
    ewd = np.asarray(inputs["L0_ffn_experts_wd"], f32)

    CC = np.ascontiguousarray
    shared = {
        "wqT": CC(wq.T), "wkT": CC(wk.T), "wvT": CC(wv.T),
        "qn": CC(qnw.reshape(8, 128).T), "kn": CC(knw.reshape(8, 128).T),
        "knrow": CC(knw.reshape(1, H)),
        "twT": CC(np.tile(tw.reshape(CH, H).T, (1, B))),
        "twA": CC(np.tile(aw.reshape(CH, H).T, (1, B))),
        "twE": CC(np.tile(ew.reshape(CH, H).T, (1, B))),
        "lnr": CC(mln.reshape(1, H)),
        "onesh": np.ones((128, 128), f32),
        "eyeh": np.eye(128, dtype=f32),
        "gwH": CC(gw), "gbH": CC(gb.reshape(1, 16)),
        "wgS": CC(np.concatenate(
            [ewg.transpose(1, 0, 2).reshape(H, 16 * 512), swg], axis=1)),
        "wuS": CC(np.concatenate(
            [ewu.transpose(1, 0, 2).reshape(H, 16 * 512), swu], axis=1)),
        "wdS": CC(np.concatenate([ewd.reshape(16 * 512, H), swd], axis=0)),
    }
    w0T = m0.T
    xc = x.reshape(B, NCH, CH, H)
    in_maps = []
    for r in range(NCORE):
        esel = np.zeros((16, 128), f32)
        for p in range(128):
            esel[2 * r + p // CH, p] = 1.0
        m = dict(shared)
        m["xfc"] = CC(xc[:, 4 * r:4 * r + 4].transpose(1, 3, 0, 2)
                      .reshape(CPC * H, T))
        m["xfb"] = CC(x[2 * r:2 * r + 2].transpose(0, 2, 1).reshape(2 * H, S))
        m["w0fs"] = CC(w0T[:, JS * r:JS * (r + 1)].reshape(8, 128, JS)
                       .transpose(1, 0, 2).reshape(128, 8 * JS))
        m["w1js"] = CC(m1[:, JS * r:JS * (r + 1)].T.reshape(2, 128, H)
                       .transpose(1, 0, 2).reshape(128, 2 * H))
        m["eselh"] = esel
        in_maps.append(m)
    return in_maps


def kernel(**inputs):
    in_maps = _host_maps(inputs)
    nc = build_nc()
    res = run_bass_kernel_spmd(nc, in_maps, core_ids=list(range(NCORE)))
    outs = []
    for r in range(NCORE):
        y = np.asarray(res.results[r]["yout"], dtype=np.float32)
        outs.append(y.T.reshape(2, S, H))
    return np.ascontiguousarray(np.concatenate(outs, axis=0))
